# version 03
# baseline (speedup 1.0000x reference)
import contextlib
import os
import numpy as np
import ml_dtypes
import concourse.bass as bass
import concourse.mybir as mybir
from concourse.bass_utils import run_bass_kernel_spmd

F32 = mybir.dt.float32
BF16 = mybir.dt.bfloat16
AF = mybir.ActivationFunctionType
ALU = mybir.AluOpType
AX = mybir.AxisListType

TIL = 128
HALO = 2176
OWN = 2048
NT = (HALO + OWN) // TIL
NTOK = HALO + OWN
D = 1024
NEG = -1.0e30
EPS = 1e-5

N_STAGES = 6

ENGINES = ("sync", "gpsimd", "tensor", "vector", "scalar")


class K:

    def __init__(self):
        self.q = {e: [] for e in ENGINES}
        self.cnt = {}
        self.waited = {e: {} for e in ENGINES}
        self.semh = {}
        self.last_w = {}
        self.readers = {}
        self.chan_last = {}
        self.esem = {"tensor": "tns", "vector": "vec", "scalar": "scl",
                     "gpsimd": "gps", "sync": "snc"}

    def sems_needed(self):
        return ["tns", "vec", "scl", "gps", "snc"] + sorted(
            s for s in self.cnt if s.startswith("c_"))

    def inc(self, sem, kk=1):
        self.cnt[sem] = self.cnt.get(sem, 0) + kk
        return (sem, self.cnt[sem])

    def last(self, sem):
        return (sem, self.cnt.get(sem, 0))

    def wait(self, eng, tok):
        if tok is None:
            return
        sem, val = tok
        if val <= 0 or self.waited[eng].get(sem, 0) >= val:
            return
        self.waited[eng][sem] = val
        self.q[eng].append(lambda e, s=sem, v=val: e.wait_ge(self.semh[s], v))

    PSUM_KEYS = ("PSMM", "PSS", "PSO", "PSST", "PST")

    def _is_psum(self, key):
        kk = key[0] if isinstance(key, tuple) else key
        return kk in self.PSUM_KEYS

    def _deps(self, reads, writes, waits):
        deps = list(waits)
        for r in reads:
            deps.append(self.last_w.get(r))
            if self._is_psum(r):
                deps.extend(self.readers.get(r, ()))
        for w in writes:
            deps.append(self.last_w.get(w))
            deps.extend(self.readers.get(w, ()))
        return deps

    def _record(self, t, reads, writes):
        for r in reads:
            if self._is_psum(r):
                self.last_w[r] = t
                self.readers[r] = []
            else:
                self.readers.setdefault(r, []).append(t)
        for w in writes:
            self.last_w[w] = t
            self.readers[w] = []

    def op(self, eng, fn, reads=(), writes=(), waits=()):
        for d in self._deps(reads, writes, waits):
            self.wait(eng, d)
        sem = self.esem[eng]
        t = self.inc(sem, 1)
        def wrapped(e, fn=fn, s=sem):
            fn(e).then_inc(self.semh[s], 1)
        self.q[eng].append(wrapped)
        self._record(t, reads, writes)
        return t

    def dma(self, eng, ch, out, in_, reads=(), writes=(), waits=()):
        sem = "c_" + ch
        self.cnt.setdefault(sem, 0)
        deps = self._deps(reads, writes, waits)
        deps.append(self.chan_last.get(ch))
        for d in deps:
            self.wait(eng, d)
        t = self.inc(sem, 16)
        def wrapped(e, o=out, i=in_, s=sem):
            e.dma_start(out=o, in_=i).then_inc(self.semh[s], 16)
        self.q[eng].append(wrapped)
        self.chan_last[ch] = t
        self._record(t, reads, writes)
        return t


def chunks4(lo, hi, w=4):
    out = []
    j = lo
    while j < hi:
        out.append(list(range(j, min(hi, j + w))))
        j += w
    return out


def build_nc(n_stages=N_STAGES, dbg=None):
    nc = bass.Bass("TRN2", target_bir_lowering=False, debug=False)
    k = K()

    def din(name, shape, dt=F32):
        return nc.dram_tensor(name, shape, dt, kind="ExternalInput").ap()

    xin = din("xin", [NTOK, D])
    frow_d = din("frow", [1, NTOK])
    ident_d = din("ident", [TIL, TIL])
    causalT_d = din("causalT", [TIL, TIL])
    fpart_d = din("fpart", [TIL, NT])
    mask_base_d = din("mask_base", [TIL, 640])
    mask_sp_d = din("mask_sp", [4, TIL, 640])
    tab4_d = din("tab4", [NT, TIL, 256])
    qs4_d = din("qs4", [2, TIL, 4, 64])
    gam_d = din("gam", [2, TIL, D])
    bet_d = din("bet", [2, TIL, D])
    b1a_d = din("b1a", [2, TIL, 22])
    b1g_d = din("b1g", [2, TIL, 22])
    b2_d = din("b2", [2, 1, D])
    w_ffn_qkv_d = din("w_ffn_qkv", [2, D, 1536])
    w_ffn_wo_d = din("w_ffn_wo", [2, 512, D])
    w_ffn_w1_d = din("w_ffn_w1", [2, D, 5460])
    w_ffn_w2_d = din("w_ffn_w2", [2, 2730, D])
    w_l1q_d = din("w_l1_q", [D, D])
    w_l1k_d = din("w_l1_k", [D, D])
    w_l1v_d = din("w_l1_v", [D, D])
    w_l1fT_d = din("w_l1f_T", [D, D])
    gwT_d = din("gate_wT", [D, 8])
    UcT_d = din("U_catT", [D, TIL])
    Vc_d = din("V_cat", [TIL, D])
    m1_base_d = din("m1_base", [TIL, 256])
    m1_sp_d = din("m1_sp", [TIL, 256])
    w_qkvT_d = din("w_l0a_qkvT", [D, 3 * D])
    b_qkv_d = din("b_l0a_qkv", [1, 3 * D])
    w_outT_d = din("w_l0a_outT", [D, D])
    b_out_d = din("b_l0a_out", [1, D])

    xout = nc.dram_tensor("xout", [OWN, D], F32, kind="ExternalOutput").ap()
    x1_d = nc.dram_tensor("x1_d", [NTOK, D], F32, kind="Internal").ap()
    y_d = nc.dram_tensor("y_d", [NTOK, D], F32, kind="Internal").ap()
    xl2_d = nc.dram_tensor("xl2_d", [NTOK, D], F32, kind="Internal").ap()
    z_d = nc.dram_tensor("z_d", [NTOK, D], F32, kind="Internal").ap()
    x2_d = nc.dram_tensor("x2_d", [NTOK, D], F32, kind="Internal").ap()
    streams = {"x1": x1_d, "y": y_d, "xl2": xl2_d, "z": z_d, "x2": x2_d,
               "x3": z_d, "x4": x1_d}
    spy_out = None
    if int(os.environ.get("KSPY", "0")):
        spy_out = nc.dram_tensor("spy", [9 * TIL, 1024], F32,
                                 kind="ExternalOutput").ap()
    dbg_out = None
    if dbg is not None:
        dbg_out = nc.dram_tensor("dbgo", [len(dbg[1]) * TIL, D], F32,
                                 kind="ExternalOutput").ap()

    with contextlib.ExitStack() as ctx:
        E = ctx.enter_context

        def sb(name, shape, dt):
            return E(nc.sbuf_tensor(name, shape, dt))

        def ps(name, shape, dt):
            return E(nc.psum_tensor(name, shape, dt))

        XWN = 5
        XW = sb("XW", [TIL, XWN, D], F32)
        SMB = sb("SMB", [TIL, NTOK + 4 * D], BF16)
        frow_b = SMB[0:1, 0:NTOK]
        biasq_b = SMB[0:1, NTOK:NTOK + 3 * D]
        biaso_b = SMB[0:1, NTOK + 3 * D:NTOK + 4 * D]
        identb = sb("identb_s", [TIL, TIL], BF16)
        causalT = sb("causalT_s", [TIL, TIL], F32)

        xhT = sb("xhT", [TIL, 2, 8, 512], BF16)
        wqk_buf = sb("wqk_buf", [TIL, 2, 8, TIL], BF16)
        wv_res = sb("wv_res", [TIL, 8, D], BF16)
        woT_res = sb("woT_res", [TIL, 8, D], BF16)
        qT_blk = sb("qT_blk", [TIL, 8, 512], BF16)
        kT_blk = sb("kT_blk", [TIL, 8, 512], BF16)
        v_blk = sb("v_blk", [TIL, 4, D], BF16)
        ktok_blk = sb("ktok_blk", [TIL, 4, D], BF16)
        sT_m = sb("sT_m", [TIL, 4, TIL], BF16)
        state_acc = sb("state_acc", [TIL, 512], F32)
        state_sb = sb("state_sb", [TIL, 512], BF16)
        o_tok = sb("o_tok", [TIL, D], BF16)
        oT_t = sb("oT_t", [TIL, 8, TIL], BF16)
        stat = sb("stat", [TIL, 64], F32)
        stat2 = sb("stat2", [TIL, 64], F32)
        statm = sb("statm", [TIL, 64], F32)
        stati = sb("stati", [TIL, 64], F32)
        a_tile = sb("a_tile", [TIL, 4, D], BF16)
        junk = sb("junk", [TIL, D], F32)
        cst = sb("cst", [TIL, 4], F32)
        junk2s = sb("junk2s", [TIL, D], F32) if spy_out is not None else None
        ynorm = sb("ynorm", [TIL, D], F32)

        A2W = sb("A2W", [TIL, 4, D], F32)
        PTA = sb("PTA", [TIL, 11264], BF16)
        EXT2 = sb("EXT2", [TIL, 4096], BF16)
        w1a_buf = wqk_buf
        w1g_buf = sb("w1g_buf", [TIL, 2, 8, TIL], BF16)
        w2_buf = sb("w2_buf", [TIL, 2, 512], BF16)
        mask_base = sb("mask_base_s", [TIL, 640], BF16)
        maskcur = sb("maskcur", [TIL, 640], BF16)
        qs4 = sb("qs4_s", [TIL, 4, 64], F32)
        gam_cur = sb("gam_cur", [TIL, D], BF16)
        bet_cur = sb("bet_cur", [TIL, D], BF16)
        b1a_col = sb("b1a_col", [TIL, 22], F32)
        b1g_col = sb("b1g_col", [TIL, 22], F32)
        fpart = sb("fpart_s", [TIL, NT], F32)
        U_buf = wqk_buf[:, 0, :, :]
        V_buf = v_blk[:, 0, :]
        gw_buf = wqk_buf[:, 1, :, 0:8]

        v_roll = v_blk[:].rearrange("p a b -> p (a b)").rearrange(
            "p (a b) -> p a b", b=512)
        kT_roll = ktok_blk[:].rearrange("p a b -> p (a b)").rearrange(
            "p (a b) -> p a b", b=1024)
        qT2 = a_tile[:].rearrange("p a b -> p (a b)")[:, 0:2048].rearrange(
            "p (a b) -> p a b", b=512)
        qfb = a_tile[:].rearrange("p a b -> p (a b)")[:, 2048:4096].rearrange(
            "p (a b) -> p a b", b=512)
        kfb = xhT[:].rearrange("p a b c -> p (a b c)")[:, 0:2048].rearrange(
            "p (a b) -> p a b", b=512)
        _xhT_flat = xhT[:].rearrange("p a b c -> p (a b c)")
        junk2v = _xhT_flat[:, 2048:6144].bitcast(F32).rearrange(
            "p (a b) -> p a b", b=512)
        junk3v = PTA[:, 0:4096].bitcast(F32).rearrange(
            "p (a b) -> p a b", b=512)
        tb_raw = PTA[:, 4096:6144].bitcast(F32).rearrange(
            "p (a b) -> p a b", b=256)
        tb_f = PTA[:, 6144:8192].bitcast(F32).rearrange(
            "p (a b) -> p a b", b=256)
        Sm = PTA[:, 8192:9472].bitcast(F32)
        qraw = qT_blk[:].rearrange("p a b -> p (a b)").bitcast(F32).rearrange(
            "p (a b) -> p a b", b=512)
        kraw = kT_blk[:].rearrange("p a b -> p (a b)").bitcast(F32).rearrange(
            "p (a b) -> p a b", b=512)
        hT_ring = EXT2[:, 0:2064].rearrange("p (a b) -> p a b", b=1032)
        PT5 = EXT2[:, 2064:2704].rearrange("p (a b) -> p a b", b=TIL)
        ob2 = EXT2[:, 2704:3216]
        oT2 = EXT2[:, 3216:3728].rearrange("p (a b) -> p a b", b=TIL)
        pT_g = PTA[:, 0:11264].rearrange("p (a b) -> p a b", b=512)
        hTf = _xhT_flat[:, 0:4096].rearrange("p (a b) -> p a b", b=512)
        wffn_v = wv_res[:].rearrange("p a b -> p (a b)")
        wffn2 = woT_res[:].rearrange("p a b -> p (a b)")
        PS_MM = ps("PS_MM", [TIL, 2, 512], F32)
        PS_S = ps("PS_S", [TIL, 1024], F32)
        PS_O = ps("PS_O", [TIL, 2, 512], F32)
        PS_ST = ps("PS_ST", [TIL, 512], F32)
        PS_T = ps("PS_T", [TIL, 1024], BF16)

        t_c125 = k.op("vector", lambda e: e.memset(cst[:, 0:1], 0.125),
                      writes=["cst"])
        k.op("vector", lambda e: e.memset(hT_ring[:, 0, :], 0.0),
             writes=[("hT", 0)])
        k.op("vector", lambda e: e.memset(hT_ring[:, 1, :], 0.0),
             writes=[("hT", 1)])
        k.dma("sync", "cst", fpart[:], fpart_d, writes=["fpart"])
        if spy_out is not None:
            k.op("vector", lambda e: e.memset(junk2s[:], 0.0), writes=["junk2s"])
        k.dma("sync", "cst", causalT[:], causalT_d, writes=["causalT"])
        k.dma("gpsimd", "cstg", identb[:], ident_d, writes=["identb"])
        k.dma("gpsimd", "cstg", frow_b, frow_d, writes=["SMBf"])
        k.dma("gpsimd", "cstg", biasq_b, b_qkv_d, writes=["SMBq"])
        k.dma("gpsimd", "cstg", biaso_b, b_out_d, writes=["SMBo"])
        for a in range(8):
            k.dma("gpsimd", "wres%d" % (a % 2), wv_res[:, a, :],
                  w_qkvT_d[a * TIL:(a + 1) * TIL, 2 * D:3 * D], writes=["wv"])
            k.dma("gpsimd", "wres%d" % (a % 2), woT_res[:, a, :],
                  w_outT_d[a * TIL:(a + 1) * TIL, :], writes=["woT"])

        def ln_stats(srcs, n_feat=D):
            n = len(srcs)
            for ii, (ap, rk) in enumerate(srcs):
                k.op("scalar", lambda e, ap=ap, ii=ii: e.activation(
                    junk[:], ap, AF.Square, bias=0.0, scale=1.0,
                    accum_out=stat[:, ii:ii + 1]),
                    reads=list(rk), writes=["junk", "stat"])
                k.op("vector", lambda e, ap=ap, ii=ii: e.reduce_sum(
                    stat2[:, ii:ii + 1].unsqueeze(2), ap.unsqueeze(1), axis=AX.X),
                    reads=list(rk), writes=["stat2"])
            sl = slice(0, n)
            k.op("vector", lambda e, sl=sl: e.tensor_scalar(
                statm[:, sl], stat2[:, sl], 1.0 / n_feat, 0.0,
                op0=ALU.mult, op1=ALU.add), reads=["stat2"], writes=["statm"])
            k.op("vector", lambda e, sl=sl: e.tensor_scalar(
                stat[:, sl], stat[:, sl], 1.0 / n_feat, 0.0,
                op0=ALU.mult, op1=ALU.add), reads=["stat"], writes=["stat"])
            k.op("vector", lambda e, sl=sl: e.tensor_tensor(
                stat2[:, sl], statm[:, sl], statm[:, sl], op=ALU.mult),
                reads=["statm"], writes=["stat2"])
            k.op("vector", lambda e, sl=sl: e.tensor_tensor(
                stat[:, sl], stat[:, sl], stat2[:, sl], op=ALU.subtract),
                reads=["stat", "stat2"], writes=["stat"])
            k.op("vector", lambda e, sl=sl: e.tensor_scalar(
                stat[:, sl], stat[:, sl], EPS, 0.0, op0=ALU.add, op1=ALU.add),
                reads=["stat"], writes=["stat"])
            k.op("scalar", lambda e, sl=sl: e.activation(
                stat2[:, sl], stat[:, sl], AF.Sqrt, bias=0.0, scale=1.0),
                reads=["stat"], writes=["stat2"])
            return k.op("vector", lambda e, sl=sl: e.reciprocal(
                stati[:, sl], stat2[:, sl]), reads=["stat2"], writes=["stati"])

        def transpose_128(src_ap_fn, nchunk, dst_fn, src_reads, dst_writes):
            for c in range(nchunk):
                k.op("tensor", lambda e, c=c: e.transpose(
                    PS_T[:, c * TIL:(c + 1) * TIL], src_ap_fn(c), identb[:]),
                    reads=list(src_reads) + ["identb"], writes=["PST"])
            t = None
            nh = (nchunk + 3) // 4
            for h in range(nh):
                c0, c1 = 4 * h, min(4 * h + 4, nchunk)
                t = k.op("vector", lambda e, h=h, c0=c0, c1=c1: e.tensor_copy(
                    dst_fn(h), PS_T[:, c0 * TIL:c1 * TIL]
                    .rearrange("p (a b) -> p a b", b=TIL)),
                    reads=["PST"], writes=list(dst_writes))
            return t

        _os = os
        _nb = int(_os.environ.get("KBLOCKS", "99"))
        _ph = int(_os.environ.get("KPHASE", "999"))
        blocks = chunks4(0, NT)[:_nb]
        t_x1_st = [None] * NT
        k.op("vector", lambda e: e.memset(state_acc[:], 0.0), writes=["stacc"])
        k.op("vector", lambda e: e.memset(state_sb[:], 0.0), writes=["stsb"])

        for bi, tiles in enumerate(blocks):
            nt_b = len(tiles)
            bcols = nt_b * TIL
            ring = bi % 2
            lo = tiles[0] * TIL

            for j in tiles:
                k.dma("sync", "xw%d" % (j % 2), XW[:, j % XWN, :],
                      xin[j * TIL:(j + 1) * TIL, :], writes=[("XW", j % XWN)])

            ln_stats([(XW[:, j % XWN, :], [("XW", j % XWN)]) for j in tiles])
            for ii, j in enumerate(tiles):
                k.op("vector", lambda e, j=j, ii=ii: e.tensor_scalar(
                    a_tile[:, ii, :], XW[:, j % XWN, :], statm[:, ii:ii + 1],
                    stati[:, ii:ii + 1], op0=ALU.subtract, op1=ALU.mult),
                    reads=[("XW", j % XWN), "statm", "stati"], writes=[("a", ii)])
                transpose_128(
                    lambda c, ii=ii: a_tile[:, ii, c * TIL:(c + 1) * TIL], 8,
                    lambda h, ii=ii, ring=ring: xhT[:, ring, 4 * h:4 * h + 4,
                                                    ii * TIL:(ii + 1) * TIL],
                    [("a", ii)], [("xhT", ring)])

            for ni in range(16 if _ph >= 2 else 0):
                wring = ni % 2
                k.dma("gpsimd", "wqk%d" % wring, wqk_buf[:, wring, :, :],
                      w_qkvT_d[:, ni * TIL:(ni + 1) * TIL]
                      .rearrange("(a p) b -> p a b", p=TIL),
                      writes=[("wqk", wring)])
                psr = ni % 2
                for d in range(8):
                    k.op("tensor", lambda e, d=d, wring=wring, psr=psr,
                         bcols=bcols, ring=ring: e.matmul(
                        PS_MM[:, psr, 0:bcols], wqk_buf[:, wring, d, :],
                        xhT[:, ring, d, 0:bcols], start=(d == 0), stop=False),
                        reads=[("wqk", wring), ("xhT", ring)],
                        writes=[("PSMM", psr)])
                k.op("tensor", lambda e, ni=ni, psr=psr, bcols=bcols, lo=lo:
                     e.matmul(
                    PS_MM[:, psr, 0:bcols], biasq_b[:, ni * TIL:(ni + 1) * TIL],
                    frow_b[:, lo:lo + bcols], start=False, stop=True),
                    reads=["SMBq", "SMBf"], writes=[("PSMM", psr)])
                if ni < 8:
                    k.op("scalar", lambda e, ni=ni, psr=psr, bcols=bcols:
                         e.activation(
                        qT_blk[:, ni, 0:bcols], PS_MM[:, psr, 0:bcols],
                        AF.Identity, bias=0.0, scale=cst[:, 0:1]),
                        reads=[("PSMM", psr), "cst"], writes=["qT"])
                else:
                    k.op("vector", lambda e, ni=ni, psr=psr, bcols=bcols:
                         e.tensor_copy(
                        kT_blk[:, ni - 8, 0:bcols], PS_MM[:, psr, 0:bcols]),
                        reads=[("PSMM", psr)], writes=["kT"])

            for ii, j in enumerate(tiles if _ph >= 3 else []):
                for nh in range(2):
                    psr = (ii * 2 + nh) % 2
                    for d in range(8):
                        k.op("tensor", lambda e, d=d, ii=ii, nh=nh, psr=psr,
                             ring=ring: e.matmul(
                            PS_MM[:, psr, :],
                            xhT[:, ring, d, ii * TIL:(ii + 1) * TIL],
                            wv_res[:, d, nh * 512:(nh + 1) * 512],
                            start=(d == 0), stop=False),
                            reads=[("xhT", ring), "wv"], writes=[("PSMM", psr)])
                    k.op("tensor", lambda e, nh=nh, psr=psr, j=j: e.matmul(
                        PS_MM[:, psr, :], frow_b[:, j * TIL:(j + 1) * TIL],
                        biasq_b[:, 2 * D + nh * 512:2 * D + (nh + 1) * 512],
                        start=False, stop=True),
                        reads=["SMBq", "SMBf"], writes=[("PSMM", psr)])
                    k.op("vector", lambda e, ii=ii, nh=nh, psr=psr: e.tensor_copy(
                        v_blk[:, ii, nh * 512:(nh + 1) * 512], PS_MM[:, psr, :]),
                        reads=[("PSMM", psr)], writes=["v"])

            for ii, j in enumerate(tiles if _ph >= 4 else []):
                transpose_128(
                    lambda c, ii=ii: kT_blk[:, c, ii * TIL:(ii + 1) * TIL], 8,
                    lambda h, ii=ii: ktok_blk[:, ii, 4 * h * TIL:(4 * h + 4) * TIL]
                    .rearrange("p (a b) -> p a b", b=TIL),
                    ["kT"], ["ktok"])

            for ii, j in enumerate(tiles if _ph >= 5 else []):
                for g in range(4):
                    psr = g % 2
                    for hh in range(4):
                        h = g * 4 + hh
                        p, b = h // 2, (h % 2) * 64
                        k.op("tensor", lambda e, p=p, b=b, hh=hh, psr=psr,
                             ii=ii: e.matmul(
                            PS_S[:, psr * 512 + hh * TIL:psr * 512 + (hh + 1) * TIL],
                            kT_blk[b:b + 64, p, ii * TIL:(ii + 1) * TIL],
                            qT_blk[b:b + 64, p, ii * TIL:(ii + 1) * TIL],
                            start=True, stop=True),
                            reads=["kT", "qT"], writes=[("PSS", psr)])
                    k.op("vector", lambda e, psr=psr: e.tensor_tensor(
                        sT_m[:], PS_S[:, psr * 512:psr * 512 + 512]
                        .rearrange("p (a b) -> p a b", b=TIL),
                        causalT[:].unsqueeze(1).broadcast_to([TIL, 4, TIL]),
                        op=ALU.mult),
                        reads=[("PSS", psr), "causalT"], writes=["sT"])
                    for hh in range(4):
                        h = g * 4 + hh
                        p, b = h // 2, (h % 2) * 64
                        bank, col = h // 8, (h % 8) * 64
                        k.op("tensor", lambda e, h=h, hh=hh, bank=bank, col=col,
                             ii=ii: e.matmul(
                            PS_O[:, bank, col:col + 64], sT_m[:, hh, :],
                            v_blk[:, ii, h * 64:(h + 1) * 64],
                            start=True, stop=False),
                            reads=["sT", "v", "stsb"], writes=[("PSO", bank)])
                        k.op("tensor", lambda e, h=h, p=p, b=b, bank=bank,
                             col=col, ii=ii: e.matmul(
                            PS_O[:, bank, col:col + 64],
                            qT_blk[b:b + 64, p, ii * TIL:(ii + 1) * TIL],
                            state_sb[b:b + 64, p * 64:(p + 1) * 64],
                            start=False, stop=True),
                            reads=["qT", "stsb"], writes=[("PSO", bank)])

                for h in range(16):
                    p, b = h // 2, (h % 2) * 64
                    k.op("tensor", lambda e, h=h, p=p, b=b, ii=ii: e.matmul(
                        PS_ST[b:b + 64, p * 64:(p + 1) * 64],
                        ktok_blk[:, ii, h * 64:(h + 1) * 64],
                        v_blk[:, ii, h * 64:(h + 1) * 64],
                        start=True, stop=True, tile_position=(0, b)),
                        reads=["ktok", "v"], writes=["PSST"])
                k.op("vector", lambda e: e.tensor_tensor(
                    state_acc[:], state_acc[:], PS_ST[:], op=ALU.add),
                    reads=["PSST", "stacc"], writes=["stacc"])
                k.op("vector", lambda e: e.tensor_copy(state_sb[:], state_acc[:]),
                    reads=["stacc"], writes=["stsb"])

                for bank in range(2):
                    k.op("scalar", lambda e, bank=bank: e.activation(
                        o_tok[:, bank * 512:(bank + 1) * 512], PS_O[:, bank, :],
                        AF.Identity, bias=0.0, scale=1.0),
                        reads=[("PSO", bank)], writes=["otok"])
                transpose_128(
                    lambda c: o_tok[:, c * TIL:(c + 1) * TIL], 8,
                    lambda h: oT_t[:, 4 * h:4 * h + 4, :], ["otok"], ["oT"])

                for nh in range(2 if _ph >= 6 else 0):
                    psr = nh
                    for c in range(8):
                        k.op("tensor", lambda e, c=c, nh=nh, psr=psr: e.matmul(
                            PS_MM[:, psr, :], oT_t[:, c, :],
                            woT_res[:, c, nh * 512:(nh + 1) * 512],
                            start=(c == 0), stop=False),
                            reads=["oT", "woT"], writes=[("PSMM", psr)])
                    k.op("tensor", lambda e, nh=nh, psr=psr, j=j: e.matmul(
                        PS_MM[:, psr, :], frow_b[:, j * TIL:(j + 1) * TIL],
                        biaso_b[:, nh * 512:(nh + 1) * 512],
                        start=False, stop=True),
                        reads=["SMBf", "SMBo"], writes=[("PSMM", psr)])

                if _ph < 7:
                    continue
                k.op("scalar", lambda e: e.activation(
                    junk[:, 0:512], PS_MM[:, 0, :], AF.Square, bias=0.0,
                    scale=1.0, accum_out=stat[:, 8:9]),
                    reads=[("PSMM", 0)], writes=["junk", "stat"])
                k.op("scalar", lambda e: e.activation(
                    junk[:, 512:1024], PS_MM[:, 1, :], AF.Square, bias=0.0,
                    scale=1.0, accum_out=stat[:, 9:10]),
                    reads=[("PSMM", 1)], writes=["junk", "stat"])
                if _ph < 8:
                    continue
                k.op("vector", lambda e: e.reduce_sum(
                    stat2[:, 8:9].unsqueeze(2), PS_MM[:, 0, :].unsqueeze(1),
                    axis=AX.X), reads=[("PSMM", 0)], writes=["stat2"])
                k.op("vector", lambda e: e.reduce_sum(
                    stat2[:, 9:10].unsqueeze(2), PS_MM[:, 1, :].unsqueeze(1),
                    axis=AX.X), reads=[("PSMM", 1)], writes=["stat2"])
                if _ph < 82:
                    continue
                k.op("vector", lambda e: e.tensor_tensor(
                    statm[:, 8:9], stat2[:, 8:9], stat2[:, 9:10], op=ALU.add),
                    reads=["stat2"], writes=["statm"])
                k.op("vector", lambda e: e.tensor_scalar(
                    statm[:, 8:9], statm[:, 8:9], 1.0 / D, 0.0,
                    op0=ALU.mult, op1=ALU.add), reads=["statm"], writes=["statm"])
                k.op("vector", lambda e: e.tensor_tensor(
                    stat[:, 8:9], stat[:, 8:9], stat[:, 9:10], op=ALU.add),
                    reads=["stat"], writes=["stat"])
                k.op("vector", lambda e: e.tensor_scalar(
                    stat[:, 8:9], stat[:, 8:9], 1.0 / D, 0.0,
                    op0=ALU.mult, op1=ALU.add), reads=["stat"], writes=["stat"])
                k.op("vector", lambda e: e.tensor_tensor(
                    stat2[:, 10:11], statm[:, 8:9], statm[:, 8:9], op=ALU.mult),
                    reads=["statm"], writes=["stat2"])
                k.op("vector", lambda e: e.tensor_tensor(
                    stat[:, 8:9], stat[:, 8:9], stat2[:, 10:11],
                    op=ALU.subtract), reads=["stat", "stat2"], writes=["stat"])
                k.op("vector", lambda e: e.tensor_scalar(
                    stat[:, 8:9], stat[:, 8:9], EPS, 0.0, op0=ALU.add,
                    op1=ALU.add), reads=["stat"], writes=["stat"])
                if _ph < 84:
                    continue
                k.op("scalar", lambda e: e.activation(
                    stat2[:, 8:9], stat[:, 8:9], AF.Sqrt, bias=0.0, scale=1.0),
                    reads=["stat"], writes=["stat2"])
                k.op("vector", lambda e: e.reciprocal(
                    stati[:, 8:9], stat2[:, 8:9]), reads=["stat2"],
                    writes=["stati"])
                if _ph < 90:
                    continue
                k.op("vector", lambda e: e.tensor_scalar(
                    ynorm[:, 0:512], PS_MM[:, 0, :], statm[:, 8:9],
                    stati[:, 8:9], op0=ALU.subtract, op1=ALU.mult),
                    reads=[("PSMM", 0), "statm", "stati"], writes=["ynorm"])
                k.op("vector", lambda e: e.tensor_scalar(
                    ynorm[:, 512:1024], PS_MM[:, 1, :], statm[:, 8:9],
                    stati[:, 8:9], op0=ALU.subtract, op1=ALU.mult),
                    reads=[("PSMM", 1), "statm", "stati"], writes=["ynorm"])
                k.op("vector", lambda e, j=j: e.tensor_tensor(
                    XW[:, j % XWN, :], XW[:, j % XWN, :], ynorm[:], op=ALU.add),
                    reads=["ynorm", ("XW", j % XWN)], writes=[("XW", j % XWN)])
                if _ph < 100:
                    continue
                t_x1_st[j] = k.dma("sync", "xst", x1_d[j * TIL:(j + 1) * TIL, :],
                                   XW[:, j % XWN, :], reads=[("XW", j % XWN)],
                                   writes=[("x1", j)])

        def barrier():
            toks = [k.last(s) for s in list(k.cnt)]
            for e in ENGINES:
                for t in toks:
                    k.wait(e, t)

        def wffn_rhs(d, part):
            qq = d * 3 + part
            if qq < 16:
                return wffn_v[:, qq * 512:(qq + 1) * 512]
            return wffn2[:, (qq - 16) * 512:(qq - 16 + 1) * 512]

        def wffn_keys():
            return ["wv", "woT"]

        def wo_rhs(c, nh):
            off = 4096 + c * 1024 + nh * 512
            return wffn2[:, off:off + 512]

        Pm = PTA[:, 9472:10112]

        def prep_tile(ind, in_done, j, do_ln):
            if do_ln:
                k.dma("sync", "xw%d" % (j % 2), XW[:, j % XWN, :],
                      ind[j * TIL:(j + 1) * TIL, :],
                      waits=[in_done[j]] if in_done else (),
                      writes=[("XW", j % XWN)])
                ln_stats([(XW[:, j % XWN, :], [("XW", j % XWN)])])
                k.op("vector", lambda e, j=j: e.tensor_scalar(
                    A2W[:, j % 4, :], XW[:, j % XWN, :], statm[:, 0:1],
                    stati[:, 0:1], op0=ALU.subtract, op1=ALU.mult),
                    reads=[("XW", j % XWN), "statm", "stati"],
                    writes=[("A2W", j % 4)])
            else:
                k.dma("sync", "xw%d" % (j % 2), A2W[:, j % 4, :],
                      ind[j * TIL:(j + 1) * TIL, :],
                      waits=[in_done[j]] if in_done else (),
                      writes=[("A2W", j % 4)])
            k.op("scalar", lambda e, j=j: e.activation(
                o_tok[:], A2W[:, j % 4, :], AF.Identity, bias=0.0, scale=1.0),
                reads=[("A2W", j % 4)], writes=["otok"])
            for c in range(8):
                k.op("tensor", lambda e, c=c: e.transpose(
                    PS_T[:, c * TIL:(c + 1) * TIL],
                    o_tok[:, c * TIL:(c + 1) * TIL], identb[:]),
                    reads=["otok", "identb"], writes=["PST"])
            hr = hT_ring[:, j % 2, :].rearrange("p (a b) -> p a b", b=129)
            hprev = hT_ring[:, (j + 1) % 2, :].rearrange("p (a b) -> p a b", b=129)
            for h in range(2):
                k.op("vector", lambda e, h=h, hr=hr: e.tensor_copy(
                    hr[:, 4 * h:4 * h + 4, 1:129],
                    PS_T[:, 4 * h * TIL:(4 * h + 4) * TIL]
                    .rearrange("p (a b) -> p a b", b=TIL)),
                    reads=["PST"], writes=[("hT", j % 2)])
            k.op("vector", lambda e, hr=hr, hprev=hprev: e.tensor_copy(
                hr[:, :, 0:1], hprev[:, :, 128:129]),
                reads=[("hT", (j + 1) % 2)], writes=[("hT", j % 2)])

        def attn_tile(l, j, ii, out_done):
            key_tiles = list(range(j - 4, j + 1))
            segs = []
            poff = 0
            run_start = key_tiles[0] % 8
            run_len = 1
            for t in key_tiles[1:]:
                if t % 8 == (run_start + run_len) % 8 and run_start + run_len < 8:
                    run_len += 1
                else:
                    segs.append((poff, run_start * TIL, run_len * TIL))
                    poff += run_len * TIL
                    run_start = t % 8
                    run_len = 1
            segs.append((poff, run_start * TIL, run_len * TIL))
            sub = []
            for (po, ro, w) in segs:
                while w > 0:
                    lim = 512 - (po % 512)
                    ww = min(w, lim, 512)
                    sub.append((po, ro, ww))
                    po, ro, w = po + ww, ro + ww, w - ww
            if 17 <= j <= 20:
                k.dma("gpsimd", "cstg", maskcur[:],
                      mask_sp_d[j - 17], writes=["msp"])
            for h in range(8):
                p, b = h // 2, (h % 2) * 64
                for (po, ro, w) in sub:
                    k.op("tensor", lambda e, p=p, b=b, po=po, ro=ro, w=w,
                         ii=ii: e.matmul(
                        PS_S[:, po:po + w],
                        qT2[b:b + 64, ii, p * TIL:(p + 1) * TIL],
                        kT_roll[b:b + 64, p, ro:ro + w],
                        start=True, stop=True),
                        reads=["qT2"] + [("ktr", s % 8) for s in key_tiles],
                        writes=[("PSS", 0), ("PSS", 1)])
                mk = maskcur[:] if 17 <= j <= 20 else mask_base[:]
                mkk = "msp" if 17 <= j <= 20 else "mb"
                k.op("vector", lambda e, mk=mk: e.tensor_tensor(
                    Sm[:], PS_S[:, 0:640], mk, op=ALU.add),
                    reads=[("PSS", 0), ("PSS", 1), mkk], writes=["Sm"])
                k.op("vector", lambda e: e.reduce_max(
                    stat2[:, 40:41], Sm[:], axis=AX.X, negate=True),
                    reads=["Sm"], writes=["stat2"])
                k.op("scalar", lambda e: e.activation(
                    Pm[:], Sm[:], AF.Exp, bias=stat2[:, 40:41], scale=1.0,
                    accum_out=stat[:, 40:41]),
                    reads=["Sm", "stat2"], writes=["Pm", "stat"])
                k.op("vector", lambda e, h=h: e.reciprocal(
                    stati[:, 40 + h:41 + h], stat[:, 40:41]),
                    reads=["stat"], writes=["stati"])
                for c in range(5):
                    k.op("tensor", lambda e, c=c: e.transpose(
                        PS_T[:, c * TIL:(c + 1) * TIL],
                        Pm[:, c * TIL:(c + 1) * TIL], identb[:]),
                        reads=["Pm", "identb"], writes=["PST"])
                k.op("vector", lambda e: e.tensor_copy(
                    PT5[:], PS_T[:, 0:640].rearrange("p (a b) -> p a b", b=TIL)),
                    reads=["PST"], writes=["PT5"])
                for t5, jt in enumerate(key_tiles):
                    k.op("tensor", lambda e, t5=t5, jt=jt, h=h, j=j: e.matmul(
                        PS_O[:, j % 2, h * 64:(h + 1) * 64], PT5[:, t5, :],
                        v_roll[:, jt % 8, h * 64:(h + 1) * 64],
                        start=(t5 == 0), stop=(t5 == 4)),
                        reads=["PT5", ("vr", jt % 8)], writes=[("PSO", j % 2)])
            for h in range(8):
                k.op("scalar", lambda e, h=h, j=j: e.activation(
                    ob2[:, h * 64:(h + 1) * 64],
                    PS_O[:, j % 2, h * 64:(h + 1) * 64],
                    AF.Identity, bias=0.0, scale=stati[:, 40 + h:41 + h]),
                    reads=[("PSO", j % 2), "stati"], writes=["ob2"])
            for c in range(4):
                k.op("tensor", lambda e, c=c: e.transpose(
                    PS_T[:, c * TIL:(c + 1) * TIL],
                    ob2[:, c * TIL:(c + 1) * TIL], identb[:]),
                    reads=["ob2", "identb"], writes=["PST"])
            k.op("vector", lambda e: e.tensor_copy(
                oT2[:], PS_T[:, 0:512].rearrange("p (a b) -> p a b", b=TIL)),
                reads=["PST"], writes=["oT2"])
            for nh in range(2):
                for c in range(4):
                    k.op("tensor", lambda e, c=c, nh=nh: e.matmul(
                        PS_MM[:, nh, :], oT2[:, c, :], wo_rhs(c, nh),
                        start=(c == 0), stop=(c == 3)),
                        reads=["oT2", "woT"], writes=[("PSMM", nh)])
                k.op("vector", lambda e, nh=nh, j=j: e.scalar_tensor_tensor(
                    ynorm[:, nh * 512:(nh + 1) * 512], PS_MM[:, nh, :],
                    fpart[:, j:j + 1], A2W[:, j % 4, nh * 512:(nh + 1) * 512],
                    op0=ALU.mult, op1=ALU.add),
                    reads=[("PSMM", nh), "fpart", ("A2W", j % 4)],
                    writes=["ynorm"])
            out_done[j] = k.dma("sync", "xst", y_d[j * TIL:(j + 1) * TIL, :],
                                ynorm[:], reads=["ynorm"], writes=[("y", j)])

        def emit_l0ffn_attn(l, in_stream, in_done, kv_lo, q_lo, do_ln):
            ind = streams[in_stream]
            y_done = [None] * NT
            for qq in range(24):
                dst = wffn_rhs(qq // 3, qq % 3)
                d, part = qq // 3, qq % 3
                k.dma("gpsimd", "wres%d" % (qq % 2), dst,
                      w_ffn_qkv_d[l, d * TIL:(d + 1) * TIL,
                                  part * 512:(part + 1) * 512],
                      writes=wffn_keys())
            for c in range(4):
                for nh in range(2):
                    k.dma("gpsimd", "wres%d" % ((c * 2 + nh) % 2),
                          wo_rhs(c, nh),
                          w_ffn_wo_d[l, c * TIL:(c + 1) * TIL,
                                     nh * 512:(nh + 1) * 512], writes=["woT"])
            k.dma("sync", "cst", qs4[:], qs4_d[l], writes=["qs4"])
            k.dma("gpsimd", "cstg", mask_base[:], mask_base_d, writes=["mb"])


            prep_tile(ind, in_done, kv_lo - 1, do_ln)
            for tiles in chunks4(kv_lo, NT):
                nt_b = len(tiles)
                i0t = tiles[0]
                k.dma("sync", "tab", tb_raw[:, 0:nt_b, :],
                      tab4_d[i0t:i0t + nt_b].rearrange("a p b -> p a b"),
                      writes=["tbr"])
                for t4 in range(4):
                    rollix = t4 % 2
                    qk = 0 if t4 < 2 else 2
                    k.op("vector", lambda e, t4=t4, rollix=rollix, qk=qk,
                         nt_b=nt_b: e.tensor_tensor(
                        tb_f[:, 0:nt_b, t4 * 64:(t4 + 1) * 64],
                        tb_raw[:, 0:nt_b, t4 * 64:(t4 + 1) * 64],
                        qs4[:, qk + rollix, :].unsqueeze(1)
                        .broadcast_to([TIL, nt_b, 64]), op=ALU.mult),
                        reads=["tbr", "qs4"], writes=["tbf"])
                for ii, j in enumerate(tiles):
                    prep_tile(ind, in_done, j, do_ln)
                    hr = hT_ring[:, j % 2, :].rearrange("p (a b) -> p a b", b=129)
                    for part in range(3):
                        psr = part % 2
                        for d in range(8):
                            sh = 1 if d < 4 else 0
                            k.op("tensor", lambda e, d=d, part=part, psr=psr,
                                 sh=sh, hr=hr: e.matmul(
                                PS_MM[:, psr, :], hr[:, d, sh:sh + TIL],
                                wffn_rhs(d, part), start=(d == 0),
                                stop=(d == 7)),
                                reads=[("hT", j % 2)] + wffn_keys(),
                                writes=[("PSMM", psr)])
                        if part == 0:
                            k.op("vector", lambda e, ii=ii, psr=psr:
                                 e.tensor_copy(qraw[:, ii, :], PS_MM[:, psr, :]),
                                 reads=[("PSMM", psr)], writes=["qraw"])
                        elif part == 1:
                            k.op("vector", lambda e, ii=ii, psr=psr:
                                 e.tensor_copy(kraw[:, ii, :], PS_MM[:, psr, :]),
                                 reads=[("PSMM", psr)], writes=["kraw"])
                        else:
                            k.op("vector", lambda e, j=j, psr=psr:
                                 e.tensor_copy(v_roll[:, j % 8, :],
                                               PS_MM[:, psr, :]),
                                 reads=[("PSMM", psr)], writes=[("vr", j % 8)])

                for side in range(2):
                    raw = qraw if side == 0 else kraw
                    rk = "qraw" if side == 0 else "kraw"
                    fb = qfb if side == 0 else kfb
                    fbk = "qfb" if side == 0 else "kfb"
                    toff = 0 if side == 0 else 128
                    sl2 = slice(0, nt_b * 8)
                    k.op("scalar", lambda e, raw=raw, nt_b=nt_b: e.activation(
                        junk2v[:, 0:nt_b, :], raw[:, 0:nt_b, :], AF.Square,
                        bias=0.0, scale=1.0), reads=[rk], writes=["junk2"])
                    k.op("vector", lambda e, nt_b=nt_b, sl2=sl2: e.reduce_sum(
                        stat[:, sl2].rearrange("p (a b) -> p a b", b=8)
                        .unsqueeze(3),
                        junk2v[:, 0:nt_b, :].rearrange("p a (b c) -> p a b c",
                                                       c=64), axis=AX.X),
                        reads=["junk2"], writes=["stat"])
                    k.op("scalar", lambda e, sl2=sl2: e.activation(
                        stat2[:, sl2], stat[:, sl2], AF.Sqrt, bias=0.0,
                        scale=1.0), reads=["stat"], writes=["stat2"])
                    k.op("vector", lambda e, sl2=sl2: e.tensor_scalar(
                        stat2[:, sl2], stat2[:, sl2], 1e-12, 0.0,
                        op0=ALU.max, op1=ALU.add),
                        reads=["stat2"], writes=["stat2"])
                    k.op("vector", lambda e, sl2=sl2: e.reciprocal(
                        stati[:, sl2], stat2[:, sl2]),
                        reads=["stat2"], writes=["stati"])
                    k.op("vector", lambda e, nt_b=nt_b: e.tensor_copy(
                        junk3v[:, 0:nt_b, :].rearrange("p a (b c) -> p a b c",
                                                       c=64),
                        stati[:, 0:nt_b * 8].rearrange("p (a b) -> p a b", b=8)
                        .unsqueeze(3).broadcast_to([TIL, nt_b, 8, 64])),
                        reads=["stati"], writes=["junk3"])
                    k.op("vector", lambda e, raw=raw, nt_b=nt_b: e.tensor_tensor(
                        raw[:, 0:nt_b, :], raw[:, 0:nt_b, :],
                        junk3v[:, 0:nt_b, :], op=ALU.mult),
                        reads=[rk, "junk3"], writes=[rk])
                    vw = lambda a: a.rearrange("p t (h d) -> p t h d", d=64)
                    k.op("vector", lambda e, raw=raw, nt_b=nt_b, toff=toff,
                         vw=vw: e.tensor_tensor(
                        vw(junk2v[:, 0:nt_b, :])[:, :, :, 0:32],
                        vw(raw[:, 0:nt_b, :])[:, :, :, 32:64],
                        tb_f[:, 0:nt_b, toff + 64:toff + 96].unsqueeze(2)
                        .broadcast_to([TIL, nt_b, 8, 32]), op=ALU.mult),
                        reads=[rk, "tbf"], writes=["junk2"])
                    k.op("vector", lambda e, raw=raw, nt_b=nt_b, toff=toff,
                         vw=vw: e.tensor_tensor(
                        vw(junk2v[:, 0:nt_b, :])[:, :, :, 32:64],
                        vw(raw[:, 0:nt_b, :])[:, :, :, 0:32],
                        tb_f[:, 0:nt_b, toff + 96:toff + 128].unsqueeze(2)
                        .broadcast_to([TIL, nt_b, 8, 32]), op=ALU.mult),
                        reads=[rk, "tbf"], writes=["junk2"])
                    k.op("vector", lambda e, raw=raw, nt_b=nt_b, toff=toff,
                         vw=vw: e.tensor_tensor(
                        vw(junk3v[:, 0:nt_b, :]), vw(raw[:, 0:nt_b, :]),
                        tb_f[:, 0:nt_b, toff + 0:toff + 64].unsqueeze(2)
                        .broadcast_to([TIL, nt_b, 8, 64]), op=ALU.mult),
                        reads=[rk, "tbf"], writes=["junk3"])
                    k.op("vector", lambda e, fb=fb, nt_b=nt_b: e.tensor_tensor(
                        fb[:, 0:nt_b, :], junk2v[:, 0:nt_b, :],
                        junk3v[:, 0:nt_b, :], op=ALU.add),
                        reads=["junk2", "junk3"], writes=[fbk])

                for ii, j in enumerate(tiles):
                    transpose_128(
                        lambda c, ii=ii: qfb[:, ii, c * TIL:(c + 1) * TIL], 4,
                        lambda h, ii=ii: qT2[:, ii, :]
                        .rearrange("p (a b) -> p a b", b=TIL),
                        ["qfb"], ["qT2"])
                    transpose_128(
                        lambda c, ii=ii: kfb[:, ii, c * TIL:(c + 1) * TIL], 4,
                        lambda h, j=j: kT_roll[:, :, (j % 8) * TIL:
                                               (j % 8 + 1) * TIL],
                        ["kfb"], [("ktr", j % 8)])

                for ii, j in enumerate(tiles):
                    if j >= q_lo:
                        attn_tile(l, j, ii, y_done)
            return y_done

        def emit_l0ffn_ffn(l, y_done, out_stream, q_lo):
            outd = streams[out_stream]
            out_done = [None] * NT
            k.dma("gpsimd", "cstg", gam_cur[:], gam_d[l], writes=["gam"])
            k.dma("gpsimd", "cstg", bet_cur[:], bet_d[l], writes=["bet"])
            k.dma("sync", "cst", b1a_col[:], b1a_d[l], writes=["b1a"])
            k.dma("sync", "cst", b1g_col[:], b1g_d[l], writes=["b1g"])
            k.dma("gpsimd", "cstg", biaso_b, b2_d[l], writes=["SMBo"])
            for tiles in chunks4(q_lo, NT):
                nt_b = len(tiles)
                for ii, j in enumerate(tiles):
                    k.dma("sync", "xw%d" % (j % 2), XW[:, j % XWN, :],
                          y_d[j * TIL:(j + 1) * TIL, :],
                          waits=[y_done[j]], writes=[("XW", j % XWN)])
                for ii, j in enumerate(tiles):
                    k.dma("sync", "hsh%d" % (ii % 2), A2W[:, ii, 0:512],
                          XW[:, j % XWN, 0:512], reads=[("XW", j % XWN)],
                          writes=[("A2W", ii)])
                    if j == q_lo:
                        k.dma("sync", "hsh%d" % (ii % 2), A2W[0:1, ii, 512:1024],
                              XW[0:1, j % XWN, 512:1024], reads=[("XW", j % XWN)],
                              writes=[("A2W", ii)])
                    else:
                        k.dma("sync", "hsh%d" % (ii % 2), A2W[0:1, ii, 512:1024],
                              XW[127:128, (j - 1) % XWN, 512:1024],
                              reads=[("XW", (j - 1) % XWN)], writes=[("A2W", ii)])
                    k.dma("sync", "hsh%d" % (ii % 2), A2W[1:128, ii, 512:1024],
                          XW[0:127, j % XWN, 512:1024], reads=[("XW", j % XWN)],
                          writes=[("A2W", ii)])
                ln_stats([(A2W[:, ii, :], [("A2W", ii)])
                          for ii in range(nt_b)])
                for ii, j in enumerate(tiles):
                    k.op("vector", lambda e, ii=ii: e.tensor_scalar(
                        junk[:], A2W[:, ii, :], statm[:, ii:ii + 1],
                        stati[:, ii:ii + 1], op0=ALU.subtract, op1=ALU.mult),
                        reads=[("A2W", ii), "statm", "stati"], writes=["junk"])
                    k.op("vector", lambda e: e.tensor_tensor(
                        junk[:], junk[:], gam_cur[:], op=ALU.mult),
                        reads=["junk", "gam"], writes=["junk"])
                    k.op("vector", lambda e: e.tensor_tensor(
                        o_tok[:], junk[:], bet_cur[:], op=ALU.add),
                        reads=["junk", "bet"], writes=["otok"])
                    for c in range(8):
                        k.op("tensor", lambda e, c=c: e.transpose(
                            PS_T[:, c * TIL:(c + 1) * TIL],
                            o_tok[:, c * TIL:(c + 1) * TIL], identb[:]),
                            reads=["otok", "identb"], writes=["PST"])
                    for h in range(2):
                        k.op("vector", lambda e, h=h, ii=ii: e.tensor_copy(
                            hTf[:, 4 * h:4 * h + 4, ii * TIL:(ii + 1) * TIL],
                            PS_T[:, 4 * h * TIL:(4 * h + 4) * TIL]
                            .rearrange("p (a b) -> p a b", b=TIL)),
                            reads=["PST"],
                            writes=["hTf", "kfb", "junk2"])
                PT_KEYS = ["pT", "junk3", "tbr", "tbf", "Sm", "Pm"]
                bcol = nt_b * TIL
                for cc in range(22):
                    wdt = 42 if cc == 21 else TIL
                    wr = cc % 2
                    k.dma("gpsimd", "w1a%d" % wr, w1a_buf[:, wr, :, 0:wdt],
                          w_ffn_w1_d[l, :, cc * TIL:cc * TIL + wdt]
                          .rearrange("(a p) b -> p a b", p=TIL),
                          writes=[("wqk", wr)])
                    k.dma("gpsimd", "w1g%d" % wr, w1g_buf[:, wr, :, 0:wdt],
                          w_ffn_w1_d[l, :, 2730 + cc * TIL:2730 + cc * TIL + wdt]
                          .rearrange("(a p) b -> p a b", p=TIL),
                          writes=[("w1g", wr)])
                    for d in range(8):
                        k.op("tensor", lambda e, d=d, cc=cc, wr=wr, wdt=wdt,
                             bcol=bcol: e.matmul(
                            PS_MM[0:wdt, 0, 0:bcol],
                            w1a_buf[:, wr, d, 0:wdt], hTf[:, d, 0:bcol],
                            start=(d == 0), stop=(d == 7)),
                            reads=[("wqk", wr), "hTf"], writes=[("PSMM", 0)])
                        k.op("tensor", lambda e, d=d, cc=cc, wr=wr, wdt=wdt,
                             bcol=bcol: e.matmul(
                            PS_S[0:wdt, 0:bcol], w1g_buf[:, wr, d, 0:wdt],
                            hTf[:, d, 0:bcol], start=(d == 0), stop=(d == 7)),
                            reads=[("w1g", wr), "hTf"],
                            writes=[("PSS", 0), ("PSS", 1)])
                    k.op("scalar", lambda e, cc=cc, wdt=wdt, bcol=bcol:
                         e.activation(
                        junk[0:wdt, 0:bcol], PS_S[0:wdt, 0:bcol], AF.Gelu,
                        bias=b1g_col[0:wdt, cc:cc + 1], scale=1.0),
                        reads=[("PSS", 0), ("PSS", 1), "b1g"], writes=["junk"])
                    k.op("vector", lambda e, cc=cc, wdt=wdt, bcol=bcol:
                         e.scalar_tensor_tensor(
                        pT_g[0:wdt, cc, 0:bcol], PS_MM[0:wdt, 0, 0:bcol],
                        b1a_col[0:wdt, cc:cc + 1], junk[0:wdt, 0:bcol],
                        op0=ALU.add, op1=ALU.mult),
                        reads=[("PSMM", 0), "b1a", "junk"], writes=PT_KEYS)
                PSB = [PS_MM[:, 0, :], PS_MM[:, 1, :],
                       PS_S[:, 0:512], PS_S[:, 512:1024]]
                PSK = [("PSMM", 0), ("PSMM", 1), ("PSS", 0), ("PSS", 1)]
                for half in range(2):
                    for cc in range(22):
                        wdt = 42 if cc == 21 else TIL
                        wr = cc % 2
                        k.dma("gpsimd", "w2%d" % wr, w2_buf[0:wdt, wr, :],
                              w_ffn_w2_d[l, cc * TIL:cc * TIL + wdt,
                                         half * 512:(half + 1) * 512],
                              writes=[("w2", wr)])
                        for ti in range(nt_b):
                            k.op("tensor", lambda e, cc=cc, wr=wr, wdt=wdt,
                                 ti=ti: e.matmul(
                                PSB[ti], pT_g[0:wdt, cc, ti * TIL:(ti + 1) * TIL],
                                w2_buf[0:wdt, wr, :],
                                start=(cc == 0), stop=False),
                                reads=[("w2", wr), "pT"],
                                writes=[PSK[ti]])
                    for ti, j in enumerate(tiles):
                        k.op("tensor", lambda e, ti=ti, j=j, half=half: e.matmul(
                            PSB[ti], frow_b[:, j * TIL:(j + 1) * TIL],
                            biaso_b[:, half * 512:(half + 1) * 512],
                            start=False, stop=True),
                            reads=["SMBf", "SMBo"], writes=[PSK[ti]])
                        k.op("vector", lambda e, ti=ti, j=j, half=half:
                             e.scalar_tensor_tensor(
                            A2W[:, ti, half * 512:(half + 1) * 512], PSB[ti],
                            fpart[:, j:j + 1],
                            XW[:, j % XWN, half * 512:(half + 1) * 512],
                            op0=ALU.mult, op1=ALU.add),
                            reads=[PSK[ti], "fpart", ("XW", j % XWN)],
                            writes=[("A2W", ti)])
                for ti, j in enumerate(tiles):
                    out_done[j] = k.dma(
                        "sync", "xst", outd[j * TIL:(j + 1) * TIL, :],
                        A2W[:, ti, :], reads=[("A2W", ti)],
                        writes=[(out_stream, j)])
            return out_done

        def emit_l1_attn(in_done):
            ind = streams["x2"]
            outd = streams["x3"]
            out_done = [None] * NT
            kT1 = v_blk[:].rearrange("p a b -> p (a b)").rearrange(
                "p (a b) -> p a b", b=512)
            v1 = ktok_blk
            vprev = a_tile[:, 0, :]
            kTprev = a_tile[:, 1, :].rearrange("p (a b) -> p a b", b=TIL)
            for a in range(8):
                k.dma("gpsimd", "wres%d" % (a % 2), wv_res[:, a, :],
                      w_l1v_d[a * TIL:(a + 1) * TIL, :], writes=["wv"])
            k.dma("gpsimd", "cstg", mask_base[:, 0:256], m1_base_d,
                  writes=["mb"])
            k.dma("gpsimd", "cstg", maskcur[:, 0:256], m1_sp_d, writes=["msp"])
            blocks5 = chunks4(16, NT)
            for bi, tiles in enumerate(blocks5):
                nt_b = len(tiles)
                bcols = nt_b * TIL
                ring = bi % 2
                for ii, j in enumerate(tiles):
                    k.dma("sync", "xw%d" % (j % 2), XW[:, j % XWN, :],
                          ind[j * TIL:(j + 1) * TIL, :],
                          waits=[in_done[j]], writes=[("XW", j % XWN)])
                ln_stats([(XW[:, j % XWN, :], [("XW", j % XWN)])
                          for j in tiles])
                for ii, j in enumerate(tiles):
                    k.op("scalar", lambda e, j=j, ii=ii: e.activation(
                        o_tok[:], XW[:, j % XWN, :], AF.Identity,
                        bias=statm[:, ii:ii + 1], scale=stati[:, ii:ii + 1]),
                        reads=[("XW", j % XWN), "statm", "stati"],
                        writes=["otok"]) if False else None
                    k.op("vector", lambda e, j=j, ii=ii: e.tensor_scalar(
                        junk[:], XW[:, j % XWN, :], statm[:, ii:ii + 1],
                        stati[:, ii:ii + 1], op0=ALU.subtract, op1=ALU.mult),
                        reads=[("XW", j % XWN), "statm", "stati"],
                        writes=["junk"])
                    k.op("scalar", lambda e: e.activation(
                        o_tok[:], junk[:], AF.Identity, bias=0.0, scale=1.0),
                        reads=["junk"], writes=["otok"])
                    transpose_128(
                        lambda c: o_tok[:, c * TIL:(c + 1) * TIL], 8,
                        lambda h, ii=ii, ring=ring: xhT[:, ring, 4 * h:4 * h + 4,
                                                        ii * TIL:(ii + 1) * TIL],
                        ["otok"], [("xhT", ring)])
                for ni in range(16):
                    wring = ni % 2
                    wd = w_l1q_d if ni < 8 else w_l1k_d
                    nc_ = ni % 8
                    k.dma("gpsimd", "wqk%d" % wring, wqk_buf[:, wring, :, :],
                          wd[:, nc_ * TIL:(nc_ + 1) * TIL]
                          .rearrange("(a p) b -> p a b", p=TIL),
                          writes=[("wqk", wring)])
                    psr = ni % 2
                    for d in range(8):
                        k.op("tensor", lambda e, d=d, wring=wring, psr=psr,
                             bcols=bcols, ring=ring: e.matmul(
                            PS_MM[:, psr, 0:bcols], wqk_buf[:, wring, d, :],
                            xhT[:, ring, d, 0:bcols], start=(d == 0),
                            stop=(d == 7)),
                            reads=[("wqk", wring), ("xhT", ring)],
                            writes=[("PSMM", psr)])
                    if ni < 8:
                        k.op("scalar", lambda e, ni=ni, psr=psr, bcols=bcols:
                             e.activation(
                            qT_blk[:, ni, 0:bcols], PS_MM[:, psr, 0:bcols],
                            AF.Identity, bias=0.0, scale=cst[:, 0:1]),
                            reads=[("PSMM", psr), "cst"], writes=["qT"])
                    else:
                        for ii, j in enumerate(tiles):
                            k.op("vector", lambda e, ni=ni, psr=psr, ii=ii,
                                 j=j: e.tensor_copy(
                                kT1[:, ni - 8, (j % 4) * TIL:(j % 4 + 1) * TIL],
                                PS_MM[:, psr, ii * TIL:(ii + 1) * TIL]),
                                reads=[("PSMM", psr)], writes=[("k1", j % 4)])
                for ii, j in enumerate(tiles):
                    for nh in range(2):
                        psr = (ii * 2 + nh) % 2
                        for d in range(8):
                            k.op("tensor", lambda e, d=d, ii=ii, nh=nh, psr=psr,
                                 ring=ring: e.matmul(
                                PS_MM[:, psr, :],
                                xhT[:, ring, d, ii * TIL:(ii + 1) * TIL],
                                wv_res[:, d, nh * 512:(nh + 1) * 512],
                                start=(d == 0), stop=(d == 7)),
                                reads=[("xhT", ring), "wv"],
                                writes=[("PSMM", psr)])
                        k.op("vector", lambda e, j=j, nh=nh, psr=psr:
                             e.tensor_copy(
                            v1[:, j % 4, nh * 512:(nh + 1) * 512],
                            PS_MM[:, psr, :]),
                            reads=[("PSMM", psr)], writes=[("v1", j % 4)])
                for ii, j in enumerate(tiles):
                    if j < 17:
                        continue
                    mk = maskcur[:, 0:256] if j == 17 else mask_base[:, 0:256]
                    mkk = "msp" if j == 17 else "mb"
                    prev_in_block = (j - 1) in tiles
                    for h in range(16):
                        p, b = h // 2, (h % 2) * 64
                        for si in range(2):
                            sl = (j - 1) % 4 if si == 0 else j % 4
                            if si == 0 and not prev_in_block:
                                krhs = kTprev[b:b + 64, p, :]
                                krd = "kprev"
                            else:
                                krhs = kT1[b:b + 64, p, sl * TIL:(sl + 1) * TIL]
                                krd = ("k1", sl)
                            k.op("tensor", lambda e, p=p, b=b, si=si,
                                 krhs=krhs, ii=ii: e.matmul(
                                PS_S[:, si * TIL:(si + 1) * TIL],
                                qT_blk[b:b + 64, p, ii * TIL:(ii + 1) * TIL],
                                krhs, start=True, stop=True),
                                reads=["qT", krd],
                                writes=[("PSS", 0)])
                        k.op("vector", lambda e, mk=mk: e.tensor_tensor(
                            Sm[:, 0:256], PS_S[:, 0:256], mk, op=ALU.add),
                            reads=[("PSS", 0), mkk], writes=["Sm"])
                        k.op("vector", lambda e: e.reduce_max(
                            stat2[:, 40:41], Sm[:, 0:256], axis=AX.X,
                            negate=True), reads=["Sm"], writes=["stat2"])
                        k.op("scalar", lambda e: e.activation(
                            Pm[:, 0:256], Sm[:, 0:256], AF.Exp,
                            bias=stat2[:, 40:41], scale=1.0,
                            accum_out=stat[:, 40:41]),
                            reads=["Sm", "stat2"], writes=["Pm", "stat"])
                        k.op("vector", lambda e, h=h: e.reciprocal(
                            stati[:, 40 + h:41 + h], stat[:, 40:41]),
                            reads=["stat"], writes=["stati"])
                        for c in range(2):
                            k.op("tensor", lambda e, c=c: e.transpose(
                                PS_T[:, c * TIL:(c + 1) * TIL],
                                Pm[:, c * TIL:(c + 1) * TIL], identb[:]),
                                reads=["Pm", "identb"], writes=["PST"])
                        k.op("vector", lambda e: e.tensor_copy(
                            PT5[:, 0:2, :], PS_T[:, 0:256]
                            .rearrange("p (a b) -> p a b", b=TIL)),
                            reads=["PST"], writes=["PT5"])
                        bank = h // 8
                        col = (h % 8) * 64
                        for si in range(2):
                            sl = (j - 1) % 4 if si == 0 else j % 4
                            if si == 0 and not prev_in_block:
                                vrhs = vprev[:, h * 64:(h + 1) * 64]
                                vrd = "vprev"
                            else:
                                vrhs = v1[:, sl, h * 64:(h + 1) * 64]
                                vrd = ("v1", sl)
                            k.op("tensor", lambda e, si=si, vrhs=vrhs, h=h,
                                 bank=bank, col=col: e.matmul(
                                PS_O[:, bank, col:col + 64], PT5[:, si, :],
                                vrhs, start=(si == 0), stop=(si == 1)),
                                reads=["PT5", vrd],
                                writes=[("PSO", bank)])
                    for h in range(16):
                        bank, col = h // 8, (h % 8) * 64
                        k.op("scalar", lambda e, h=h, bank=bank, col=col:
                             e.activation(
                            ynorm[:, h * 64:(h + 1) * 64],
                            PS_O[:, bank, col:col + 64], AF.Identity,
                            bias=0.0, scale=stati[:, 40 + h:41 + h]),
                            reads=[("PSO", bank), "stati"], writes=["ynorm"])
                    if _os.environ.get("KX3RAW", "0") == "1":
                        out_done[j] = k.dma("sync", "xst",
                                            outd[j * TIL:(j + 1) * TIL, :],
                                            ynorm[:], reads=["ynorm"],
                                            writes=[("x3", j)])
                        continue
                    ln_stats([(ynorm[:], ["ynorm"])])
                    k.op("vector", lambda e: e.tensor_scalar(
                        junk[:], ynorm[:], statm[:, 0:1], stati[:, 0:1],
                        op0=ALU.subtract, op1=ALU.mult),
                        reads=["ynorm", "statm", "stati"], writes=["junk"])
                    k.op("vector", lambda e, j=j: e.tensor_tensor(
                        XW[:, j % XWN, :], XW[:, j % XWN, :], junk[:],
                        op=ALU.add),
                        reads=["junk", ("XW", j % XWN)],
                        writes=[("XW", j % XWN)])
                    out_done[j] = k.dma("sync", "xst",
                                        outd[j * TIL:(j + 1) * TIL, :],
                                        XW[:, j % XWN, :],
                                        reads=[("XW", j % XWN)],
                                        writes=[("x3", j)])
                jl = tiles[-1]
                k.op("vector", lambda e, jl=jl: e.tensor_copy(
                    vprev[:], v1[:, jl % 4, :]),
                    reads=[("v1", jl % 4)], writes=["vprev"])
                k.op("vector", lambda e, jl=jl: e.tensor_copy(
                    kTprev[:], kT1[:, :, (jl % 4) * TIL:(jl % 4 + 1) * TIL]),
                    reads=[("k1", jl % 4)], writes=["kprev"])
            return out_done

        def emit_l1_ffn(in_done):
            ind = streams["x3"]
            outd = streams["x4"]
            out_done = [None] * NT
            for a in range(8):
                k.dma("gpsimd", "wres%d" % (a % 2), woT_res[:, a, :],
                      w_l1fT_d[a * TIL:(a + 1) * TIL, :], writes=["woT"])
                k.dma("gpsimd", "wres%d" % (a % 2), U_buf[:, a, :].unsqueeze(1).squeeze(1) if False else U_buf[:, a, :],
                      UcT_d[a * TIL:(a + 1) * TIL, :], writes=["Ub"])
                k.dma("gpsimd", "wres%d" % (a % 2), gw_buf[:, a, :],
                      gwT_d[a * TIL:(a + 1) * TIL, :], writes=["gw"])
            k.dma("gpsimd", "cstg", V_buf[:], Vc_d, writes=["Vb"])
            for tiles in chunks4(17, NT):
                for ii, j in enumerate(tiles):
                    k.dma("sync", "xw%d" % (j % 2), XW[:, j % XWN, :],
                          ind[j * TIL:(j + 1) * TIL, :],
                          waits=[in_done[j]], writes=[("XW", j % XWN)])
                ln_stats([(XW[:, j % XWN, :], [("XW", j % XWN)])
                          for j in tiles])
                for ii, j in enumerate(tiles):
                    k.op("vector", lambda e, j=j, ii=ii: e.tensor_scalar(
                        junk[:], XW[:, j % XWN, :], statm[:, ii:ii + 1],
                        stati[:, ii:ii + 1], op0=ALU.subtract, op1=ALU.mult),
                        reads=[("XW", j % XWN), "statm", "stati"],
                        writes=["junk"])
                    k.op("scalar", lambda e: e.activation(
                        o_tok[:], junk[:], AF.Identity, bias=0.0, scale=1.0),
                        reads=["junk"], writes=["otok"])
                    transpose_128(
                        lambda c: o_tok[:, c * TIL:(c + 1) * TIL], 8,
                        lambda h: oT_t[:, 4 * h:4 * h + 4, :],
                        ["otok"], ["oT"])
                    for d in range(8):
                        k.op("tensor", lambda e, d=d: e.matmul(
                            PS_ST[:, 0:8], oT_t[:, d, :], gw_buf[:, d, :],
                            start=(d == 0), stop=(d == 7)),
                            reads=["oT", "gw"], writes=["PSST"])
                    for d in range(8):
                        k.op("tensor", lambda e, d=d: e.matmul(
                            PS_S[:, 0:TIL], oT_t[:, d, :], U_buf[:, d, :],
                            start=(d == 0), stop=(d == 7)),
                            reads=["oT", "Ub"], writes=[("PSS", 0)])
                    k.op("vector", lambda e: e.reduce_max(
                        stat2[:, 40:41], PS_ST[:, 0:8], axis=AX.X,
                        negate=True), reads=["PSST"], writes=["stat2"])
                    k.op("scalar", lambda e: e.activation(
                        junk[:, 0:8], PS_ST[:, 0:8], AF.Exp,
                        bias=stat2[:, 40:41], scale=1.0,
                        accum_out=stat[:, 40:41]),
                        reads=["PSST", "stat2"], writes=["junk", "stat"])
                    k.op("vector", lambda e: e.reciprocal(
                        stati[:, 40:41], stat[:, 40:41]),
                        reads=["stat"], writes=["stati"])
                    k.op("vector", lambda e: e.tensor_scalar(
                        stati[:, 41:42], stati[:, 40:41], 1.0 / 16.0, 0.0,
                        op0=ALU.mult, op1=ALU.add),
                        reads=["stati"], writes=["stati"])
                    k.op("vector", lambda e: e.scalar_tensor_tensor(
                        sT_m[:, 0, :].rearrange("p (a b) -> p a b", b=16),
                        PS_S[:, 0:TIL].rearrange("p (a b) -> p a b", b=16),
                        stati[:, 41:42],
                        junk[:, 0:8].unsqueeze(2).broadcast_to([TIL, 8, 16]),
                        op0=ALU.mult, op1=ALU.mult),
                        reads=[("PSS", 0), "stati", "junk"], writes=["sT"])
                    k.op("tensor", lambda e: e.transpose(
                        PS_T[:, 0:TIL], sT_m[:, 0, :], identb[:]),
                        reads=["sT", "identb"], writes=["PST"])
                    k.op("vector", lambda e: e.tensor_copy(
                        PT5[:, 0, :], PS_T[:, 0:TIL]),
                        reads=["PST"], writes=["PT5"])
                    for nh in range(2):
                        psr = nh
                        for d in range(8):
                            k.op("tensor", lambda e, d=d, nh=nh, psr=psr:
                                 e.matmul(
                                PS_MM[:, psr, :], oT_t[:, d, :],
                                woT_res[:, d, nh * 512:(nh + 1) * 512],
                                start=(d == 0), stop=False),
                                reads=["oT", "woT"], writes=[("PSMM", psr)])
                        k.op("tensor", lambda e, nh=nh, psr=psr: e.matmul(
                            PS_MM[:, psr, :], PT5[:, 0, :],
                            V_buf[:, nh * 512:(nh + 1) * 512],
                            start=False, stop=True),
                            reads=["PT5", "Vb"], writes=[("PSMM", psr)])
                    k.op("scalar", lambda e: e.activation(
                        junk[:, 0:512], PS_MM[:, 0, :], AF.Square, bias=0.0,
                        scale=1.0, accum_out=stat[:, 8:9]),
                        reads=[("PSMM", 0)], writes=["junk", "stat"])
                    k.op("scalar", lambda e: e.activation(
                        junk[:, 512:1024], PS_MM[:, 1, :], AF.Square,
                        bias=0.0, scale=1.0, accum_out=stat[:, 9:10]),
                        reads=[("PSMM", 1)], writes=["junk", "stat"])
                    k.op("vector", lambda e: e.reduce_sum(
                        stat2[:, 8:9].unsqueeze(2), PS_MM[:, 0, :]
                        .unsqueeze(1), axis=AX.X),
                        reads=[("PSMM", 0)], writes=["stat2"])
                    k.op("vector", lambda e: e.reduce_sum(
                        stat2[:, 9:10].unsqueeze(2), PS_MM[:, 1, :]
                        .unsqueeze(1), axis=AX.X),
                        reads=[("PSMM", 1)], writes=["stat2"])
                    k.op("vector", lambda e: e.tensor_tensor(
                        statm[:, 8:9], stat2[:, 8:9], stat2[:, 9:10],
                        op=ALU.add), reads=["stat2"], writes=["statm"])
                    k.op("vector", lambda e: e.tensor_scalar(
                        statm[:, 8:9], statm[:, 8:9], 1.0 / D, 0.0,
                        op0=ALU.mult, op1=ALU.add),
                        reads=["statm"], writes=["statm"])
                    k.op("vector", lambda e: e.tensor_tensor(
                        stat[:, 8:9], stat[:, 8:9], stat[:, 9:10], op=ALU.add),
                        reads=["stat"], writes=["stat"])
                    k.op("vector", lambda e: e.tensor_scalar(
                        stat[:, 8:9], stat[:, 8:9], 1.0 / D, 0.0,
                        op0=ALU.mult, op1=ALU.add),
                        reads=["stat"], writes=["stat"])
                    k.op("vector", lambda e: e.tensor_tensor(
                        stat2[:, 10:11], statm[:, 8:9], statm[:, 8:9],
                        op=ALU.mult), reads=["statm"], writes=["stat2"])
                    k.op("vector", lambda e: e.tensor_tensor(
                        stat[:, 8:9], stat[:, 8:9], stat2[:, 10:11],
                        op=ALU.subtract), reads=["stat", "stat2"],
                        writes=["stat"])
                    k.op("vector", lambda e: e.tensor_scalar(
                        stat[:, 8:9], stat[:, 8:9], EPS, 0.0, op0=ALU.add,
                        op1=ALU.add), reads=["stat"], writes=["stat"])
                    k.op("scalar", lambda e: e.activation(
                        stat2[:, 8:9], stat[:, 8:9], AF.Sqrt, bias=0.0,
                        scale=1.0), reads=["stat"], writes=["stat2"])
                    k.op("vector", lambda e: e.reciprocal(
                        stati[:, 8:9], stat2[:, 8:9]),
                        reads=["stat2"], writes=["stati"])
                    k.op("vector", lambda e: e.tensor_scalar(
                        ynorm[:, 0:512], PS_MM[:, 0, :], statm[:, 8:9],
                        stati[:, 8:9], op0=ALU.subtract, op1=ALU.mult),
                        reads=[("PSMM", 0), "statm", "stati"],
                        writes=["ynorm"])
                    k.op("vector", lambda e: e.tensor_scalar(
                        ynorm[:, 512:1024], PS_MM[:, 1, :], statm[:, 8:9],
                        stati[:, 8:9], op0=ALU.subtract, op1=ALU.mult),
                        reads=[("PSMM", 1), "statm", "stati"],
                        writes=["ynorm"])
                    k.op("vector", lambda e, j=j: e.tensor_tensor(
                        XW[:, j % XWN, :], XW[:, j % XWN, :], ynorm[:],
                        op=ALU.add),
                        reads=["ynorm", ("XW", j % XWN)],
                        writes=[("XW", j % XWN)])
                    out_done[j] = k.dma("sync", "xst",
                                        outd[j * TIL:(j + 1) * TIL, :],
                                        XW[:, j % XWN, :],
                                        reads=[("XW", j % XWN)],
                                        writes=[("x4", j)])
            return out_done

        stream_done = {"x1": t_x1_st}
        final_stream = "x1"
        if n_stages >= 2:
            barrier()
            yd = emit_l0ffn_attn(0, "x1", t_x1_st, kv_lo=1, q_lo=8, do_ln=True)
            stream_done["y"] = yd
            final_stream = "y"
        if n_stages >= 3:
            barrier()
            xl2d = emit_l0ffn_ffn(0, yd, "xl2", q_lo=8)
            stream_done["xl2"] = xl2d
            final_stream = "xl2"
        if n_stages >= 4:
            barrier()
            yd2 = emit_l0ffn_attn(1, "xl2", xl2d, kv_lo=9, q_lo=16, do_ln=False)
            barrier()
            zd = emit_l0ffn_ffn(1, yd2, "z", q_lo=16)
            stream_done["z"] = zd
            barrier()
            x2d = [None] * NT
            for tiles in chunks4(16, NT):
                nt_b = len(tiles)
                for ii, j in enumerate(tiles):
                    k.dma("sync", "xw%d" % (j % 2), XW[:, j % XWN, :],
                          x1_d[j * TIL:(j + 1) * TIL, :],
                          waits=[t_x1_st[j]], writes=[("XW", j % XWN)])
                    k.dma("sync", "hsh%d" % (j % 2), A2W[:, ii, :],
                          z_d[j * TIL:(j + 1) * TIL, :],
                          waits=[zd[j]], writes=[("A2W", ii)])
                ln_stats([(A2W[:, ii, :], [("A2W", ii)])
                          for ii in range(nt_b)])
                for ii, j in enumerate(tiles):
                    k.op("vector", lambda e, ii=ii: e.tensor_scalar(
                        ynorm[:], A2W[:, ii, :], statm[:, ii:ii + 1],
                        stati[:, ii:ii + 1], op0=ALU.subtract, op1=ALU.mult),
                        reads=[("A2W", ii), "statm", "stati"], writes=["ynorm"])
                    k.op("vector", lambda e, j=j: e.tensor_tensor(
                        XW[:, j % XWN, :], XW[:, j % XWN, :], ynorm[:], op=ALU.add),
                        reads=["ynorm", ("XW", j % XWN)], writes=[("XW", j % XWN)])
                    x2d[j] = k.dma("sync", "xst",
                                   x2_d[j * TIL:(j + 1) * TIL, :],
                                   XW[:, j % XWN, :], reads=[("XW", j % XWN)],
                                   writes=[("x2", j)])
            stream_done["x2"] = x2d
            final_stream = "x2"
        if n_stages >= 5:
            barrier()
            x3d = emit_l1_attn(x2d)
            stream_done["x3"] = x3d
            final_stream = "x3"
        if n_stages >= 6:
            barrier()
            x4d = emit_l1_ffn(x3d)
            stream_done["x4"] = x4d
            final_stream = "x4"

        if spy_out is not None:
            k.op("vector", lambda e: e.tensor_copy(
                junk[:], xhT[:, 0, 0:2, 0:512].rearrange("p a b -> p (a b)")),
                reads=[("xhT", 0)], writes=["junk"])
            k.dma("sync", "spy", spy_out[0:TIL, :], junk[:], reads=["junk"])
            k.op("vector", lambda e: e.tensor_copy(
                junk[:], qT_blk[:, 0:2, 0:512].rearrange("p a b -> p (a b)")),
                reads=["qT"], writes=["junk"])
            k.dma("sync", "spy", spy_out[TIL:2 * TIL, :], junk[:], reads=["junk"])
            k.op("vector", lambda e: e.tensor_copy(
                junk[:], kT_blk[:, 0:2, 0:512].rearrange("p a b -> p (a b)")),
                reads=["kT"], writes=["junk"])
            k.dma("sync", "spy", spy_out[2 * TIL:3 * TIL, :], junk[:], reads=["junk"])
            k.op("vector", lambda e: e.tensor_copy(junk[:], v_blk[:, 3, :]),
                reads=["v"], writes=["junk"])
            k.dma("sync", "spy", spy_out[3 * TIL:4 * TIL, :], junk[:], reads=["junk"])
            k.op("vector", lambda e: e.tensor_copy(junk[:], ktok_blk[:, 3, :]),
                reads=["ktok"], writes=["junk"])
            k.dma("sync", "spy", spy_out[4 * TIL:5 * TIL, :], junk[:], reads=["junk"])
            k.op("vector", lambda e: e.tensor_copy(
                junk[:, 0:512], state_sb[:]), reads=["stsb"], writes=["junk"])
            k.op("vector", lambda e: e.tensor_copy(
                junk2s[:, 0:512], sT_m[:].rearrange("p a b -> p (a b)")),
                reads=["sT"], writes=["junk2s"])
            k.dma("sync", "spy", spy_out[6 * TIL:7 * TIL, :], junk2s[:],
                  reads=["junk2s"])
            k.op("vector", lambda e: e.tensor_copy(junk2s[:], o_tok[:]),
                reads=["otok"], writes=["junk2s"])
            k.dma("sync", "spy", spy_out[7 * TIL:8 * TIL, :], junk2s[:],
                  reads=["junk2s"])
            k.op("vector", lambda e: e.tensor_copy(junk2s[:], ynorm[:]),
                reads=["ynorm"], writes=["junk2s"])
            k.dma("sync", "spy", spy_out[8 * TIL:9 * TIL, :], junk2s[:],
                  reads=["junk2s"])
            k.op("vector", lambda e: e.tensor_copy(
                junk[0:1, 512:1024], SMB[0:1, NTOK:NTOK + 512]),
                reads=["SMBq"], writes=["junk"])
            k.dma("sync", "spy", spy_out[5 * TIL:6 * TIL, :], junk[:], reads=["junk"])

        sdram = streams[final_stream]
        built = {j for tt in blocks for j in tt}
        if dbg is not None:
            nm, tl = dbg
            tl = [j for j in tl if j in built]
            for i, j in enumerate(tl):
                k.dma("sync", "out%d" % (i % 2), dbg_out[i * TIL:(i + 1) * TIL, :],
                      streams[nm][j * TIL:(j + 1) * TIL, :], reads=[(nm, j)])
        for i, j in enumerate(range(NT - OWN // TIL, NT)):
            if j not in built:
                continue
            k.dma("sync", "out%d" % (i % 2), xout[i * TIL:(i + 1) * TIL, :],
                  sdram[j * TIL:(j + 1) * TIL, :], reads=[(final_stream, j)])

        for s in k.sems_needed():
            k.semh[s] = E(nc.semaphore(s))
        final_out = [k.last("c_out0"), k.last("c_out1")]
        with nc.Block() as block:
            @block.sync
            def _(sync):
                for fn in k.q["sync"]:
                    fn(sync)
                for t in final_out:
                    if t[1] > 0:
                        sync.wait_ge(k.semh[t[0]], t[1])

            @block.gpsimd
            def _(gpsimd):
                for fn in k.q["gpsimd"]:
                    fn(gpsimd)

            @block.tensor
            def _(tensor):
                for fn in k.q["tensor"]:
                    fn(tensor)

            @block.vector
            def _(vector):
                for fn in k.q["vector"]:
                    fn(vector)

            @block.scalar
            def _(scalar):
                for fn in k.q["scalar"]:
                    fn(scalar)

    return nc


def _rot_tables(t):
    t = t.astype(np.float64)[:, None]
    half = np.arange(0, 64, 2, dtype=np.float64)
    inv = 1.0 / 10000.0 ** (half / 64.0)
    fr = t * inv[None, :]
    fr = np.concatenate([fr, fr], axis=1)
    basev = (half + 0.4 * 64.0) / (1.4 * 64.0)
    power = (t - 2048.0) / 256.0
    sc = basev[None, :] ** power
    sc = np.concatenate([sc, sc], axis=1)
    cos, sin = np.cos(fr), np.sin(fr)
    sgn = np.where(np.arange(64) < 32, -1.0, 1.0)[None, :]
    Aq = 8.0 * cos * sc
    Bq = 8.0 * sgn * sin * sc
    Ak = cos / sc
    Bk = sgn * sin / sc
    return np.concatenate([Aq, Bq, Ak, Bk], axis=1).astype(np.float32)


def _host_prep(L0_attn_x):
    B = L0_attn_x.shape[0]
    base = np.zeros((TIL, 640), np.float32)
    r = np.arange(TIL)[:, None]
    c = np.arange(640)[None, :]
    base[~((c >= r) & (c <= r + 512))] = NEG
    cores = []
    for b in range(B):
        for h in range(2):
            s = h * OWN
            lo = s - HALO
            pad = max(0, -lo)
            xsl = np.zeros((NTOK, D), np.float32)
            xsl[pad:] = np.asarray(L0_attn_x[b, max(0, lo):s + OWN], np.float32)
            frow = np.zeros((1, NTOK), np.float32)
            frow[0, pad:] = 1.0
            fpart = np.zeros((TIL, NT), np.float32)
            for j in range(NT):
                fpart[:, j] = (j * TIL + np.arange(TIL)) >= pad
            msp = np.zeros((4, TIL, 640), np.float32)
            for q in range(4):
                lj = (17 + q) * TIL
                cut = pad - lj + 512
                m = base.copy()
                if cut > 0:
                    m[:, 0:cut] += NEG
                msp[q] = m
            tglob = np.arange(NTOK, dtype=np.float64) + (s - HALO)
            tab = _rot_tables(tglob).reshape(NT, TIL, 256)
            m1b = np.zeros((TIL, 256), np.float32)
            r1 = np.arange(TIL)[:, None]
            c1 = np.arange(256)[None, :]
            allowed = ((c1 >= 64) & (c1 <= 128 + r1) & (r1 < 64)) | \
                      ((c1 >= 128) & (c1 <= 128 + r1) & (r1 >= 64))
            m1b[~allowed] = NEG
            m1s = m1b.copy()
            cut1 = pad - 2048
            if cut1 > 0:
                m1s[:, 0:cut1] += NEG
            cores.append(dict(xin=np.ascontiguousarray(xsl), frow=frow,
                              fpart=fpart, mask_base=base, mask_sp=msp,
                              tab4=np.ascontiguousarray(tab),
                              m1_base=m1b, m1_sp=m1s))
    return cores


def _host_consts():
    ident = np.eye(TIL, dtype=np.float32)
    causalT = np.triu(np.ones((TIL, TIL), np.float32))
    return dict(ident=ident, causalT=causalT)


def _host_weights(i):
    qs = np.asarray(i["L0_ffn_q_scale"], np.float32)
    ks = np.asarray(i["L0_ffn_k_scale"], np.float32)
    qs4 = np.zeros((2, TIL, 4, 64), np.float32)
    for l in range(2):
        qs4[l, :, 0] = np.tile(qs[l], (TIL, 1))
        qs4[l, :, 1] = np.tile(np.roll(qs[l], -32), (TIL, 1))
        qs4[l, :, 2] = np.tile(ks[l], (TIL, 1))
        qs4[l, :, 3] = np.tile(np.roll(ks[l], -32), (TIL, 1))
    b1 = np.asarray(i["L0_ffn_b1"], np.float32)
    b1a = np.zeros((2, TIL, 22), np.float32)
    b1g = np.zeros((2, TIL, 22), np.float32)
    for cc in range(22):
        w = min(TIL, 2730 - cc * TIL)
        b1a[:, 0:w, cc] = b1[:, cc * TIL:cc * TIL + w]
        b1g[:, 0:w, cc] = b1[:, 2730 + cc * TIL:2730 + cc * TIL + w]
    return dict(
        w_l0a_qkvT=np.ascontiguousarray(np.asarray(i["L0_attn_Wqkv_w"], np.float32).T),
        b_l0a_qkv=np.asarray(i["L0_attn_Wqkv_b"], np.float32).reshape(1, -1),
        w_l0a_outT=np.ascontiguousarray(np.asarray(i["L0_attn_out_w"], np.float32).T),
        b_l0a_out=np.asarray(i["L0_attn_out_b"], np.float32).reshape(1, -1),
        w_ffn_qkv=np.ascontiguousarray(np.asarray(i["L0_ffn_wqkv"], np.float32)),
        w_ffn_wo=np.ascontiguousarray(np.asarray(i["L0_ffn_wo"], np.float32)),
        w_ffn_w1=np.ascontiguousarray(np.asarray(i["L0_ffn_w1"], np.float32)),
        w_ffn_w2=np.ascontiguousarray(np.asarray(i["L0_ffn_w2"], np.float32)),
        qs4=qs4,
        gam=np.ascontiguousarray(np.tile(
            np.asarray(i["L0_ffn_ln_g"], np.float32)[:, None, :], (1, TIL, 1))),
        bet=np.ascontiguousarray(np.tile(
            np.asarray(i["L0_ffn_ln_b"], np.float32)[:, None, :], (1, TIL, 1))),
        b1a=b1a, b1g=b1g,
        b2=np.asarray(i["L0_ffn_b2"], np.float32).reshape(2, 1, D),
        w_l1_q=np.ascontiguousarray(np.asarray(i["L1_attn_Wq"], np.float32)),
        w_l1_k=np.ascontiguousarray(np.asarray(i["L1_attn_Wk"], np.float32)),
        w_l1_v=np.ascontiguousarray(np.asarray(i["L1_attn_Wv"], np.float32)),
        w_l1f_T=np.ascontiguousarray(np.asarray(i["L1_ffn_weight"], np.float32).T),
        gate_wT=np.ascontiguousarray(np.asarray(i["L1_ffn_gate_w"], np.float32).T),
        U_catT=np.ascontiguousarray(np.asarray(i["L1_ffn_lora_U"], np.float32)
                                    .transpose(2, 0, 1).reshape(D, TIL)),
        V_cat=np.ascontiguousarray(np.asarray(i["L1_ffn_lora_V"], np.float32)
                                   .transpose(0, 2, 1).reshape(TIL, D)),
    )


def kernel(**inputs):
    x = np.asarray(inputs["L0_attn_x"], np.float32)
    B = x.shape[0]
    cores = _host_prep(x)
    consts = _host_consts()
    wmap = _host_weights(inputs)
    nc = build_nc(n_stages=N_STAGES, dbg=None)
    in_maps = [dict(c, **consts, **wmap) for c in cores]
    res = run_bass_kernel_spmd(nc, in_maps, core_ids=list(range(8)))
    out = np.zeros((B, 2 * OWN, D), np.float32)
    for ci in range(8):
        b, h = ci // 2, ci % 2
        out[b, h * OWN:(h + 1) * OWN] = res.results[ci]["xout"]
    return out


# version 05
# speedup vs baseline: 1.0634x; 1.0634x over previous
import contextlib
import os
import numpy as np
import ml_dtypes
import concourse.bass as bass
import concourse.mybir as mybir
from concourse.bass_utils import run_bass_kernel_spmd

F32 = mybir.dt.float32
BF16 = mybir.dt.bfloat16
AF = mybir.ActivationFunctionType
ALU = mybir.AluOpType
AX = mybir.AxisListType

TIL = 128
HALO = 2176
OWN = 2048
NT = (HALO + OWN) // TIL
NTOK = HALO + OWN
D = 1024
NEG = -1.0e30
EPS = 1e-5

N_STAGES = 6

ENGINES = ("sync", "gpsimd", "tensor", "vector", "scalar")


class K:

    def __init__(self):
        self.q = {e: [] for e in ENGINES}
        self.cnt = {}
        self.waited = {e: {} for e in ENGINES}
        self.semh = {}
        self.last_w = {}
        self.readers = {}
        self.chan_last = {}
        self.esem = {"tensor": "tns", "vector": "vec", "scalar": "scl",
                     "gpsimd": "gps", "sync": "snc"}

    def sems_needed(self):
        return ["tns", "vec", "scl", "gps", "snc"] + sorted(
            s for s in self.cnt if s.startswith("c_"))

    def inc(self, sem, kk=1):
        self.cnt[sem] = self.cnt.get(sem, 0) + kk
        return (sem, self.cnt[sem])

    def last(self, sem):
        return (sem, self.cnt.get(sem, 0))

    def wait(self, eng, tok):
        if tok is None:
            return
        sem, val = tok
        if val <= 0 or self.waited[eng].get(sem, 0) >= val:
            return
        self.waited[eng][sem] = val
        self.q[eng].append(lambda e, s=sem, v=val: e.wait_ge(self.semh[s], v))

    PSUM_KEYS = ("PSMM", "PSS", "PSO", "PSST", "PST")

    def _is_psum(self, key):
        kk = key[0] if isinstance(key, tuple) else key
        return kk in self.PSUM_KEYS

    def _deps(self, reads, writes, waits):
        deps = list(waits)
        for r in reads:
            deps.append(self.last_w.get(r))
            if self._is_psum(r):
                deps.extend(self.readers.get(r, ()))
        for w in writes:
            deps.append(self.last_w.get(w))
            deps.extend(self.readers.get(w, ()))
        return deps

    def _record(self, t, reads, writes):
        for r in reads:
            if self._is_psum(r):
                self.last_w[r] = t
                self.readers[r] = []
            else:
                self.readers.setdefault(r, []).append(t)
        for w in writes:
            self.last_w[w] = t
            self.readers[w] = []

    def op(self, eng, fn, reads=(), writes=(), waits=()):
        for d in self._deps(reads, writes, waits):
            self.wait(eng, d)
        sem = self.esem[eng]
        t = self.inc(sem, 1)
        def wrapped(e, fn=fn, s=sem):
            fn(e).then_inc(self.semh[s], 1)
        self.q[eng].append(wrapped)
        self._record(t, reads, writes)
        return t

    def dma(self, eng, ch, out, in_, reads=(), writes=(), waits=()):
        sem = "c_" + ch
        self.cnt.setdefault(sem, 0)
        deps = self._deps(reads, writes, waits)
        deps.append(self.chan_last.get(ch))
        for d in deps:
            self.wait(eng, d)
        t = self.inc(sem, 16)
        def wrapped(e, o=out, i=in_, s=sem):
            e.dma_start(out=o, in_=i).then_inc(self.semh[s], 16)
        self.q[eng].append(wrapped)
        self.chan_last[ch] = t
        self._record(t, reads, writes)
        return t


def chunks4(lo, hi, w=4):
    out = []
    j = lo
    while j < hi:
        out.append(list(range(j, min(hi, j + w))))
        j += w
    return out


def build_nc(n_stages=N_STAGES, dbg=None):
    nc = bass.Bass("TRN2", target_bir_lowering=False, debug=False)
    k = K()

    def din(name, shape, dt=F32):
        return nc.dram_tensor(name, shape, dt, kind="ExternalInput").ap()

    xin = din("xin", [NTOK, D])
    frow_d = din("frow", [1, NTOK])
    ident_d = din("ident", [TIL, TIL])
    causalT_d = din("causalT", [TIL, TIL])
    fpart_d = din("fpart", [TIL, NT])
    mask_base_d = din("mask_base", [TIL, 640])
    mask_sp_d = din("mask_sp", [4, TIL, 640])
    tab4_d = din("tab4", [NT, TIL, 256])
    qs4_d = din("qs4", [2, TIL, 4, 64])
    gam_d = din("gam", [2, TIL, D])
    bet_d = din("bet", [2, TIL, D])
    b1a_d = din("b1a", [2, TIL, 22])
    b1g_d = din("b1g", [2, TIL, 22])
    b2_d = din("b2", [2, 1, D])
    w_ffn_qkv_d = din("w_ffn_qkv", [2, D, 1536])
    w_ffn_wo_d = din("w_ffn_wo", [2, 512, D])
    w_ffn_w1_d = din("w_ffn_w1", [2, D, 5460])
    w_ffn_w2_d = din("w_ffn_w2", [2, 2730, D])
    w_l1q_d = din("w_l1_q", [D, D])
    w_l1k_d = din("w_l1_k", [D, D])
    w_l1v_d = din("w_l1_v", [D, D])
    w_l1fT_d = din("w_l1f_T", [D, D])
    gwT_d = din("gate_wT", [D, 8])
    UcT_d = din("U_catT", [D, TIL])
    Vc_d = din("V_cat", [TIL, D])
    m1_base_d = din("m1_base", [TIL, 256])
    m1_sp_d = din("m1_sp", [TIL, 256])
    w_qkvT_d = din("w_l0a_qkvT", [D, 3 * D])
    b_qkv_d = din("b_l0a_qkv", [1, 3 * D])
    w_outT_d = din("w_l0a_outT", [D, D])
    b_out_d = din("b_l0a_out", [1, D])

    xout = nc.dram_tensor("xout", [OWN, D], F32, kind="ExternalOutput").ap()
    x1_d = nc.dram_tensor("x1_d", [NTOK, D], F32, kind="Internal").ap()
    w1_st = nc.dram_tensor("w1_st", [2, D, 5460], BF16, kind="Internal").ap()
    w2_st = nc.dram_tensor("w2_st", [2, 2730, D], BF16, kind="Internal").ap()
    y_d = nc.dram_tensor("y_d", [NTOK, D], F32, kind="Internal").ap()
    xl2_d = nc.dram_tensor("xl2_d", [NTOK, D], F32, kind="Internal").ap()
    z_d = nc.dram_tensor("z_d", [NTOK, D], F32, kind="Internal").ap()
    x2_d = nc.dram_tensor("x2_d", [NTOK, D], F32, kind="Internal").ap()
    streams = {"x1": x1_d, "y": y_d, "xl2": xl2_d, "z": z_d, "x2": x2_d,
               "x3": z_d, "x4": x1_d}
    spy_out = None
    if int(os.environ.get("KSPY", "0")):
        spy_out = nc.dram_tensor("spy", [9 * TIL, 1024], F32,
                                 kind="ExternalOutput").ap()
    dbg_out = None
    if dbg is not None:
        dbg_out = nc.dram_tensor("dbgo", [len(dbg[1]) * TIL, D], F32,
                                 kind="ExternalOutput").ap()

    with contextlib.ExitStack() as ctx:
        E = ctx.enter_context

        def sb(name, shape, dt):
            return E(nc.sbuf_tensor(name, shape, dt))

        def ps(name, shape, dt):
            return E(nc.psum_tensor(name, shape, dt))

        XWN = 5
        XW = sb("XW", [TIL, XWN, D], F32)
        SMB = sb("SMB", [TIL, NTOK + 4 * D], BF16)
        frow_b = SMB[0:1, 0:NTOK]
        biasq_b = SMB[0:1, NTOK:NTOK + 3 * D]
        biaso_b = SMB[0:1, NTOK + 3 * D:NTOK + 4 * D]
        identb = sb("identb_s", [TIL, TIL], BF16)
        causalT = sb("causalT_s", [TIL, TIL], F32)

        xhT = sb("xhT", [TIL, 2, 8, 512], BF16)
        wqk_buf = sb("wqk_buf", [TIL, 2, 8, TIL], BF16)
        wv_res = sb("wv_res", [TIL, 8, D], BF16)
        woT_res = sb("woT_res", [TIL, 8, D], BF16)
        qT_blk = sb("qT_blk", [TIL, 8, 512], BF16)
        kT_blk = sb("kT_blk", [TIL, 8, 512], BF16)
        v_blk = sb("v_blk", [TIL, 4, D], BF16)
        ktok_blk = sb("ktok_blk", [TIL, 4, D], BF16)
        sT_m = sb("sT_m", [TIL, 4, TIL], BF16)
        state_acc = sb("state_acc", [TIL, 512], F32)
        state_sb = sb("state_sb", [TIL, 512], BF16)
        o_tok = sb("o_tok", [TIL, D], BF16)
        oT_t = sb("oT_t", [TIL, 8, TIL], BF16)
        stat = sb("stat", [TIL, 64], F32)
        stat2 = sb("stat2", [TIL, 64], F32)
        statm = sb("statm", [TIL, 64], F32)
        stati = sb("stati", [TIL, 64], F32)
        a_tile = sb("a_tile", [TIL, 4, D], BF16)
        junk = sb("junk", [TIL, D], F32)
        cst = sb("cst", [TIL, 4], F32)
        junk2s = sb("junk2s", [TIL, D], F32) if spy_out is not None else None
        ynorm = sb("ynorm", [TIL, D], F32)

        A2W = sb("A2W", [TIL, 4, D], F32)
        PTA = sb("PTA", [TIL, 11264], BF16)
        EXT2 = sb("EXT2", [TIL, 4096], BF16)
        w1a_buf = wqk_buf
        w1g_buf = sb("w1g_buf", [TIL, 2, 8, TIL], BF16)
        w2_buf = sb("w2_buf", [TIL, 2, 512], BF16)
        mask_base = sb("mask_base_s", [TIL, 640], BF16)
        maskcur = sb("maskcur", [TIL, 640], BF16)
        qs4 = sb("qs4_s", [TIL, 4, 64], F32)
        gam_cur = sb("gam_cur", [TIL, D], BF16)
        bet_cur = sb("bet_cur", [TIL, D], BF16)
        b1a_col = sb("b1a_col", [TIL, 22], F32)
        b1g_col = sb("b1g_col", [TIL, 22], F32)
        fpart = sb("fpart_s", [TIL, NT], F32)
        U_buf = wqk_buf[:, 0, :, :]
        V_buf = v_blk[:, 0, :]
        gw_buf = wqk_buf[:, 1, :, 0:8]

        v_roll = v_blk[:].rearrange("p a b -> p (a b)").rearrange(
            "p (a b) -> p a b", b=512)
        kT_roll = ktok_blk[:].rearrange("p a b -> p (a b)").rearrange(
            "p (a b) -> p a b", b=1024)
        qT2 = a_tile[:].rearrange("p a b -> p (a b)")[:, 0:2048].rearrange(
            "p (a b) -> p a b", b=512)
        qfb = a_tile[:].rearrange("p a b -> p (a b)")[:, 2048:4096].rearrange(
            "p (a b) -> p a b", b=512)
        kfb = xhT[:].rearrange("p a b c -> p (a b c)")[:, 0:2048].rearrange(
            "p (a b) -> p a b", b=512)
        _xhT_flat = xhT[:].rearrange("p a b c -> p (a b c)")
        junk2v = _xhT_flat[:, 2048:6144].bitcast(F32).rearrange(
            "p (a b) -> p a b", b=512)
        junk3v = PTA[:, 0:4096].bitcast(F32).rearrange(
            "p (a b) -> p a b", b=512)
        tb_raw = PTA[:, 4096:6144].bitcast(F32).rearrange(
            "p (a b) -> p a b", b=256)
        tb_f = PTA[:, 6144:8192].bitcast(F32).rearrange(
            "p (a b) -> p a b", b=256)
        Sm = PTA[:, 8192:9472].bitcast(F32)
        qraw = qT_blk[:].rearrange("p a b -> p (a b)").bitcast(F32).rearrange(
            "p (a b) -> p a b", b=512)
        kraw = kT_blk[:].rearrange("p a b -> p (a b)").bitcast(F32).rearrange(
            "p (a b) -> p a b", b=512)
        hT_ring = EXT2[:, 0:2064].rearrange("p (a b) -> p a b", b=1032)
        PT5 = EXT2[:, 2064:2704].rearrange("p (a b) -> p a b", b=TIL)
        ob2 = EXT2[:, 2704:3216]
        oT2 = EXT2[:, 3216:3728].rearrange("p (a b) -> p a b", b=TIL)
        pT_g = PTA[:, 0:11264].rearrange("p (a b) -> p a b", b=512)
        hTf = _xhT_flat[:, 0:4096].rearrange("p (a b) -> p a b", b=512)
        wffn_v = wv_res[:].rearrange("p a b -> p (a b)")
        wffn2 = woT_res[:].rearrange("p a b -> p (a b)")
        PS_MM = ps("PS_MM", [TIL, 2, 512], F32)
        PS_S = ps("PS_S", [TIL, 1024], F32)
        PS_O = ps("PS_O", [TIL, 2, 512], F32)
        PS_ST = ps("PS_ST", [TIL, 512], F32)
        PS_T = ps("PS_T", [TIL, 1024], BF16)

        t_c125 = k.op("vector", lambda e: e.memset(cst[:, 0:1], 0.125),
                      writes=["cst"])
        k.op("vector", lambda e: e.memset(hT_ring[:, 0, :], 0.0),
             writes=[("hT", 0)])
        k.op("vector", lambda e: e.memset(hT_ring[:, 1, :], 0.0),
             writes=[("hT", 1)])
        k.dma("sync", "cst", fpart[:], fpart_d, writes=["fpart"])
        if spy_out is not None:
            k.op("vector", lambda e: e.memset(junk2s[:], 0.0), writes=["junk2s"])
        k.dma("sync", "cst", causalT[:], causalT_d, writes=["causalT"])
        k.dma("gpsimd", "cstg", identb[:], ident_d, writes=["identb"])
        k.dma("gpsimd", "cstg", frow_b, frow_d, writes=["SMBf"])
        k.dma("gpsimd", "cstg", biasq_b, b_qkv_d, writes=["SMBq"])
        k.dma("gpsimd", "cstg", biaso_b, b_out_d, writes=["SMBo"])
        for a in range(8):
            k.dma("gpsimd", "wres%d" % (a % 2), wv_res[:, a, :],
                  w_qkvT_d[a * TIL:(a + 1) * TIL, 2 * D:3 * D], writes=["wv"])
            k.dma("gpsimd", "wres%d" % (a % 2), woT_res[:, a, :],
                  w_outT_d[a * TIL:(a + 1) * TIL, :], writes=["woT"])

        for l in range(2 if n_stages >= 3 else 0):
            for a in range(8):
                for half in range(6):
                    wslc = slice(half * 910, (half + 1) * 910)
                    rr = (a * 6 + half) % 2
                    buf = w1g_buf[:, rr, :, :] \
                        .rearrange("p a b -> p (a b)")[:, 0:910]
                    k.dma("gpsimd", "stg%d" % rr, buf,
                          w_ffn_w1_d[l, a * TIL:(a + 1) * TIL, wslc],
                          writes=[("w1g", rr)])
                    k.dma("sync", "stgo%d" % rr,
                          w1_st[l, a * TIL:(a + 1) * TIL, wslc], buf,
                          reads=[("w1g", rr)])
            for a in range(22):
                wdt = 42 if a == 21 else TIL
                buf = w1g_buf[:, a % 2, :, :] \
                    .rearrange("p a b -> p (a b)")[0:wdt, 0:1024]
                k.dma("gpsimd", "stg%d" % (a % 2), buf,
                      w_ffn_w2_d[l, a * TIL:a * TIL + wdt, :],
                      writes=[("w1g", a % 2)])
                k.dma("sync", "stgo%d" % (a % 2),
                      w2_st[l, a * TIL:a * TIL + wdt, :], buf,
                      reads=[("w1g", a % 2)])

        def ln_stats(srcs, n_feat=D):
            n = len(srcs)
            for ii, (ap, rk) in enumerate(srcs):
                k.op("scalar", lambda e, ap=ap, ii=ii: e.activation(
                    junk[:], ap, AF.Square, bias=0.0, scale=1.0,
                    accum_out=stat[:, ii:ii + 1]),
                    reads=list(rk), writes=["junk", "stat"])
                k.op("vector", lambda e, ap=ap, ii=ii: e.reduce_sum(
                    stat2[:, ii:ii + 1].unsqueeze(2), ap.unsqueeze(1), axis=AX.X),
                    reads=list(rk), writes=["stat2"])
            sl = slice(0, n)
            k.op("vector", lambda e, sl=sl: e.tensor_scalar(
                statm[:, sl], stat2[:, sl], 1.0 / n_feat, 0.0,
                op0=ALU.mult, op1=ALU.add), reads=["stat2"], writes=["statm"])
            k.op("vector", lambda e, sl=sl: e.tensor_scalar(
                stat[:, sl], stat[:, sl], 1.0 / n_feat, 0.0,
                op0=ALU.mult, op1=ALU.add), reads=["stat"], writes=["stat"])
            k.op("vector", lambda e, sl=sl: e.tensor_tensor(
                stat2[:, sl], statm[:, sl], statm[:, sl], op=ALU.mult),
                reads=["statm"], writes=["stat2"])
            k.op("vector", lambda e, sl=sl: e.tensor_tensor(
                stat[:, sl], stat[:, sl], stat2[:, sl], op=ALU.subtract),
                reads=["stat", "stat2"], writes=["stat"])
            k.op("vector", lambda e, sl=sl: e.tensor_scalar(
                stat[:, sl], stat[:, sl], EPS, 0.0, op0=ALU.add, op1=ALU.add),
                reads=["stat"], writes=["stat"])
            k.op("scalar", lambda e, sl=sl: e.activation(
                stat2[:, sl], stat[:, sl], AF.Sqrt, bias=0.0, scale=1.0),
                reads=["stat"], writes=["stat2"])
            return k.op("vector", lambda e, sl=sl: e.reciprocal(
                stati[:, sl], stat2[:, sl]), reads=["stat2"], writes=["stati"])

        def transpose_128(src_ap_fn, nchunk, dst_fn, src_reads, dst_writes):
            for c in range(nchunk):
                k.op("tensor", lambda e, c=c: e.transpose(
                    PS_T[:, c * TIL:(c + 1) * TIL], src_ap_fn(c), identb[:]),
                    reads=list(src_reads) + ["identb"], writes=["PST"])
            t = None
            nh = (nchunk + 3) // 4
            for h in range(nh):
                c0, c1 = 4 * h, min(4 * h + 4, nchunk)
                t = k.op("vector", lambda e, h=h, c0=c0, c1=c1: e.tensor_copy(
                    dst_fn(h), PS_T[:, c0 * TIL:c1 * TIL]
                    .rearrange("p (a b) -> p a b", b=TIL)),
                    reads=["PST"], writes=list(dst_writes))
            return t

        _os = os
        _nb = int(_os.environ.get("KBLOCKS", "99"))
        _ph = int(_os.environ.get("KPHASE", "999"))
        blocks = chunks4(0, NT)[:_nb]
        t_x1_st = [None] * NT
        k.op("vector", lambda e: e.memset(state_acc[:], 0.0), writes=["stacc"])
        k.op("vector", lambda e: e.memset(state_sb[:], 0.0), writes=["stsb"])

        for bi, tiles in enumerate(blocks):
            nt_b = len(tiles)
            bcols = nt_b * TIL
            ring = bi % 2
            lo = tiles[0] * TIL

            for j in tiles:
                k.dma("sync", "xw%d" % (j % 2), XW[:, j % XWN, :],
                      xin[j * TIL:(j + 1) * TIL, :], writes=[("XW", j % XWN)])

            ln_stats([(XW[:, j % XWN, :], [("XW", j % XWN)]) for j in tiles])
            for ii, j in enumerate(tiles):
                k.op("vector", lambda e, j=j, ii=ii: e.tensor_scalar(
                    a_tile[:, ii, :], XW[:, j % XWN, :], statm[:, ii:ii + 1],
                    stati[:, ii:ii + 1], op0=ALU.subtract, op1=ALU.mult),
                    reads=[("XW", j % XWN), "statm", "stati"], writes=[("a", ii)])
                transpose_128(
                    lambda c, ii=ii: a_tile[:, ii, c * TIL:(c + 1) * TIL], 8,
                    lambda h, ii=ii, ring=ring: xhT[:, ring, 4 * h:4 * h + 4,
                                                    ii * TIL:(ii + 1) * TIL],
                    [("a", ii)], [("xhT", ring)])

            for ni in range(16 if _ph >= 2 else 0):
                wring = ni % 2
                k.dma("gpsimd", "wqk%d" % wring, wqk_buf[:, wring, :, :],
                      w_qkvT_d[:, ni * TIL:(ni + 1) * TIL]
                      .rearrange("(a p) b -> p a b", p=TIL),
                      writes=[("wqk", wring)])
                psr = ni % 2
                for d in range(8):
                    k.op("tensor", lambda e, d=d, wring=wring, psr=psr,
                         bcols=bcols, ring=ring: e.matmul(
                        PS_MM[:, psr, 0:bcols], wqk_buf[:, wring, d, :],
                        xhT[:, ring, d, 0:bcols], start=(d == 0), stop=False),
                        reads=[("wqk", wring), ("xhT", ring)],
                        writes=[("PSMM", psr)])
                k.op("tensor", lambda e, ni=ni, psr=psr, bcols=bcols, lo=lo:
                     e.matmul(
                    PS_MM[:, psr, 0:bcols], biasq_b[:, ni * TIL:(ni + 1) * TIL],
                    frow_b[:, lo:lo + bcols], start=False, stop=True),
                    reads=["SMBq", "SMBf"], writes=[("PSMM", psr)])
                if ni < 8:
                    k.op("scalar", lambda e, ni=ni, psr=psr, bcols=bcols:
                         e.activation(
                        qT_blk[:, ni, 0:bcols], PS_MM[:, psr, 0:bcols],
                        AF.Identity, bias=0.0, scale=cst[:, 0:1]),
                        reads=[("PSMM", psr), "cst"], writes=["qT"])
                else:
                    k.op("vector", lambda e, ni=ni, psr=psr, bcols=bcols:
                         e.tensor_copy(
                        kT_blk[:, ni - 8, 0:bcols], PS_MM[:, psr, 0:bcols]),
                        reads=[("PSMM", psr)], writes=["kT"])

            for ii, j in enumerate(tiles if _ph >= 3 else []):
                for nh in range(2):
                    psr = (ii * 2 + nh) % 2
                    for d in range(8):
                        k.op("tensor", lambda e, d=d, ii=ii, nh=nh, psr=psr,
                             ring=ring: e.matmul(
                            PS_MM[:, psr, :],
                            xhT[:, ring, d, ii * TIL:(ii + 1) * TIL],
                            wv_res[:, d, nh * 512:(nh + 1) * 512],
                            start=(d == 0), stop=False),
                            reads=[("xhT", ring), "wv"], writes=[("PSMM", psr)])
                    k.op("tensor", lambda e, nh=nh, psr=psr, j=j: e.matmul(
                        PS_MM[:, psr, :], frow_b[:, j * TIL:(j + 1) * TIL],
                        biasq_b[:, 2 * D + nh * 512:2 * D + (nh + 1) * 512],
                        start=False, stop=True),
                        reads=["SMBq", "SMBf"], writes=[("PSMM", psr)])
                    k.op("vector", lambda e, ii=ii, nh=nh, psr=psr: e.tensor_copy(
                        v_blk[:, ii, nh * 512:(nh + 1) * 512], PS_MM[:, psr, :]),
                        reads=[("PSMM", psr)], writes=["v"])

            for ii, j in enumerate(tiles if _ph >= 4 else []):
                transpose_128(
                    lambda c, ii=ii: kT_blk[:, c, ii * TIL:(ii + 1) * TIL], 8,
                    lambda h, ii=ii: ktok_blk[:, ii, 4 * h * TIL:(4 * h + 4) * TIL]
                    .rearrange("p (a b) -> p a b", b=TIL),
                    ["kT"], ["ktok"])

            for ii, j in enumerate(tiles if _ph >= 5 else []):
                for g in range(4):
                    psr = g % 2
                    for hh in range(4):
                        h = g * 4 + hh
                        p, b = h // 2, (h % 2) * 64
                        k.op("tensor", lambda e, p=p, b=b, hh=hh, psr=psr,
                             ii=ii: e.matmul(
                            PS_S[:, psr * 512 + hh * TIL:psr * 512 + (hh + 1) * TIL],
                            kT_blk[b:b + 64, p, ii * TIL:(ii + 1) * TIL],
                            qT_blk[b:b + 64, p, ii * TIL:(ii + 1) * TIL],
                            start=True, stop=True),
                            reads=["kT", "qT"], writes=[("PSS", psr)])
                    k.op("vector", lambda e, psr=psr: e.tensor_tensor(
                        sT_m[:], PS_S[:, psr * 512:psr * 512 + 512]
                        .rearrange("p (a b) -> p a b", b=TIL),
                        causalT[:].unsqueeze(1).broadcast_to([TIL, 4, TIL]),
                        op=ALU.mult),
                        reads=[("PSS", psr), "causalT"], writes=["sT"])
                    for hh in range(4):
                        h = g * 4 + hh
                        p, b = h // 2, (h % 2) * 64
                        bank, col = h // 8, (h % 8) * 64
                        k.op("tensor", lambda e, h=h, hh=hh, bank=bank, col=col,
                             ii=ii: e.matmul(
                            PS_O[:, bank, col:col + 64], sT_m[:, hh, :],
                            v_blk[:, ii, h * 64:(h + 1) * 64],
                            start=True, stop=False),
                            reads=["sT", "v", "stsb"], writes=[("PSO", bank)])
                        k.op("tensor", lambda e, h=h, p=p, b=b, bank=bank,
                             col=col, ii=ii: e.matmul(
                            PS_O[:, bank, col:col + 64],
                            qT_blk[b:b + 64, p, ii * TIL:(ii + 1) * TIL],
                            state_sb[b:b + 64, p * 64:(p + 1) * 64],
                            start=False, stop=True),
                            reads=["qT", "stsb"], writes=[("PSO", bank)])

                for h in range(16):
                    p, b = h // 2, (h % 2) * 64
                    k.op("tensor", lambda e, h=h, p=p, b=b, ii=ii: e.matmul(
                        PS_ST[b:b + 64, p * 64:(p + 1) * 64],
                        ktok_blk[:, ii, h * 64:(h + 1) * 64],
                        v_blk[:, ii, h * 64:(h + 1) * 64],
                        start=True, stop=True, tile_position=(0, b)),
                        reads=["ktok", "v"], writes=["PSST"])
                k.op("vector", lambda e: e.tensor_tensor(
                    state_acc[:], state_acc[:], PS_ST[:], op=ALU.add),
                    reads=["PSST", "stacc"], writes=["stacc"])
                k.op("vector", lambda e: e.tensor_copy(state_sb[:], state_acc[:]),
                    reads=["stacc"], writes=["stsb"])

                for bank in range(2):
                    k.op("scalar", lambda e, bank=bank: e.activation(
                        o_tok[:, bank * 512:(bank + 1) * 512], PS_O[:, bank, :],
                        AF.Identity, bias=0.0, scale=1.0),
                        reads=[("PSO", bank)], writes=["otok"])
                transpose_128(
                    lambda c: o_tok[:, c * TIL:(c + 1) * TIL], 8,
                    lambda h: oT_t[:, 4 * h:4 * h + 4, :], ["otok"], ["oT"])

                for nh in range(2 if _ph >= 6 else 0):
                    psr = nh
                    for c in range(8):
                        k.op("tensor", lambda e, c=c, nh=nh, psr=psr: e.matmul(
                            PS_MM[:, psr, :], oT_t[:, c, :],
                            woT_res[:, c, nh * 512:(nh + 1) * 512],
                            start=(c == 0), stop=False),
                            reads=["oT", "woT"], writes=[("PSMM", psr)])
                    k.op("tensor", lambda e, nh=nh, psr=psr, j=j: e.matmul(
                        PS_MM[:, psr, :], frow_b[:, j * TIL:(j + 1) * TIL],
                        biaso_b[:, nh * 512:(nh + 1) * 512],
                        start=False, stop=True),
                        reads=["SMBf", "SMBo"], writes=[("PSMM", psr)])

                if _ph < 7:
                    continue
                k.op("scalar", lambda e: e.activation(
                    junk[:, 0:512], PS_MM[:, 0, :], AF.Square, bias=0.0,
                    scale=1.0, accum_out=stat[:, 8:9]),
                    reads=[("PSMM", 0)], writes=["junk", "stat"])
                k.op("scalar", lambda e: e.activation(
                    junk[:, 512:1024], PS_MM[:, 1, :], AF.Square, bias=0.0,
                    scale=1.0, accum_out=stat[:, 9:10]),
                    reads=[("PSMM", 1)], writes=["junk", "stat"])
                if _ph < 8:
                    continue
                k.op("vector", lambda e: e.reduce_sum(
                    stat2[:, 8:9].unsqueeze(2), PS_MM[:, 0, :].unsqueeze(1),
                    axis=AX.X), reads=[("PSMM", 0)], writes=["stat2"])
                k.op("vector", lambda e: e.reduce_sum(
                    stat2[:, 9:10].unsqueeze(2), PS_MM[:, 1, :].unsqueeze(1),
                    axis=AX.X), reads=[("PSMM", 1)], writes=["stat2"])
                if _ph < 82:
                    continue
                k.op("vector", lambda e: e.tensor_tensor(
                    statm[:, 8:9], stat2[:, 8:9], stat2[:, 9:10], op=ALU.add),
                    reads=["stat2"], writes=["statm"])
                k.op("vector", lambda e: e.tensor_scalar(
                    statm[:, 8:9], statm[:, 8:9], 1.0 / D, 0.0,
                    op0=ALU.mult, op1=ALU.add), reads=["statm"], writes=["statm"])
                k.op("vector", lambda e: e.tensor_tensor(
                    stat[:, 8:9], stat[:, 8:9], stat[:, 9:10], op=ALU.add),
                    reads=["stat"], writes=["stat"])
                k.op("vector", lambda e: e.tensor_scalar(
                    stat[:, 8:9], stat[:, 8:9], 1.0 / D, 0.0,
                    op0=ALU.mult, op1=ALU.add), reads=["stat"], writes=["stat"])
                k.op("vector", lambda e: e.tensor_tensor(
                    stat2[:, 10:11], statm[:, 8:9], statm[:, 8:9], op=ALU.mult),
                    reads=["statm"], writes=["stat2"])
                k.op("vector", lambda e: e.tensor_tensor(
                    stat[:, 8:9], stat[:, 8:9], stat2[:, 10:11],
                    op=ALU.subtract), reads=["stat", "stat2"], writes=["stat"])
                k.op("vector", lambda e: e.tensor_scalar(
                    stat[:, 8:9], stat[:, 8:9], EPS, 0.0, op0=ALU.add,
                    op1=ALU.add), reads=["stat"], writes=["stat"])
                if _ph < 84:
                    continue
                k.op("scalar", lambda e: e.activation(
                    stat2[:, 8:9], stat[:, 8:9], AF.Sqrt, bias=0.0, scale=1.0),
                    reads=["stat"], writes=["stat2"])
                k.op("vector", lambda e: e.reciprocal(
                    stati[:, 8:9], stat2[:, 8:9]), reads=["stat2"],
                    writes=["stati"])
                if _ph < 90:
                    continue
                k.op("vector", lambda e: e.tensor_scalar(
                    ynorm[:, 0:512], PS_MM[:, 0, :], statm[:, 8:9],
                    stati[:, 8:9], op0=ALU.subtract, op1=ALU.mult),
                    reads=[("PSMM", 0), "statm", "stati"], writes=["ynorm"])
                k.op("vector", lambda e: e.tensor_scalar(
                    ynorm[:, 512:1024], PS_MM[:, 1, :], statm[:, 8:9],
                    stati[:, 8:9], op0=ALU.subtract, op1=ALU.mult),
                    reads=[("PSMM", 1), "statm", "stati"], writes=["ynorm"])
                k.op("vector", lambda e, j=j: e.tensor_tensor(
                    XW[:, j % XWN, :], XW[:, j % XWN, :], ynorm[:], op=ALU.add),
                    reads=["ynorm", ("XW", j % XWN)], writes=[("XW", j % XWN)])
                if _ph < 100:
                    continue
                t_x1_st[j] = k.dma("gpsimd", "xst", x1_d[j * TIL:(j + 1) * TIL, :],
                                   XW[:, j % XWN, :], reads=[("XW", j % XWN)],
                                   writes=[("x1", j)])

        def barrier():
            toks = [k.last(s) for s in list(k.cnt)]
            for e in ENGINES:
                for t in toks:
                    k.wait(e, t)

        def wffn_rhs(d, part):
            qq = d * 3 + part
            if qq < 16:
                return wffn_v[:, qq * 512:(qq + 1) * 512]
            return wffn2[:, (qq - 16) * 512:(qq - 16 + 1) * 512]

        def wffn_keys():
            return ["wv", "woT"]

        def wo_rhs(c, nh):
            off = 4096 + c * 1024 + nh * 512
            return wffn2[:, off:off + 512]

        Pm = PTA[:, 9472:10112]

        def prep_tile(ind, in_done, j, do_ln):
            if do_ln:
                k.dma("sync", "xw%d" % (j % 2), XW[:, j % XWN, :],
                      ind[j * TIL:(j + 1) * TIL, :],
                      waits=[in_done[j]] if in_done else (),
                      writes=[("XW", j % XWN)])
                ln_stats([(XW[:, j % XWN, :], [("XW", j % XWN)])])
                k.op("vector", lambda e, j=j: e.tensor_scalar(
                    A2W[:, j % 4, :], XW[:, j % XWN, :], statm[:, 0:1],
                    stati[:, 0:1], op0=ALU.subtract, op1=ALU.mult),
                    reads=[("XW", j % XWN), "statm", "stati"],
                    writes=[("A2W", j % 4)])
            else:
                k.dma("sync", "xw%d" % (j % 2), A2W[:, j % 4, :],
                      ind[j * TIL:(j + 1) * TIL, :],
                      waits=[in_done[j]] if in_done else (),
                      writes=[("A2W", j % 4)])
            k.op("scalar", lambda e, j=j: e.activation(
                o_tok[:], A2W[:, j % 4, :], AF.Identity, bias=0.0, scale=1.0),
                reads=[("A2W", j % 4)], writes=["otok"])
            for c in range(8):
                k.op("tensor", lambda e, c=c: e.transpose(
                    PS_T[:, c * TIL:(c + 1) * TIL],
                    o_tok[:, c * TIL:(c + 1) * TIL], identb[:]),
                    reads=["otok", "identb"], writes=["PST"])
            hr = hT_ring[:, j % 2, :].rearrange("p (a b) -> p a b", b=129)
            hprev = hT_ring[:, (j + 1) % 2, :].rearrange("p (a b) -> p a b", b=129)
            for h in range(2):
                k.op("vector", lambda e, h=h, hr=hr: e.tensor_copy(
                    hr[:, 4 * h:4 * h + 4, 1:129],
                    PS_T[:, 4 * h * TIL:(4 * h + 4) * TIL]
                    .rearrange("p (a b) -> p a b", b=TIL)),
                    reads=["PST"], writes=[("hT", j % 2)])
            k.op("vector", lambda e, hr=hr, hprev=hprev: e.tensor_copy(
                hr[:, :, 0:1], hprev[:, :, 128:129]),
                reads=[("hT", (j + 1) % 2)], writes=[("hT", j % 2)])

        def attn_tile(l, j, ii, out_done):
            key_tiles = list(range(j - 4, j + 1))
            segs = []
            poff = 0
            run_start = key_tiles[0] % 8
            run_len = 1
            for t in key_tiles[1:]:
                if t % 8 == (run_start + run_len) % 8 and run_start + run_len < 8:
                    run_len += 1
                else:
                    segs.append((poff, run_start * TIL, run_len * TIL))
                    poff += run_len * TIL
                    run_start = t % 8
                    run_len = 1
            segs.append((poff, run_start * TIL, run_len * TIL))
            sub = []
            for (po, ro, w) in segs:
                while w > 0:
                    lim = 512 - (po % 512)
                    ww = min(w, lim, 512)
                    sub.append((po, ro, ww))
                    po, ro, w = po + ww, ro + ww, w - ww
            if 17 <= j <= 20:
                k.dma("gpsimd", "cstg", maskcur[:],
                      mask_sp_d[j - 17], writes=["msp"])
            mk = maskcur[:] if 17 <= j <= 20 else mask_base[:]
            mkk = "msp" if 17 <= j <= 20 else "mb"
            PSMMf = PS_MM[:].rearrange("p a b -> p (a b)")
            slots = [PS_S, PSMMf]
            skeys = [[("PSS", 0), ("PSS", 1)], [("PSMM", 0), ("PSMM", 1)]]

            def s_mms(h):
                sl_ = slots[h % 2]
                for (po, ro, w) in sub:
                    p, bb = h // 2, (h % 2 == 1) * 0 + (h % 2) * 0
                    p = h // 2
                    bb = (h % 2) * 64
                    k.op("tensor", lambda e, p=p, bb=bb, po=po, ro=ro, w=w,
                         ii=ii, sl_=sl_: e.matmul(
                        sl_[:, po:po + w],
                        qT2[bb:bb + 64, ii, p * TIL:(p + 1) * TIL],
                        kT_roll[bb:bb + 64, p, ro:ro + w],
                        start=True, stop=True),
                        reads=["qT2"] + [("ktr", s % 8) for s in key_tiles],
                        writes=skeys[h % 2])

            def v_chain(h):
                sl_ = slots[h % 2]
                k.op("vector", lambda e, mk=mk, sl_=sl_: e.tensor_tensor(
                    Sm[:], sl_[:, 0:640], mk, op=ALU.add),
                    reads=skeys[h % 2] + [mkk], writes=["Sm"])
                k.op("vector", lambda e: e.reduce_max(
                    stat2[:, 40:41], Sm[:], axis=AX.X, negate=True),
                    reads=["Sm"], writes=["stat2"])
                k.op("scalar", lambda e: e.activation(
                    Pm[:], Sm[:], AF.Exp, bias=stat2[:, 40:41], scale=1.0,
                    accum_out=stat[:, 40:41]),
                    reads=["Sm", "stat2"], writes=["Pm", "stat"])
                k.op("vector", lambda e, h=h: e.reciprocal(
                    stati[:, 40 + h:41 + h], stat[:, 40:41]),
                    reads=["stat"], writes=["stati"])

            def pv_chain(h):
                for c in range(5):
                    k.op("tensor", lambda e, c=c: e.transpose(
                        PS_T[:, c * TIL:(c + 1) * TIL],
                        Pm[:, c * TIL:(c + 1) * TIL], identb[:]),
                        reads=["Pm", "identb"], writes=["PST"])
                k.op("vector", lambda e: e.tensor_copy(
                    PT5[:], PS_T[:, 0:640].rearrange("p (a b) -> p a b", b=TIL)),
                    reads=["PST"], writes=["PT5"])
                for t5, jt in enumerate(key_tiles):
                    k.op("tensor", lambda e, t5=t5, jt=jt, h=h, j=j: e.matmul(
                        PS_O[:, j % 2, h * 64:(h + 1) * 64], PT5[:, t5, :],
                        v_roll[:, jt % 8, h * 64:(h + 1) * 64],
                        start=(t5 == 0), stop=(t5 == 4)),
                        reads=["PT5", ("vr", jt % 8)], writes=[("PSO", j % 2)])

            for h in range(9):
                if h < 8:
                    s_mms(h)
                if h >= 1:
                    v_chain(h - 1)
                    pv_chain(h - 1)
            for h in range(8):
                k.op("scalar", lambda e, h=h, j=j: e.activation(
                    ob2[:, h * 64:(h + 1) * 64],
                    PS_O[:, j % 2, h * 64:(h + 1) * 64],
                    AF.Identity, bias=0.0, scale=stati[:, 40 + h:41 + h]),
                    reads=[("PSO", j % 2), "stati"], writes=["ob2"])
            for c in range(4):
                k.op("tensor", lambda e, c=c: e.transpose(
                    PS_T[:, c * TIL:(c + 1) * TIL],
                    ob2[:, c * TIL:(c + 1) * TIL], identb[:]),
                    reads=["ob2", "identb"], writes=["PST"])
            k.op("vector", lambda e: e.tensor_copy(
                oT2[:], PS_T[:, 0:512].rearrange("p (a b) -> p a b", b=TIL)),
                reads=["PST"], writes=["oT2"])
            for nh in range(2):
                for c in range(4):
                    k.op("tensor", lambda e, c=c, nh=nh: e.matmul(
                        PS_MM[:, nh, :], oT2[:, c, :], wo_rhs(c, nh),
                        start=(c == 0), stop=(c == 3)),
                        reads=["oT2", "woT"], writes=[("PSMM", nh)])
                k.op("vector", lambda e, nh=nh, j=j: e.scalar_tensor_tensor(
                    ynorm[:, nh * 512:(nh + 1) * 512], PS_MM[:, nh, :],
                    fpart[:, j:j + 1], A2W[:, j % 4, nh * 512:(nh + 1) * 512],
                    op0=ALU.mult, op1=ALU.add),
                    reads=[("PSMM", nh), "fpart", ("A2W", j % 4)],
                    writes=["ynorm"])
            out_done[j] = k.dma("gpsimd", "xst", y_d[j * TIL:(j + 1) * TIL, :],
                                ynorm[:], reads=["ynorm"], writes=[("y", j)])

        def emit_l0ffn_attn(l, in_stream, in_done, kv_lo, q_lo, do_ln):
            ind = streams[in_stream]
            y_done = [None] * NT
            for qq in range(24):
                dst = wffn_rhs(qq // 3, qq % 3)
                d, part = qq // 3, qq % 3
                k.dma("gpsimd", "wres%d" % (qq % 2), dst,
                      w_ffn_qkv_d[l, d * TIL:(d + 1) * TIL,
                                  part * 512:(part + 1) * 512],
                      writes=wffn_keys())
            for c in range(4):
                for nh in range(2):
                    k.dma("gpsimd", "wres%d" % ((c * 2 + nh) % 2),
                          wo_rhs(c, nh),
                          w_ffn_wo_d[l, c * TIL:(c + 1) * TIL,
                                     nh * 512:(nh + 1) * 512], writes=["woT"])
            k.dma("sync", "cst", qs4[:], qs4_d[l], writes=["qs4"])
            k.dma("gpsimd", "cstg", mask_base[:], mask_base_d, writes=["mb"])


            prep_tile(ind, in_done, kv_lo - 1, do_ln)
            for tiles in chunks4(kv_lo, NT):
                nt_b = len(tiles)
                i0t = tiles[0]
                k.dma("sync", "tab", tb_raw[:, 0:nt_b, :],
                      tab4_d[i0t:i0t + nt_b].rearrange("a p b -> p a b"),
                      writes=["tbr"])
                for t4 in range(4):
                    rollix = t4 % 2
                    qk = 0 if t4 < 2 else 2
                    k.op("vector", lambda e, t4=t4, rollix=rollix, qk=qk,
                         nt_b=nt_b: e.tensor_tensor(
                        tb_f[:, 0:nt_b, t4 * 64:(t4 + 1) * 64],
                        tb_raw[:, 0:nt_b, t4 * 64:(t4 + 1) * 64],
                        qs4[:, qk + rollix, :].unsqueeze(1)
                        .broadcast_to([TIL, nt_b, 64]), op=ALU.mult),
                        reads=["tbr", "qs4"], writes=["tbf"])
                for ii, j in enumerate(tiles):
                    prep_tile(ind, in_done, j, do_ln)
                    hr = hT_ring[:, j % 2, :].rearrange("p (a b) -> p a b", b=129)
                    for part in range(3):
                        psr = part % 2
                        for d in range(8):
                            sh = 1 if d < 4 else 0
                            k.op("tensor", lambda e, d=d, part=part, psr=psr,
                                 sh=sh, hr=hr: e.matmul(
                                PS_MM[:, psr, :], hr[:, d, sh:sh + TIL],
                                wffn_rhs(d, part), start=(d == 0),
                                stop=(d == 7)),
                                reads=[("hT", j % 2)] + wffn_keys(),
                                writes=[("PSMM", psr)])
                        if part == 0:
                            k.op("vector", lambda e, ii=ii, psr=psr:
                                 e.tensor_copy(qraw[:, ii, :], PS_MM[:, psr, :]),
                                 reads=[("PSMM", psr)], writes=["qraw"])
                        elif part == 1:
                            k.op("vector", lambda e, ii=ii, psr=psr:
                                 e.tensor_copy(kraw[:, ii, :], PS_MM[:, psr, :]),
                                 reads=[("PSMM", psr)], writes=["kraw"])
                        else:
                            k.op("vector", lambda e, j=j, psr=psr:
                                 e.tensor_copy(v_roll[:, j % 8, :],
                                               PS_MM[:, psr, :]),
                                 reads=[("PSMM", psr)], writes=[("vr", j % 8)])

                for side in range(2):
                    raw = qraw if side == 0 else kraw
                    rk = "qraw" if side == 0 else "kraw"
                    fb = qfb if side == 0 else kfb
                    fbk = "qfb" if side == 0 else "kfb"
                    toff = 0 if side == 0 else 128
                    sl2 = slice(0, nt_b * 8)
                    k.op("scalar", lambda e, raw=raw, nt_b=nt_b: e.activation(
                        junk2v[:, 0:nt_b, :], raw[:, 0:nt_b, :], AF.Square,
                        bias=0.0, scale=1.0), reads=[rk], writes=["junk2"])
                    k.op("vector", lambda e, nt_b=nt_b, sl2=sl2: e.reduce_sum(
                        stat[:, sl2].rearrange("p (a b) -> p a b", b=8)
                        .unsqueeze(3),
                        junk2v[:, 0:nt_b, :].rearrange("p a (b c) -> p a b c",
                                                       c=64), axis=AX.X),
                        reads=["junk2"], writes=["stat"])
                    k.op("scalar", lambda e, sl2=sl2: e.activation(
                        stat2[:, sl2], stat[:, sl2], AF.Sqrt, bias=0.0,
                        scale=1.0), reads=["stat"], writes=["stat2"])
                    k.op("vector", lambda e, sl2=sl2: e.tensor_scalar(
                        stat2[:, sl2], stat2[:, sl2], 1e-12, 0.0,
                        op0=ALU.max, op1=ALU.add),
                        reads=["stat2"], writes=["stat2"])
                    k.op("vector", lambda e, sl2=sl2: e.reciprocal(
                        stati[:, sl2], stat2[:, sl2]),
                        reads=["stat2"], writes=["stati"])
                    k.op("vector", lambda e, nt_b=nt_b: e.tensor_copy(
                        junk3v[:, 0:nt_b, :].rearrange("p a (b c) -> p a b c",
                                                       c=64),
                        stati[:, 0:nt_b * 8].rearrange("p (a b) -> p a b", b=8)
                        .unsqueeze(3).broadcast_to([TIL, nt_b, 8, 64])),
                        reads=["stati"], writes=["junk3"])
                    k.op("vector", lambda e, raw=raw, nt_b=nt_b: e.tensor_tensor(
                        raw[:, 0:nt_b, :], raw[:, 0:nt_b, :],
                        junk3v[:, 0:nt_b, :], op=ALU.mult),
                        reads=[rk, "junk3"], writes=[rk])
                    vw = lambda a: a.rearrange("p t (h d) -> p t h d", d=64)
                    k.op("vector", lambda e, raw=raw, nt_b=nt_b, toff=toff,
                         vw=vw: e.tensor_tensor(
                        vw(junk2v[:, 0:nt_b, :])[:, :, :, 0:32],
                        vw(raw[:, 0:nt_b, :])[:, :, :, 32:64],
                        tb_f[:, 0:nt_b, toff + 64:toff + 96].unsqueeze(2)
                        .broadcast_to([TIL, nt_b, 8, 32]), op=ALU.mult),
                        reads=[rk, "tbf"], writes=["junk2"])
                    k.op("vector", lambda e, raw=raw, nt_b=nt_b, toff=toff,
                         vw=vw: e.tensor_tensor(
                        vw(junk2v[:, 0:nt_b, :])[:, :, :, 32:64],
                        vw(raw[:, 0:nt_b, :])[:, :, :, 0:32],
                        tb_f[:, 0:nt_b, toff + 96:toff + 128].unsqueeze(2)
                        .broadcast_to([TIL, nt_b, 8, 32]), op=ALU.mult),
                        reads=[rk, "tbf"], writes=["junk2"])
                    k.op("vector", lambda e, raw=raw, nt_b=nt_b, toff=toff,
                         vw=vw: e.tensor_tensor(
                        vw(junk3v[:, 0:nt_b, :]), vw(raw[:, 0:nt_b, :]),
                        tb_f[:, 0:nt_b, toff + 0:toff + 64].unsqueeze(2)
                        .broadcast_to([TIL, nt_b, 8, 64]), op=ALU.mult),
                        reads=[rk, "tbf"], writes=["junk3"])
                    k.op("vector", lambda e, fb=fb, nt_b=nt_b: e.tensor_tensor(
                        fb[:, 0:nt_b, :], junk2v[:, 0:nt_b, :],
                        junk3v[:, 0:nt_b, :], op=ALU.add),
                        reads=["junk2", "junk3"], writes=[fbk])

                for ii, j in enumerate(tiles):
                    transpose_128(
                        lambda c, ii=ii: qfb[:, ii, c * TIL:(c + 1) * TIL], 4,
                        lambda h, ii=ii: qT2[:, ii, :]
                        .rearrange("p (a b) -> p a b", b=TIL),
                        ["qfb"], ["qT2"])
                    transpose_128(
                        lambda c, ii=ii: kfb[:, ii, c * TIL:(c + 1) * TIL], 4,
                        lambda h, j=j: kT_roll[:, :, (j % 8) * TIL:
                                               (j % 8 + 1) * TIL],
                        ["kfb"], [("ktr", j % 8)])

                for ii, j in enumerate(tiles):
                    if j >= q_lo:
                        attn_tile(l, j, ii, y_done)
            return y_done

        def emit_l0ffn_ffn(l, y_done, out_stream, q_lo):
            outd = streams[out_stream]
            out_done = [None] * NT
            k.dma("gpsimd", "cstg", gam_cur[:], gam_d[l], writes=["gam"])
            k.dma("gpsimd", "cstg", bet_cur[:], bet_d[l], writes=["bet"])
            k.dma("sync", "cst", b1a_col[:], b1a_d[l], writes=["b1a"])
            k.dma("sync", "cst", b1g_col[:], b1g_d[l], writes=["b1g"])
            k.dma("gpsimd", "cstg", biaso_b, b2_d[l], writes=["SMBo"])
            for tiles in chunks4(q_lo, NT):
                nt_b = len(tiles)
                for ii, j in enumerate(tiles):
                    k.dma("sync", "xw%d" % (j % 2), XW[:, j % XWN, :],
                          y_d[j * TIL:(j + 1) * TIL, :],
                          waits=[y_done[j]], writes=[("XW", j % XWN)])
                for ii, j in enumerate(tiles):
                    k.dma("sync", "hsh%d" % (ii % 2), A2W[:, ii, 0:512],
                          XW[:, j % XWN, 0:512], reads=[("XW", j % XWN)],
                          writes=[("A2W", ii)])
                    if j == q_lo:
                        k.dma("sync", "hsh%d" % (ii % 2), A2W[0:1, ii, 512:1024],
                              XW[0:1, j % XWN, 512:1024], reads=[("XW", j % XWN)],
                              writes=[("A2W", ii)])
                    else:
                        k.dma("sync", "hsh%d" % (ii % 2), A2W[0:1, ii, 512:1024],
                              XW[127:128, (j - 1) % XWN, 512:1024],
                              reads=[("XW", (j - 1) % XWN)], writes=[("A2W", ii)])
                    k.dma("sync", "hsh%d" % (ii % 2), A2W[1:128, ii, 512:1024],
                          XW[0:127, j % XWN, 512:1024], reads=[("XW", j % XWN)],
                          writes=[("A2W", ii)])
                ln_stats([(A2W[:, ii, :], [("A2W", ii)])
                          for ii in range(nt_b)])
                for ii, j in enumerate(tiles):
                    k.op("vector", lambda e, ii=ii: e.tensor_scalar(
                        junk[:], A2W[:, ii, :], statm[:, ii:ii + 1],
                        stati[:, ii:ii + 1], op0=ALU.subtract, op1=ALU.mult),
                        reads=[("A2W", ii), "statm", "stati"], writes=["junk"])
                    k.op("vector", lambda e: e.tensor_tensor(
                        junk[:], junk[:], gam_cur[:], op=ALU.mult),
                        reads=["junk", "gam"], writes=["junk"])
                    k.op("vector", lambda e: e.tensor_tensor(
                        o_tok[:], junk[:], bet_cur[:], op=ALU.add),
                        reads=["junk", "bet"], writes=["otok"])
                    for c in range(8):
                        k.op("tensor", lambda e, c=c: e.transpose(
                            PS_T[:, c * TIL:(c + 1) * TIL],
                            o_tok[:, c * TIL:(c + 1) * TIL], identb[:]),
                            reads=["otok", "identb"], writes=["PST"])
                    for h in range(2):
                        k.op("vector", lambda e, h=h, ii=ii: e.tensor_copy(
                            hTf[:, 4 * h:4 * h + 4, ii * TIL:(ii + 1) * TIL],
                            PS_T[:, 4 * h * TIL:(4 * h + 4) * TIL]
                            .rearrange("p (a b) -> p a b", b=TIL)),
                            reads=["PST"],
                            writes=["hTf", "kfb", "junk2"])
                PT_KEYS = ["pT", "junk3", "tbr", "tbf", "Sm", "Pm"]
                bcol = nt_b * TIL
                for cc in range(22):
                    wdt = 42 if cc == 21 else TIL
                    wr = cc % 2
                    k.dma("sync", "w1a%d" % wr, w1a_buf[:, wr, :, 0:wdt],
                          w1_st[l, :, cc * TIL:cc * TIL + wdt]
                          .rearrange("(a p) b -> p a b", p=TIL),
                          writes=[("wqk", wr)])
                    k.dma("sync", "w1g%d" % wr, w1g_buf[:, wr, :, 0:wdt],
                          w1_st[l, :, 2730 + cc * TIL:2730 + cc * TIL + wdt]
                          .rearrange("(a p) b -> p a b", p=TIL),
                          writes=[("w1g", wr)])
                    for d in range(8):
                        k.op("tensor", lambda e, d=d, cc=cc, wr=wr, wdt=wdt,
                             bcol=bcol: e.matmul(
                            PS_MM[0:wdt, 0, 0:bcol],
                            w1a_buf[:, wr, d, 0:wdt], hTf[:, d, 0:bcol],
                            start=(d == 0), stop=(d == 7)),
                            reads=[("wqk", wr), "hTf"], writes=[("PSMM", 0)])
                        k.op("tensor", lambda e, d=d, cc=cc, wr=wr, wdt=wdt,
                             bcol=bcol: e.matmul(
                            PS_S[0:wdt, 0:bcol], w1g_buf[:, wr, d, 0:wdt],
                            hTf[:, d, 0:bcol], start=(d == 0), stop=(d == 7)),
                            reads=[("w1g", wr), "hTf"],
                            writes=[("PSS", 0), ("PSS", 1)])
                    k.op("scalar", lambda e, cc=cc, wdt=wdt, bcol=bcol:
                         e.activation(
                        junk[0:wdt, 0:bcol], PS_S[0:wdt, 0:bcol], AF.Gelu,
                        bias=b1g_col[0:wdt, cc:cc + 1], scale=1.0),
                        reads=[("PSS", 0), ("PSS", 1), "b1g"], writes=["junk"])
                    k.op("vector", lambda e, cc=cc, wdt=wdt, bcol=bcol:
                         e.scalar_tensor_tensor(
                        pT_g[0:wdt, cc, 0:bcol], PS_MM[0:wdt, 0, 0:bcol],
                        b1a_col[0:wdt, cc:cc + 1], junk[0:wdt, 0:bcol],
                        op0=ALU.add, op1=ALU.mult),
                        reads=[("PSMM", 0), "b1a", "junk"], writes=PT_KEYS)
                PSB = [PS_MM[:, 0, :], PS_MM[:, 1, :],
                       PS_S[:, 0:512], PS_S[:, 512:1024]]
                PSK = [("PSMM", 0), ("PSMM", 1), ("PSS", 0), ("PSS", 1)]
                for half in range(2):
                    for cc in range(22):
                        wdt = 42 if cc == 21 else TIL
                        wr = cc % 2
                        k.dma("sync", "w2%d" % wr, w2_buf[0:wdt, wr, :],
                              w2_st[l, cc * TIL:cc * TIL + wdt,
                                    half * 512:(half + 1) * 512],
                              writes=[("w2", wr)])
                        for ti in range(nt_b):
                            k.op("tensor", lambda e, cc=cc, wr=wr, wdt=wdt,
                                 ti=ti: e.matmul(
                                PSB[ti], pT_g[0:wdt, cc, ti * TIL:(ti + 1) * TIL],
                                w2_buf[0:wdt, wr, :],
                                start=(cc == 0), stop=False),
                                reads=[("w2", wr), "pT"],
                                writes=[PSK[ti]])
                    for ti, j in enumerate(tiles):
                        k.op("tensor", lambda e, ti=ti, j=j, half=half: e.matmul(
                            PSB[ti], frow_b[:, j * TIL:(j + 1) * TIL],
                            biaso_b[:, half * 512:(half + 1) * 512],
                            start=False, stop=True),
                            reads=["SMBf", "SMBo"], writes=[PSK[ti]])
                        k.op("vector", lambda e, ti=ti, j=j, half=half:
                             e.scalar_tensor_tensor(
                            A2W[:, ti, half * 512:(half + 1) * 512], PSB[ti],
                            fpart[:, j:j + 1],
                            XW[:, j % XWN, half * 512:(half + 1) * 512],
                            op0=ALU.mult, op1=ALU.add),
                            reads=[PSK[ti], "fpart", ("XW", j % XWN)],
                            writes=[("A2W", ti)])
                for ti, j in enumerate(tiles):
                    out_done[j] = k.dma(
                        "sync", "xst", outd[j * TIL:(j + 1) * TIL, :],
                        A2W[:, ti, :], reads=[("A2W", ti)],
                        writes=[(out_stream, j)])
            return out_done

        def emit_l1_attn(in_done):
            ind = streams["x2"]
            outd = streams["x3"]
            out_done = [None] * NT
            kT1 = v_blk[:].rearrange("p a b -> p (a b)").rearrange(
                "p (a b) -> p a b", b=512)
            v1 = ktok_blk
            vprev = a_tile[:, 0, :]
            kTprev = a_tile[:, 1, :].rearrange("p (a b) -> p a b", b=TIL)
            for a in range(8):
                k.dma("gpsimd", "wres%d" % (a % 2), wv_res[:, a, :],
                      w_l1v_d[a * TIL:(a + 1) * TIL, :], writes=["wv"])
            k.dma("gpsimd", "cstg", mask_base[:, 0:256], m1_base_d,
                  writes=["mb"])
            k.dma("gpsimd", "cstg", maskcur[:, 0:256], m1_sp_d, writes=["msp"])
            blocks5 = chunks4(16, NT)
            for bi, tiles in enumerate(blocks5):
                nt_b = len(tiles)
                bcols = nt_b * TIL
                ring = bi % 2
                for ii, j in enumerate(tiles):
                    k.dma("sync", "xw%d" % (j % 2), XW[:, j % XWN, :],
                          ind[j * TIL:(j + 1) * TIL, :],
                          waits=[in_done[j]], writes=[("XW", j % XWN)])
                ln_stats([(XW[:, j % XWN, :], [("XW", j % XWN)])
                          for j in tiles])
                for ii, j in enumerate(tiles):
                    k.op("scalar", lambda e, j=j, ii=ii: e.activation(
                        o_tok[:], XW[:, j % XWN, :], AF.Identity,
                        bias=statm[:, ii:ii + 1], scale=stati[:, ii:ii + 1]),
                        reads=[("XW", j % XWN), "statm", "stati"],
                        writes=["otok"]) if False else None
                    k.op("vector", lambda e, j=j, ii=ii: e.tensor_scalar(
                        junk[:], XW[:, j % XWN, :], statm[:, ii:ii + 1],
                        stati[:, ii:ii + 1], op0=ALU.subtract, op1=ALU.mult),
                        reads=[("XW", j % XWN), "statm", "stati"],
                        writes=["junk"])
                    k.op("scalar", lambda e: e.activation(
                        o_tok[:], junk[:], AF.Identity, bias=0.0, scale=1.0),
                        reads=["junk"], writes=["otok"])
                    transpose_128(
                        lambda c: o_tok[:, c * TIL:(c + 1) * TIL], 8,
                        lambda h, ii=ii, ring=ring: xhT[:, ring, 4 * h:4 * h + 4,
                                                        ii * TIL:(ii + 1) * TIL],
                        ["otok"], [("xhT", ring)])
                for ni in range(16):
                    wring = ni % 2
                    wd = w_l1q_d if ni < 8 else w_l1k_d
                    nc_ = ni % 8
                    k.dma("gpsimd", "wqk%d" % wring, wqk_buf[:, wring, :, :],
                          wd[:, nc_ * TIL:(nc_ + 1) * TIL]
                          .rearrange("(a p) b -> p a b", p=TIL),
                          writes=[("wqk", wring)])
                    psr = ni % 2
                    for d in range(8):
                        k.op("tensor", lambda e, d=d, wring=wring, psr=psr,
                             bcols=bcols, ring=ring: e.matmul(
                            PS_MM[:, psr, 0:bcols], wqk_buf[:, wring, d, :],
                            xhT[:, ring, d, 0:bcols], start=(d == 0),
                            stop=(d == 7)),
                            reads=[("wqk", wring), ("xhT", ring)],
                            writes=[("PSMM", psr)])
                    if ni < 8:
                        k.op("scalar", lambda e, ni=ni, psr=psr, bcols=bcols:
                             e.activation(
                            qT_blk[:, ni, 0:bcols], PS_MM[:, psr, 0:bcols],
                            AF.Identity, bias=0.0, scale=cst[:, 0:1]),
                            reads=[("PSMM", psr), "cst"], writes=["qT"])
                    else:
                        for ii, j in enumerate(tiles):
                            k.op("vector", lambda e, ni=ni, psr=psr, ii=ii,
                                 j=j: e.tensor_copy(
                                kT1[:, ni - 8, (j % 4) * TIL:(j % 4 + 1) * TIL],
                                PS_MM[:, psr, ii * TIL:(ii + 1) * TIL]),
                                reads=[("PSMM", psr)], writes=[("k1", j % 4)])
                for ii, j in enumerate(tiles):
                    for nh in range(2):
                        psr = (ii * 2 + nh) % 2
                        for d in range(8):
                            k.op("tensor", lambda e, d=d, ii=ii, nh=nh, psr=psr,
                                 ring=ring: e.matmul(
                                PS_MM[:, psr, :],
                                xhT[:, ring, d, ii * TIL:(ii + 1) * TIL],
                                wv_res[:, d, nh * 512:(nh + 1) * 512],
                                start=(d == 0), stop=(d == 7)),
                                reads=[("xhT", ring), "wv"],
                                writes=[("PSMM", psr)])
                        k.op("vector", lambda e, j=j, nh=nh, psr=psr:
                             e.tensor_copy(
                            v1[:, j % 4, nh * 512:(nh + 1) * 512],
                            PS_MM[:, psr, :]),
                            reads=[("PSMM", psr)], writes=[("v1", j % 4)])
                for ii, j in enumerate(tiles):
                    if j < 17:
                        continue
                    mk = maskcur[:, 0:256] if j == 17 else mask_base[:, 0:256]
                    mkk = "msp" if j == 17 else "mb"
                    prev_in_block = (j - 1) in tiles
                    for h in range(16):
                        p, b = h // 2, (h % 2) * 64
                        for si in range(2):
                            sl = (j - 1) % 4 if si == 0 else j % 4
                            if si == 0 and not prev_in_block:
                                krhs = kTprev[b:b + 64, p, :]
                                krd = "kprev"
                            else:
                                krhs = kT1[b:b + 64, p, sl * TIL:(sl + 1) * TIL]
                                krd = ("k1", sl)
                            k.op("tensor", lambda e, p=p, b=b, si=si,
                                 krhs=krhs, ii=ii: e.matmul(
                                PS_S[:, si * TIL:(si + 1) * TIL],
                                qT_blk[b:b + 64, p, ii * TIL:(ii + 1) * TIL],
                                krhs, start=True, stop=True),
                                reads=["qT", krd],
                                writes=[("PSS", 0)])
                        k.op("vector", lambda e, mk=mk: e.tensor_tensor(
                            Sm[:, 0:256], PS_S[:, 0:256], mk, op=ALU.add),
                            reads=[("PSS", 0), mkk], writes=["Sm"])
                        k.op("vector", lambda e: e.reduce_max(
                            stat2[:, 40:41], Sm[:, 0:256], axis=AX.X,
                            negate=True), reads=["Sm"], writes=["stat2"])
                        k.op("scalar", lambda e: e.activation(
                            Pm[:, 0:256], Sm[:, 0:256], AF.Exp,
                            bias=stat2[:, 40:41], scale=1.0,
                            accum_out=stat[:, 40:41]),
                            reads=["Sm", "stat2"], writes=["Pm", "stat"])
                        k.op("vector", lambda e, h=h: e.reciprocal(
                            stati[:, 40 + h:41 + h], stat[:, 40:41]),
                            reads=["stat"], writes=["stati"])
                        for c in range(2):
                            k.op("tensor", lambda e, c=c: e.transpose(
                                PS_T[:, c * TIL:(c + 1) * TIL],
                                Pm[:, c * TIL:(c + 1) * TIL], identb[:]),
                                reads=["Pm", "identb"], writes=["PST"])
                        k.op("vector", lambda e: e.tensor_copy(
                            PT5[:, 0:2, :], PS_T[:, 0:256]
                            .rearrange("p (a b) -> p a b", b=TIL)),
                            reads=["PST"], writes=["PT5"])
                        bank = h // 8
                        col = (h % 8) * 64
                        for si in range(2):
                            sl = (j - 1) % 4 if si == 0 else j % 4
                            if si == 0 and not prev_in_block:
                                vrhs = vprev[:, h * 64:(h + 1) * 64]
                                vrd = "vprev"
                            else:
                                vrhs = v1[:, sl, h * 64:(h + 1) * 64]
                                vrd = ("v1", sl)
                            k.op("tensor", lambda e, si=si, vrhs=vrhs, h=h,
                                 bank=bank, col=col: e.matmul(
                                PS_O[:, bank, col:col + 64], PT5[:, si, :],
                                vrhs, start=(si == 0), stop=(si == 1)),
                                reads=["PT5", vrd],
                                writes=[("PSO", bank)])
                    for h in range(16):
                        bank, col = h // 8, (h % 8) * 64
                        k.op("scalar", lambda e, h=h, bank=bank, col=col:
                             e.activation(
                            ynorm[:, h * 64:(h + 1) * 64],
                            PS_O[:, bank, col:col + 64], AF.Identity,
                            bias=0.0, scale=stati[:, 40 + h:41 + h]),
                            reads=[("PSO", bank), "stati"], writes=["ynorm"])
                    if _os.environ.get("KX3RAW", "0") == "1":
                        out_done[j] = k.dma("gpsimd", "xst",
                                            outd[j * TIL:(j + 1) * TIL, :],
                                            ynorm[:], reads=["ynorm"],
                                            writes=[("x3", j)])
                        continue
                    ln_stats([(ynorm[:], ["ynorm"])])
                    k.op("vector", lambda e: e.tensor_scalar(
                        junk[:], ynorm[:], statm[:, 0:1], stati[:, 0:1],
                        op0=ALU.subtract, op1=ALU.mult),
                        reads=["ynorm", "statm", "stati"], writes=["junk"])
                    k.op("vector", lambda e, j=j: e.tensor_tensor(
                        XW[:, j % XWN, :], XW[:, j % XWN, :], junk[:],
                        op=ALU.add),
                        reads=["junk", ("XW", j % XWN)],
                        writes=[("XW", j % XWN)])
                    out_done[j] = k.dma("gpsimd", "xst",
                                        outd[j * TIL:(j + 1) * TIL, :],
                                        XW[:, j % XWN, :],
                                        reads=[("XW", j % XWN)],
                                        writes=[("x3", j)])
                jl = tiles[-1]
                k.op("vector", lambda e, jl=jl: e.tensor_copy(
                    vprev[:], v1[:, jl % 4, :]),
                    reads=[("v1", jl % 4)], writes=["vprev"])
                k.op("vector", lambda e, jl=jl: e.tensor_copy(
                    kTprev[:], kT1[:, :, (jl % 4) * TIL:(jl % 4 + 1) * TIL]),
                    reads=[("k1", jl % 4)], writes=["kprev"])
            return out_done

        def emit_l1_ffn(in_done):
            ind = streams["x3"]
            outd = streams["x4"]
            out_done = [None] * NT
            for a in range(8):
                k.dma("gpsimd", "wres%d" % (a % 2), woT_res[:, a, :],
                      w_l1fT_d[a * TIL:(a + 1) * TIL, :], writes=["woT"])
                k.dma("gpsimd", "wres%d" % (a % 2), U_buf[:, a, :].unsqueeze(1).squeeze(1) if False else U_buf[:, a, :],
                      UcT_d[a * TIL:(a + 1) * TIL, :], writes=["Ub"])
                k.dma("gpsimd", "wres%d" % (a % 2), gw_buf[:, a, :],
                      gwT_d[a * TIL:(a + 1) * TIL, :], writes=["gw"])
            k.dma("gpsimd", "cstg", V_buf[:], Vc_d, writes=["Vb"])
            for tiles in chunks4(17, NT):
                for ii, j in enumerate(tiles):
                    k.dma("sync", "xw%d" % (j % 2), XW[:, j % XWN, :],
                          ind[j * TIL:(j + 1) * TIL, :],
                          waits=[in_done[j]], writes=[("XW", j % XWN)])
                ln_stats([(XW[:, j % XWN, :], [("XW", j % XWN)])
                          for j in tiles])
                for ii, j in enumerate(tiles):
                    k.op("vector", lambda e, j=j, ii=ii: e.tensor_scalar(
                        junk[:], XW[:, j % XWN, :], statm[:, ii:ii + 1],
                        stati[:, ii:ii + 1], op0=ALU.subtract, op1=ALU.mult),
                        reads=[("XW", j % XWN), "statm", "stati"],
                        writes=["junk"])
                    k.op("scalar", lambda e: e.activation(
                        o_tok[:], junk[:], AF.Identity, bias=0.0, scale=1.0),
                        reads=["junk"], writes=["otok"])
                    transpose_128(
                        lambda c: o_tok[:, c * TIL:(c + 1) * TIL], 8,
                        lambda h: oT_t[:, 4 * h:4 * h + 4, :],
                        ["otok"], ["oT"])
                    for d in range(8):
                        k.op("tensor", lambda e, d=d: e.matmul(
                            PS_ST[:, 0:8], oT_t[:, d, :], gw_buf[:, d, :],
                            start=(d == 0), stop=(d == 7)),
                            reads=["oT", "gw"], writes=["PSST"])
                    for d in range(8):
                        k.op("tensor", lambda e, d=d: e.matmul(
                            PS_S[:, 0:TIL], oT_t[:, d, :], U_buf[:, d, :],
                            start=(d == 0), stop=(d == 7)),
                            reads=["oT", "Ub"], writes=[("PSS", 0)])
                    k.op("vector", lambda e: e.reduce_max(
                        stat2[:, 40:41], PS_ST[:, 0:8], axis=AX.X,
                        negate=True), reads=["PSST"], writes=["stat2"])
                    k.op("scalar", lambda e: e.activation(
                        junk[:, 0:8], PS_ST[:, 0:8], AF.Exp,
                        bias=stat2[:, 40:41], scale=1.0,
                        accum_out=stat[:, 40:41]),
                        reads=["PSST", "stat2"], writes=["junk", "stat"])
                    k.op("vector", lambda e: e.reciprocal(
                        stati[:, 40:41], stat[:, 40:41]),
                        reads=["stat"], writes=["stati"])
                    k.op("vector", lambda e: e.tensor_scalar(
                        stati[:, 41:42], stati[:, 40:41], 1.0 / 16.0, 0.0,
                        op0=ALU.mult, op1=ALU.add),
                        reads=["stati"], writes=["stati"])
                    k.op("vector", lambda e: e.scalar_tensor_tensor(
                        sT_m[:, 0, :].rearrange("p (a b) -> p a b", b=16),
                        PS_S[:, 0:TIL].rearrange("p (a b) -> p a b", b=16),
                        stati[:, 41:42],
                        junk[:, 0:8].unsqueeze(2).broadcast_to([TIL, 8, 16]),
                        op0=ALU.mult, op1=ALU.mult),
                        reads=[("PSS", 0), "stati", "junk"], writes=["sT"])
                    k.op("tensor", lambda e: e.transpose(
                        PS_T[:, 0:TIL], sT_m[:, 0, :], identb[:]),
                        reads=["sT", "identb"], writes=["PST"])
                    k.op("vector", lambda e: e.tensor_copy(
                        PT5[:, 0, :], PS_T[:, 0:TIL]),
                        reads=["PST"], writes=["PT5"])
                    for nh in range(2):
                        psr = nh
                        for d in range(8):
                            k.op("tensor", lambda e, d=d, nh=nh, psr=psr:
                                 e.matmul(
                                PS_MM[:, psr, :], oT_t[:, d, :],
                                woT_res[:, d, nh * 512:(nh + 1) * 512],
                                start=(d == 0), stop=False),
                                reads=["oT", "woT"], writes=[("PSMM", psr)])
                        k.op("tensor", lambda e, nh=nh, psr=psr: e.matmul(
                            PS_MM[:, psr, :], PT5[:, 0, :],
                            V_buf[:, nh * 512:(nh + 1) * 512],
                            start=False, stop=True),
                            reads=["PT5", "Vb"], writes=[("PSMM", psr)])
                    k.op("scalar", lambda e: e.activation(
                        junk[:, 0:512], PS_MM[:, 0, :], AF.Square, bias=0.0,
                        scale=1.0, accum_out=stat[:, 8:9]),
                        reads=[("PSMM", 0)], writes=["junk", "stat"])
                    k.op("scalar", lambda e: e.activation(
                        junk[:, 512:1024], PS_MM[:, 1, :], AF.Square,
                        bias=0.0, scale=1.0, accum_out=stat[:, 9:10]),
                        reads=[("PSMM", 1)], writes=["junk", "stat"])
                    k.op("vector", lambda e: e.reduce_sum(
                        stat2[:, 8:9].unsqueeze(2), PS_MM[:, 0, :]
                        .unsqueeze(1), axis=AX.X),
                        reads=[("PSMM", 0)], writes=["stat2"])
                    k.op("vector", lambda e: e.reduce_sum(
                        stat2[:, 9:10].unsqueeze(2), PS_MM[:, 1, :]
                        .unsqueeze(1), axis=AX.X),
                        reads=[("PSMM", 1)], writes=["stat2"])
                    k.op("vector", lambda e: e.tensor_tensor(
                        statm[:, 8:9], stat2[:, 8:9], stat2[:, 9:10],
                        op=ALU.add), reads=["stat2"], writes=["statm"])
                    k.op("vector", lambda e: e.tensor_scalar(
                        statm[:, 8:9], statm[:, 8:9], 1.0 / D, 0.0,
                        op0=ALU.mult, op1=ALU.add),
                        reads=["statm"], writes=["statm"])
                    k.op("vector", lambda e: e.tensor_tensor(
                        stat[:, 8:9], stat[:, 8:9], stat[:, 9:10], op=ALU.add),
                        reads=["stat"], writes=["stat"])
                    k.op("vector", lambda e: e.tensor_scalar(
                        stat[:, 8:9], stat[:, 8:9], 1.0 / D, 0.0,
                        op0=ALU.mult, op1=ALU.add),
                        reads=["stat"], writes=["stat"])
                    k.op("vector", lambda e: e.tensor_tensor(
                        stat2[:, 10:11], statm[:, 8:9], statm[:, 8:9],
                        op=ALU.mult), reads=["statm"], writes=["stat2"])
                    k.op("vector", lambda e: e.tensor_tensor(
                        stat[:, 8:9], stat[:, 8:9], stat2[:, 10:11],
                        op=ALU.subtract), reads=["stat", "stat2"],
                        writes=["stat"])
                    k.op("vector", lambda e: e.tensor_scalar(
                        stat[:, 8:9], stat[:, 8:9], EPS, 0.0, op0=ALU.add,
                        op1=ALU.add), reads=["stat"], writes=["stat"])
                    k.op("scalar", lambda e: e.activation(
                        stat2[:, 8:9], stat[:, 8:9], AF.Sqrt, bias=0.0,
                        scale=1.0), reads=["stat"], writes=["stat2"])
                    k.op("vector", lambda e: e.reciprocal(
                        stati[:, 8:9], stat2[:, 8:9]),
                        reads=["stat2"], writes=["stati"])
                    k.op("vector", lambda e: e.tensor_scalar(
                        ynorm[:, 0:512], PS_MM[:, 0, :], statm[:, 8:9],
                        stati[:, 8:9], op0=ALU.subtract, op1=ALU.mult),
                        reads=[("PSMM", 0), "statm", "stati"],
                        writes=["ynorm"])
                    k.op("vector", lambda e: e.tensor_scalar(
                        ynorm[:, 512:1024], PS_MM[:, 1, :], statm[:, 8:9],
                        stati[:, 8:9], op0=ALU.subtract, op1=ALU.mult),
                        reads=[("PSMM", 1), "statm", "stati"],
                        writes=["ynorm"])
                    k.op("vector", lambda e, j=j: e.tensor_tensor(
                        XW[:, j % XWN, :], XW[:, j % XWN, :], ynorm[:],
                        op=ALU.add),
                        reads=["ynorm", ("XW", j % XWN)],
                        writes=[("XW", j % XWN)])
                    out_done[j] = k.dma("gpsimd", "xst",
                                        outd[j * TIL:(j + 1) * TIL, :],
                                        XW[:, j % XWN, :],
                                        reads=[("XW", j % XWN)],
                                        writes=[("x4", j)])
            return out_done

        stream_done = {"x1": t_x1_st}
        final_stream = "x1"
        if n_stages >= 2:
            barrier()
            yd = emit_l0ffn_attn(0, "x1", t_x1_st, kv_lo=1, q_lo=8, do_ln=True)
            stream_done["y"] = yd
            final_stream = "y"
        if n_stages >= 3:
            barrier()
            xl2d = emit_l0ffn_ffn(0, yd, "xl2", q_lo=8)
            stream_done["xl2"] = xl2d
            final_stream = "xl2"
        if n_stages >= 4:
            barrier()
            yd2 = emit_l0ffn_attn(1, "xl2", xl2d, kv_lo=9, q_lo=16, do_ln=False)
            barrier()
            zd = emit_l0ffn_ffn(1, yd2, "z", q_lo=16)
            stream_done["z"] = zd
            barrier()
            x2d = [None] * NT
            for tiles in chunks4(16, NT):
                nt_b = len(tiles)
                for ii, j in enumerate(tiles):
                    k.dma("sync", "xw%d" % (j % 2), XW[:, j % XWN, :],
                          x1_d[j * TIL:(j + 1) * TIL, :],
                          waits=[t_x1_st[j]], writes=[("XW", j % XWN)])
                    k.dma("sync", "hsh%d" % (j % 2), A2W[:, ii, :],
                          z_d[j * TIL:(j + 1) * TIL, :],
                          waits=[zd[j]], writes=[("A2W", ii)])
                ln_stats([(A2W[:, ii, :], [("A2W", ii)])
                          for ii in range(nt_b)])
                for ii, j in enumerate(tiles):
                    k.op("vector", lambda e, ii=ii: e.tensor_scalar(
                        ynorm[:], A2W[:, ii, :], statm[:, ii:ii + 1],
                        stati[:, ii:ii + 1], op0=ALU.subtract, op1=ALU.mult),
                        reads=[("A2W", ii), "statm", "stati"], writes=["ynorm"])
                    k.op("vector", lambda e, j=j: e.tensor_tensor(
                        XW[:, j % XWN, :], XW[:, j % XWN, :], ynorm[:], op=ALU.add),
                        reads=["ynorm", ("XW", j % XWN)], writes=[("XW", j % XWN)])
                    x2d[j] = k.dma("gpsimd", "xst",
                                   x2_d[j * TIL:(j + 1) * TIL, :],
                                   XW[:, j % XWN, :], reads=[("XW", j % XWN)],
                                   writes=[("x2", j)])
            stream_done["x2"] = x2d
            final_stream = "x2"
        if n_stages >= 5:
            barrier()
            x3d = emit_l1_attn(x2d)
            stream_done["x3"] = x3d
            final_stream = "x3"
        if n_stages >= 6:
            barrier()
            x4d = emit_l1_ffn(x3d)
            stream_done["x4"] = x4d
            final_stream = "x4"

        if spy_out is not None:
            k.op("vector", lambda e: e.tensor_copy(
                junk[:], xhT[:, 0, 0:2, 0:512].rearrange("p a b -> p (a b)")),
                reads=[("xhT", 0)], writes=["junk"])
            k.dma("sync", "spy", spy_out[0:TIL, :], junk[:], reads=["junk"])
            k.op("vector", lambda e: e.tensor_copy(
                junk[:], qT_blk[:, 0:2, 0:512].rearrange("p a b -> p (a b)")),
                reads=["qT"], writes=["junk"])
            k.dma("sync", "spy", spy_out[TIL:2 * TIL, :], junk[:], reads=["junk"])
            k.op("vector", lambda e: e.tensor_copy(
                junk[:], kT_blk[:, 0:2, 0:512].rearrange("p a b -> p (a b)")),
                reads=["kT"], writes=["junk"])
            k.dma("sync", "spy", spy_out[2 * TIL:3 * TIL, :], junk[:], reads=["junk"])
            k.op("vector", lambda e: e.tensor_copy(junk[:], v_blk[:, 3, :]),
                reads=["v"], writes=["junk"])
            k.dma("sync", "spy", spy_out[3 * TIL:4 * TIL, :], junk[:], reads=["junk"])
            k.op("vector", lambda e: e.tensor_copy(junk[:], ktok_blk[:, 3, :]),
                reads=["ktok"], writes=["junk"])
            k.dma("sync", "spy", spy_out[4 * TIL:5 * TIL, :], junk[:], reads=["junk"])
            k.op("vector", lambda e: e.tensor_copy(
                junk[:, 0:512], state_sb[:]), reads=["stsb"], writes=["junk"])
            k.op("vector", lambda e: e.tensor_copy(
                junk2s[:, 0:512], sT_m[:].rearrange("p a b -> p (a b)")),
                reads=["sT"], writes=["junk2s"])
            k.dma("sync", "spy", spy_out[6 * TIL:7 * TIL, :], junk2s[:],
                  reads=["junk2s"])
            k.op("vector", lambda e: e.tensor_copy(junk2s[:], o_tok[:]),
                reads=["otok"], writes=["junk2s"])
            k.dma("sync", "spy", spy_out[7 * TIL:8 * TIL, :], junk2s[:],
                  reads=["junk2s"])
            k.op("vector", lambda e: e.tensor_copy(junk2s[:], ynorm[:]),
                reads=["ynorm"], writes=["junk2s"])
            k.dma("sync", "spy", spy_out[8 * TIL:9 * TIL, :], junk2s[:],
                  reads=["junk2s"])
            k.op("vector", lambda e: e.tensor_copy(
                junk[0:1, 512:1024], SMB[0:1, NTOK:NTOK + 512]),
                reads=["SMBq"], writes=["junk"])
            k.dma("sync", "spy", spy_out[5 * TIL:6 * TIL, :], junk[:], reads=["junk"])

        sdram = streams[final_stream]
        built = {j for tt in blocks for j in tt}
        if dbg is not None:
            nm, tl = dbg
            tl = [j for j in tl if j in built]
            for i, j in enumerate(tl):
                k.dma("sync", "out%d" % (i % 2), dbg_out[i * TIL:(i + 1) * TIL, :],
                      streams[nm][j * TIL:(j + 1) * TIL, :], reads=[(nm, j)])
        for i, j in enumerate(range(NT - OWN // TIL, NT)):
            if j not in built:
                continue
            k.dma("sync", "out%d" % (i % 2), xout[i * TIL:(i + 1) * TIL, :],
                  sdram[j * TIL:(j + 1) * TIL, :], reads=[(final_stream, j)])

        for s in k.sems_needed():
            k.semh[s] = E(nc.semaphore(s))
        final_out = [k.last("c_out0"), k.last("c_out1")]
        with nc.Block() as block:
            @block.sync
            def _(sync):
                for fn in k.q["sync"]:
                    fn(sync)
                for t in final_out:
                    if t[1] > 0:
                        sync.wait_ge(k.semh[t[0]], t[1])

            @block.gpsimd
            def _(gpsimd):
                for fn in k.q["gpsimd"]:
                    fn(gpsimd)

            @block.tensor
            def _(tensor):
                for fn in k.q["tensor"]:
                    fn(tensor)

            @block.vector
            def _(vector):
                for fn in k.q["vector"]:
                    fn(vector)

            @block.scalar
            def _(scalar):
                for fn in k.q["scalar"]:
                    fn(scalar)

    return nc


def _rot_tables(t):
    t = t.astype(np.float64)[:, None]
    half = np.arange(0, 64, 2, dtype=np.float64)
    inv = 1.0 / 10000.0 ** (half / 64.0)
    fr = t * inv[None, :]
    fr = np.concatenate([fr, fr], axis=1)
    basev = (half + 0.4 * 64.0) / (1.4 * 64.0)
    power = (t - 2048.0) / 256.0
    sc = basev[None, :] ** power
    sc = np.concatenate([sc, sc], axis=1)
    cos, sin = np.cos(fr), np.sin(fr)
    sgn = np.where(np.arange(64) < 32, -1.0, 1.0)[None, :]
    Aq = 8.0 * cos * sc
    Bq = 8.0 * sgn * sin * sc
    Ak = cos / sc
    Bk = sgn * sin / sc
    return np.concatenate([Aq, Bq, Ak, Bk], axis=1).astype(np.float32)


def _host_prep(L0_attn_x):
    B = L0_attn_x.shape[0]
    base = np.zeros((TIL, 640), np.float32)
    r = np.arange(TIL)[:, None]
    c = np.arange(640)[None, :]
    base[~((c >= r) & (c <= r + 512))] = NEG
    cores = []
    for b in range(B):
        for h in range(2):
            s = h * OWN
            lo = s - HALO
            pad = max(0, -lo)
            xsl = np.zeros((NTOK, D), np.float32)
            xsl[pad:] = np.asarray(L0_attn_x[b, max(0, lo):s + OWN], np.float32)
            frow = np.zeros((1, NTOK), np.float32)
            frow[0, pad:] = 1.0
            fpart = np.zeros((TIL, NT), np.float32)
            for j in range(NT):
                fpart[:, j] = (j * TIL + np.arange(TIL)) >= pad
            msp = np.zeros((4, TIL, 640), np.float32)
            for q in range(4):
                lj = (17 + q) * TIL
                cut = pad - lj + 512
                m = base.copy()
                if cut > 0:
                    m[:, 0:cut] += NEG
                msp[q] = m
            tglob = np.arange(NTOK, dtype=np.float64) + (s - HALO)
            tab = _rot_tables(tglob).reshape(NT, TIL, 256)
            m1b = np.zeros((TIL, 256), np.float32)
            r1 = np.arange(TIL)[:, None]
            c1 = np.arange(256)[None, :]
            allowed = ((c1 >= 64) & (c1 <= 128 + r1) & (r1 < 64)) | \
                      ((c1 >= 128) & (c1 <= 128 + r1) & (r1 >= 64))
            m1b[~allowed] = NEG
            m1s = m1b.copy()
            cut1 = pad - 2048
            if cut1 > 0:
                m1s[:, 0:cut1] += NEG
            cores.append(dict(xin=np.ascontiguousarray(xsl), frow=frow,
                              fpart=fpart, mask_base=base, mask_sp=msp,
                              tab4=np.ascontiguousarray(tab),
                              m1_base=m1b, m1_sp=m1s))
    return cores


def _host_consts():
    ident = np.eye(TIL, dtype=np.float32)
    causalT = np.triu(np.ones((TIL, TIL), np.float32))
    return dict(ident=ident, causalT=causalT)


def _host_weights(i):
    qs = np.asarray(i["L0_ffn_q_scale"], np.float32)
    ks = np.asarray(i["L0_ffn_k_scale"], np.float32)
    qs4 = np.zeros((2, TIL, 4, 64), np.float32)
    for l in range(2):
        qs4[l, :, 0] = np.tile(qs[l], (TIL, 1))
        qs4[l, :, 1] = np.tile(np.roll(qs[l], -32), (TIL, 1))
        qs4[l, :, 2] = np.tile(ks[l], (TIL, 1))
        qs4[l, :, 3] = np.tile(np.roll(ks[l], -32), (TIL, 1))
    b1 = np.asarray(i["L0_ffn_b1"], np.float32)
    b1a = np.zeros((2, TIL, 22), np.float32)
    b1g = np.zeros((2, TIL, 22), np.float32)
    for cc in range(22):
        w = min(TIL, 2730 - cc * TIL)
        b1a[:, 0:w, cc] = b1[:, cc * TIL:cc * TIL + w]
        b1g[:, 0:w, cc] = b1[:, 2730 + cc * TIL:2730 + cc * TIL + w]
    return dict(
        w_l0a_qkvT=np.ascontiguousarray(np.asarray(i["L0_attn_Wqkv_w"], np.float32).T),
        b_l0a_qkv=np.asarray(i["L0_attn_Wqkv_b"], np.float32).reshape(1, -1),
        w_l0a_outT=np.ascontiguousarray(np.asarray(i["L0_attn_out_w"], np.float32).T),
        b_l0a_out=np.asarray(i["L0_attn_out_b"], np.float32).reshape(1, -1),
        w_ffn_qkv=np.ascontiguousarray(np.asarray(i["L0_ffn_wqkv"], np.float32)),
        w_ffn_wo=np.ascontiguousarray(np.asarray(i["L0_ffn_wo"], np.float32)),
        w_ffn_w1=np.ascontiguousarray(np.asarray(i["L0_ffn_w1"], np.float32)),
        w_ffn_w2=np.ascontiguousarray(np.asarray(i["L0_ffn_w2"], np.float32)),
        qs4=qs4,
        gam=np.ascontiguousarray(np.tile(
            np.asarray(i["L0_ffn_ln_g"], np.float32)[:, None, :], (1, TIL, 1))),
        bet=np.ascontiguousarray(np.tile(
            np.asarray(i["L0_ffn_ln_b"], np.float32)[:, None, :], (1, TIL, 1))),
        b1a=b1a, b1g=b1g,
        b2=np.asarray(i["L0_ffn_b2"], np.float32).reshape(2, 1, D),
        w_l1_q=np.ascontiguousarray(np.asarray(i["L1_attn_Wq"], np.float32)),
        w_l1_k=np.ascontiguousarray(np.asarray(i["L1_attn_Wk"], np.float32)),
        w_l1_v=np.ascontiguousarray(np.asarray(i["L1_attn_Wv"], np.float32)),
        w_l1f_T=np.ascontiguousarray(np.asarray(i["L1_ffn_weight"], np.float32).T),
        gate_wT=np.ascontiguousarray(np.asarray(i["L1_ffn_gate_w"], np.float32).T),
        U_catT=np.ascontiguousarray(np.asarray(i["L1_ffn_lora_U"], np.float32)
                                    .transpose(2, 0, 1).reshape(D, TIL)),
        V_cat=np.ascontiguousarray(np.asarray(i["L1_ffn_lora_V"], np.float32)
                                   .transpose(0, 2, 1).reshape(TIL, D)),
    )


def kernel(**inputs):
    x = np.asarray(inputs["L0_attn_x"], np.float32)
    B = x.shape[0]
    cores = _host_prep(x)
    consts = _host_consts()
    wmap = _host_weights(inputs)
    nc = build_nc(n_stages=N_STAGES, dbg=None)
    in_maps = [dict(c, **consts, **wmap) for c in cores]
    res = run_bass_kernel_spmd(nc, in_maps, core_ids=list(range(8)))
    out = np.zeros((B, 2 * OWN, D), np.float32)
    for ci in range(8):
        b, h = ci // 2, ci % 2
        out[b, h * OWN:(h + 1) * OWN] = res.results[ci]["xout"]
    return out


# version 06
# speedup vs baseline: 1.0980x; 1.0325x over previous
import contextlib
import os
import numpy as np
import ml_dtypes
import concourse.bass as bass
import concourse.mybir as mybir
from concourse.bass_utils import run_bass_kernel_spmd

F32 = mybir.dt.float32
BF16 = mybir.dt.bfloat16
AF = mybir.ActivationFunctionType
ALU = mybir.AluOpType
AX = mybir.AxisListType

TIL = 128
HALO = 2176
OWN = 2048
NT = (HALO + OWN) // TIL
NTOK = HALO + OWN
D = 1024
NEG = -1.0e30
EPS = 1e-5

N_STAGES = 6

ENGINES = ("sync", "gpsimd", "tensor", "vector", "scalar")


class K:

    def __init__(self):
        self.q = {e: [] for e in ENGINES}
        self.cnt = {}
        self.waited = {e: {} for e in ENGINES}
        self.semh = {}
        self.last_w = {}
        self.readers = {}
        self.chan_last = {}
        self.esem = {"tensor": "tns", "vector": "vec", "scalar": "scl",
                     "gpsimd": "gps", "sync": "snc"}

    def sems_needed(self):
        return ["tns", "vec", "scl", "gps", "snc"] + sorted(
            s for s in self.cnt if s.startswith("c_"))

    def inc(self, sem, kk=1):
        self.cnt[sem] = self.cnt.get(sem, 0) + kk
        return (sem, self.cnt[sem])

    def last(self, sem):
        return (sem, self.cnt.get(sem, 0))

    def wait(self, eng, tok):
        if tok is None:
            return
        sem, val = tok
        if val <= 0 or self.waited[eng].get(sem, 0) >= val:
            return
        self.waited[eng][sem] = val
        self.q[eng].append(lambda e, s=sem, v=val: e.wait_ge(self.semh[s], v))

    PSUM_KEYS = ("PSMM", "PSS", "PSO", "PSST", "PST")

    def _is_psum(self, key):
        kk = key[0] if isinstance(key, tuple) else key
        return kk in self.PSUM_KEYS

    def _deps(self, reads, writes, waits):
        deps = list(waits)
        for r in reads:
            deps.append(self.last_w.get(r))
            if self._is_psum(r):
                deps.extend(self.readers.get(r, ()))
        for w in writes:
            deps.append(self.last_w.get(w))
            deps.extend(self.readers.get(w, ()))
        return deps

    def _record(self, t, reads, writes):
        for r in reads:
            if self._is_psum(r):
                self.last_w[r] = t
                self.readers[r] = []
            else:
                self.readers.setdefault(r, []).append(t)
        for w in writes:
            self.last_w[w] = t
            self.readers[w] = []

    def op(self, eng, fn, reads=(), writes=(), waits=()):
        for d in self._deps(reads, writes, waits):
            self.wait(eng, d)
        sem = self.esem[eng]
        t = self.inc(sem, 1)
        def wrapped(e, fn=fn, s=sem):
            fn(e).then_inc(self.semh[s], 1)
        self.q[eng].append(wrapped)
        self._record(t, reads, writes)
        return t

    def dma(self, eng, ch, out, in_, reads=(), writes=(), waits=()):
        sem = "c_" + ch
        self.cnt.setdefault(sem, 0)
        deps = self._deps(reads, writes, waits)
        deps.append(self.chan_last.get(ch))
        for d in deps:
            self.wait(eng, d)
        t = self.inc(sem, 16)
        def wrapped(e, o=out, i=in_, s=sem):
            e.dma_start(out=o, in_=i).then_inc(self.semh[s], 16)
        self.q[eng].append(wrapped)
        self.chan_last[ch] = t
        self._record(t, reads, writes)
        return t


def chunks4(lo, hi, w=4):
    out = []
    j = lo
    while j < hi:
        out.append(list(range(j, min(hi, j + w))))
        j += w
    return out


def build_nc(n_stages=N_STAGES, dbg=None):
    nc = bass.Bass("TRN2", target_bir_lowering=False, debug=False)
    k = K()

    def din(name, shape, dt=F32):
        return nc.dram_tensor(name, shape, dt, kind="ExternalInput").ap()

    xin = din("xin", [NTOK, D])
    frow_d = din("frow", [1, NTOK])
    ident_d = din("ident", [TIL, TIL])
    causalT_d = din("causalT", [TIL, TIL])
    fpart_d = din("fpart", [TIL, NT])
    mask_base_d = din("mask_base", [TIL, 640])
    mask_sp_d = din("mask_sp", [4, TIL, 640])
    tab4_d = din("tab4", [NT, TIL, 256])
    qs4_d = din("qs4", [2, TIL, 4, 64])
    gam_d = din("gam", [2, TIL, D])
    bet_d = din("bet", [2, TIL, D])
    b1a_d = din("b1a", [2, TIL, 22])
    b1g_d = din("b1g", [2, TIL, 22])
    b2_d = din("b2", [2, 1, D])
    w_ffn_qkv_d = din("w_ffn_qkv", [2, D, 1536])
    w_ffn_wo_d = din("w_ffn_wo", [2, 512, D])
    w_ffn_w1_d = din("w_ffn_w1", [2, D, 5460])
    w_ffn_w2_d = din("w_ffn_w2", [2, 2730, D])
    w_l1q_d = din("w_l1_q", [D, D])
    w_l1k_d = din("w_l1_k", [D, D])
    w_l1v_d = din("w_l1_v", [D, D])
    w_l1fT_d = din("w_l1f_T", [D, D])
    gwT_d = din("gate_wT", [D, 8])
    UcT_d = din("U_catT", [D, TIL])
    Vc_d = din("V_cat", [TIL, D])
    m1_base_d = din("m1_base", [TIL, 256])
    m1_sp_d = din("m1_sp", [TIL, 256])
    w_qkvT_d = din("w_l0a_qkvT", [D, 3 * D])
    b_qkv_d = din("b_l0a_qkv", [1, 3 * D])
    w_outT_d = din("w_l0a_outT", [D, D])
    b_out_d = din("b_l0a_out", [1, D])

    xout = nc.dram_tensor("xout", [OWN, D], F32, kind="ExternalOutput").ap()
    x1_d = nc.dram_tensor("x1_d", [NTOK, D], F32, kind="Internal").ap()
    w1_st = nc.dram_tensor("w1_st", [2, D, 5460], BF16, kind="Internal").ap()
    w2_st = nc.dram_tensor("w2_st", [2, 2730, D], BF16, kind="Internal").ap()
    y_d = nc.dram_tensor("y_d", [NTOK, D], F32, kind="Internal").ap()
    xl2_d = nc.dram_tensor("xl2_d", [NTOK, D], F32, kind="Internal").ap()
    z_d = nc.dram_tensor("z_d", [NTOK, D], F32, kind="Internal").ap()
    x2_d = nc.dram_tensor("x2_d", [NTOK, D], F32, kind="Internal").ap()
    streams = {"x1": x1_d, "y": y_d, "xl2": xl2_d, "z": z_d, "x2": x2_d,
               "x3": z_d, "x4": x1_d}
    spy_out = None
    if int(os.environ.get("KSPY", "0")):
        spy_out = nc.dram_tensor("spy", [9 * TIL, 1024], F32,
                                 kind="ExternalOutput").ap()
    dbg_out = None
    if dbg is not None:
        dbg_out = nc.dram_tensor("dbgo", [len(dbg[1]) * TIL, D], F32,
                                 kind="ExternalOutput").ap()

    with contextlib.ExitStack() as ctx:
        E = ctx.enter_context

        def sb(name, shape, dt):
            return E(nc.sbuf_tensor(name, shape, dt))

        def ps(name, shape, dt):
            return E(nc.psum_tensor(name, shape, dt))

        XWN = 5
        XW = sb("XW", [TIL, XWN, D], F32)
        SMB = sb("SMB", [TIL, NTOK + 4 * D], BF16)
        frow_b = SMB[0:1, 0:NTOK]
        biasq_b = SMB[0:1, NTOK:NTOK + 3 * D]
        biaso_b = SMB[0:1, NTOK + 3 * D:NTOK + 4 * D]
        identb = sb("identb_s", [TIL, TIL], BF16)
        causalT = sb("causalT_s", [TIL, TIL], F32)

        xhT = sb("xhT", [TIL, 2, 8, 512], BF16)
        wqk_buf = sb("wqk_buf", [TIL, 2, 8, TIL], BF16)
        wv_res = sb("wv_res", [TIL, 8, D], BF16)
        woT_res = sb("woT_res", [TIL, 8, D], BF16)
        qT_blk = sb("qT_blk", [TIL, 8, 512], BF16)
        kT_blk = sb("kT_blk", [TIL, 8, 512], BF16)
        v_blk = sb("v_blk", [TIL, 4, D], BF16)
        ktok_blk = sb("ktok_blk", [TIL, 4, D], BF16)
        sT_m = sb("sT_m", [TIL, 4, TIL], BF16)
        state_acc = sb("state_acc", [TIL, 512], F32)
        state_sb = sb("state_sb", [TIL, 512], BF16)
        o_tok = sb("o_tok", [TIL, D], BF16)
        oT_t = sb("oT_t", [TIL, 8, TIL], BF16)
        stat = sb("stat", [TIL, 64], F32)
        stat2 = sb("stat2", [TIL, 64], F32)
        statm = sb("statm", [TIL, 64], F32)
        stati = sb("stati", [TIL, 64], F32)
        a_tile = sb("a_tile", [TIL, 4, D], BF16)
        junk = sb("junk", [TIL, D], F32)
        cst = sb("cst", [TIL, 4], F32)
        junk2s = sb("junk2s", [TIL, D], F32) if spy_out is not None else None
        ynorm = sb("ynorm", [TIL, D], F32)

        A2W = sb("A2W", [TIL, 4, D], F32)
        PTA = sb("PTA", [TIL, 11264], BF16)
        EXT2 = sb("EXT2", [TIL, 4096], BF16)
        w1a_buf = wqk_buf
        w1g_buf = sb("w1g_buf", [TIL, 2, 8, TIL], BF16)
        w2_buf = sb("w2_buf", [TIL, 2, 512], BF16)
        mask_base = sb("mask_base_s", [TIL, 640], BF16)
        maskcur = sb("maskcur", [TIL, 640], BF16)
        qs4 = sb("qs4_s", [TIL, 4, 64], F32)
        gam_cur = sb("gam_cur", [TIL, D], BF16)
        bet_cur = sb("bet_cur", [TIL, D], BF16)
        b1a_col = sb("b1a_col", [TIL, 22], F32)
        b1g_col = sb("b1g_col", [TIL, 22], F32)
        fpart = sb("fpart_s", [TIL, NT], F32)
        U_buf = wqk_buf[:, 0, :, :]
        V_buf = v_blk[:, 0, :]
        gw_buf = wqk_buf[:, 1, :, 0:8]

        v_roll = v_blk[:].rearrange("p a b -> p (a b)").rearrange(
            "p (a b) -> p a b", b=512)
        kT_roll = ktok_blk[:].rearrange("p a b -> p (a b)").rearrange(
            "p (a b) -> p a b", b=1024)
        qT2 = a_tile[:].rearrange("p a b -> p (a b)")[:, 0:2048].rearrange(
            "p (a b) -> p a b", b=512)
        qfb = a_tile[:].rearrange("p a b -> p (a b)")[:, 2048:4096].rearrange(
            "p (a b) -> p a b", b=512)
        kfb = xhT[:].rearrange("p a b c -> p (a b c)")[:, 0:2048].rearrange(
            "p (a b) -> p a b", b=512)
        _xhT_flat = xhT[:].rearrange("p a b c -> p (a b c)")
        junk2v = _xhT_flat[:, 2048:6144].bitcast(F32).rearrange(
            "p (a b) -> p a b", b=512)
        junk3v = PTA[:, 0:4096].bitcast(F32).rearrange(
            "p (a b) -> p a b", b=512)
        tb_raw = PTA[:, 4096:6144].bitcast(F32).rearrange(
            "p (a b) -> p a b", b=256)
        tb_f = PTA[:, 6144:8192].bitcast(F32).rearrange(
            "p (a b) -> p a b", b=256)
        Sm = PTA[:, 8192:9472].bitcast(F32)
        qraw = qT_blk[:].rearrange("p a b -> p (a b)").bitcast(F32).rearrange(
            "p (a b) -> p a b", b=512)
        kraw = kT_blk[:].rearrange("p a b -> p (a b)").bitcast(F32).rearrange(
            "p (a b) -> p a b", b=512)
        hT_ring = EXT2[:, 0:2064].rearrange("p (a b) -> p a b", b=1032)
        PT5 = EXT2[:, 2064:2704].rearrange("p (a b) -> p a b", b=TIL)
        ob2 = EXT2[:, 2704:3216]
        oT2 = EXT2[:, 3216:3728].rearrange("p (a b) -> p a b", b=TIL)
        pT_g = PTA[:, 0:11264].rearrange("p (a b) -> p a b", b=512)
        hTf = _xhT_flat[:, 0:4096].rearrange("p (a b) -> p a b", b=512)
        wffn_v = wv_res[:].rearrange("p a b -> p (a b)")
        wffn2 = woT_res[:].rearrange("p a b -> p (a b)")
        PS_MM = ps("PS_MM", [TIL, 2, 512], F32)
        PS_S = ps("PS_S", [TIL, 1024], F32)
        PS_O = ps("PS_O", [TIL, 2, 512], F32)
        PS_ST = ps("PS_ST", [TIL, 512], F32)
        PS_T = ps("PS_T", [TIL, 1024], BF16)

        t_c125 = k.op("vector", lambda e: e.memset(cst[:, 0:1], 0.125),
                      writes=["cst"])
        k.op("vector", lambda e: e.memset(hT_ring[:, 0, :], 0.0),
             writes=[("hT", 0)])
        k.op("vector", lambda e: e.memset(hT_ring[:, 1, :], 0.0),
             writes=[("hT", 1)])
        k.dma("sync", "cst", fpart[:], fpart_d, writes=["fpart"])
        if spy_out is not None:
            k.op("vector", lambda e: e.memset(junk2s[:], 0.0), writes=["junk2s"])
        k.dma("sync", "cst", causalT[:], causalT_d, writes=["causalT"])
        k.dma("gpsimd", "cstg", identb[:], ident_d, writes=["identb"])
        k.dma("gpsimd", "cstg", frow_b, frow_d, writes=["SMBf"])
        k.dma("gpsimd", "cstg", biasq_b, b_qkv_d, writes=["SMBq"])
        k.dma("gpsimd", "cstg", biaso_b, b_out_d, writes=["SMBo"])
        for a in range(8):
            k.dma("gpsimd", "wres%d" % (a % 2), wv_res[:, a, :],
                  w_qkvT_d[a * TIL:(a + 1) * TIL, 2 * D:3 * D], writes=["wv"])
            k.dma("gpsimd", "wres%d" % (a % 2), woT_res[:, a, :],
                  w_outT_d[a * TIL:(a + 1) * TIL, :], writes=["woT"])

        for l in range(2 if n_stages >= 3 else 0):
            for a in range(8):
                for half in range(6):
                    wslc = slice(half * 910, (half + 1) * 910)
                    rr = (a * 6 + half) % 2
                    buf = w1g_buf[:, rr, :, :] \
                        .rearrange("p a b -> p (a b)")[:, 0:910]
                    k.dma("gpsimd", "stg%d" % rr, buf,
                          w_ffn_w1_d[l, a * TIL:(a + 1) * TIL, wslc],
                          writes=[("w1g", rr)])
                    k.dma("sync", "stgo%d" % rr,
                          w1_st[l, a * TIL:(a + 1) * TIL, wslc], buf,
                          reads=[("w1g", rr)])
            for a in range(22):
                wdt = 42 if a == 21 else TIL
                buf = w1g_buf[:, a % 2, :, :] \
                    .rearrange("p a b -> p (a b)")[0:wdt, 0:1024]
                k.dma("gpsimd", "stg%d" % (a % 2), buf,
                      w_ffn_w2_d[l, a * TIL:a * TIL + wdt, :],
                      writes=[("w1g", a % 2)])
                k.dma("sync", "stgo%d" % (a % 2),
                      w2_st[l, a * TIL:a * TIL + wdt, :], buf,
                      reads=[("w1g", a % 2)])

        def ln_stats(srcs, n_feat=D):
            n = len(srcs)
            for ii, (ap, rk) in enumerate(srcs):
                k.op("scalar", lambda e, ap=ap, ii=ii: e.activation(
                    junk[:], ap, AF.Square, bias=0.0, scale=1.0,
                    accum_out=stat[:, ii:ii + 1]),
                    reads=list(rk), writes=["junk", "stat"])
                k.op("vector", lambda e, ap=ap, ii=ii: e.reduce_sum(
                    stat2[:, ii:ii + 1].unsqueeze(2), ap.unsqueeze(1), axis=AX.X),
                    reads=list(rk), writes=["stat2"])
            sl = slice(0, n)
            k.op("vector", lambda e, sl=sl: e.tensor_scalar(
                statm[:, sl], stat2[:, sl], 1.0 / n_feat, 0.0,
                op0=ALU.mult, op1=ALU.add), reads=["stat2"], writes=["statm"])
            k.op("vector", lambda e, sl=sl: e.tensor_scalar(
                stat[:, sl], stat[:, sl], 1.0 / n_feat, 0.0,
                op0=ALU.mult, op1=ALU.add), reads=["stat"], writes=["stat"])
            k.op("vector", lambda e, sl=sl: e.tensor_tensor(
                stat2[:, sl], statm[:, sl], statm[:, sl], op=ALU.mult),
                reads=["statm"], writes=["stat2"])
            k.op("vector", lambda e, sl=sl: e.tensor_tensor(
                stat[:, sl], stat[:, sl], stat2[:, sl], op=ALU.subtract),
                reads=["stat", "stat2"], writes=["stat"])
            k.op("vector", lambda e, sl=sl: e.tensor_scalar(
                stat[:, sl], stat[:, sl], EPS, 0.0, op0=ALU.add, op1=ALU.add),
                reads=["stat"], writes=["stat"])
            k.op("scalar", lambda e, sl=sl: e.activation(
                stat2[:, sl], stat[:, sl], AF.Sqrt, bias=0.0, scale=1.0),
                reads=["stat"], writes=["stat2"])
            return k.op("vector", lambda e, sl=sl: e.reciprocal(
                stati[:, sl], stat2[:, sl]), reads=["stat2"], writes=["stati"])

        def transpose_128(src_ap_fn, nchunk, dst_fn, src_reads, dst_writes):
            for c in range(nchunk):
                k.op("tensor", lambda e, c=c: e.transpose(
                    PS_T[:, c * TIL:(c + 1) * TIL], src_ap_fn(c), identb[:]),
                    reads=list(src_reads) + ["identb"], writes=["PST"])
            t = None
            nh = (nchunk + 3) // 4
            for h in range(nh):
                c0, c1 = 4 * h, min(4 * h + 4, nchunk)
                t = k.op("vector", lambda e, h=h, c0=c0, c1=c1: e.tensor_copy(
                    dst_fn(h), PS_T[:, c0 * TIL:c1 * TIL]
                    .rearrange("p (a b) -> p a b", b=TIL)),
                    reads=["PST"], writes=list(dst_writes))
            return t

        _os = os
        _nb = int(_os.environ.get("KBLOCKS", "99"))
        _ph = int(_os.environ.get("KPHASE", "999"))
        blocks = chunks4(0, NT)[:_nb]
        t_x1_st = [None] * NT
        k.op("vector", lambda e: e.memset(state_acc[:], 0.0), writes=["stacc"])
        k.op("vector", lambda e: e.memset(state_sb[:], 0.0), writes=["stsb"])

        for bi, tiles in enumerate(blocks):
            nt_b = len(tiles)
            bcols = nt_b * TIL
            ring = bi % 2
            lo = tiles[0] * TIL

            for j in tiles:
                k.dma("sync", "xw%d" % (j % 2), XW[:, j % XWN, :],
                      xin[j * TIL:(j + 1) * TIL, :], writes=[("XW", j % XWN)])

            ln_stats([(XW[:, j % XWN, :], [("XW", j % XWN)]) for j in tiles])
            for ii, j in enumerate(tiles):
                k.op("vector", lambda e, j=j, ii=ii: e.tensor_scalar(
                    a_tile[:, ii, :], XW[:, j % XWN, :], statm[:, ii:ii + 1],
                    stati[:, ii:ii + 1], op0=ALU.subtract, op1=ALU.mult),
                    reads=[("XW", j % XWN), "statm", "stati"], writes=[("a", ii)])
                transpose_128(
                    lambda c, ii=ii: a_tile[:, ii, c * TIL:(c + 1) * TIL], 8,
                    lambda h, ii=ii, ring=ring: xhT[:, ring, 4 * h:4 * h + 4,
                                                    ii * TIL:(ii + 1) * TIL],
                    [("a", ii)], [("xhT", ring)])

            PSQ1 = [PS_MM[:, 0, :], PS_MM[:, 1, :],
                    PS_S[:, 0:512], PS_S[:, 512:1024]]
            PSQ1K = [("PSMM", 0), ("PSMM", 1), ("PSS", 0), ("PSS", 1)]
            for ni in range(16 if _ph >= 2 else 0):
                wring = ni % 2
                k.dma("gpsimd", "wqk%d" % wring, wqk_buf[:, wring, :, :],
                      w_qkvT_d[:, ni * TIL:(ni + 1) * TIL]
                      .rearrange("(a p) b -> p a b", p=TIL),
                      writes=[("wqk", wring)])
                psr = ni % 4
                for d in range(8):
                    k.op("tensor", lambda e, d=d, wring=wring, psr=psr,
                         bcols=bcols, ring=ring: e.matmul(
                        PSQ1[psr][:, 0:bcols], wqk_buf[:, wring, d, :],
                        xhT[:, ring, d, 0:bcols], start=(d == 0), stop=False),
                        reads=[("wqk", wring), ("xhT", ring)],
                        writes=[PSQ1K[psr]])
                k.op("tensor", lambda e, ni=ni, psr=psr, bcols=bcols, lo=lo:
                     e.matmul(
                    PSQ1[psr][:, 0:bcols], biasq_b[:, ni * TIL:(ni + 1) * TIL],
                    frow_b[:, lo:lo + bcols], start=False, stop=True),
                    reads=["SMBq", "SMBf"], writes=[PSQ1K[psr]])
                if ni < 8:
                    k.op("scalar", lambda e, ni=ni, psr=psr, bcols=bcols:
                         e.activation(
                        qT_blk[:, ni, 0:bcols], PSQ1[psr][:, 0:bcols],
                        AF.Identity, bias=0.0, scale=cst[:, 0:1]),
                        reads=[PSQ1K[psr], "cst"], writes=["qT"])
                else:
                    k.op("vector", lambda e, ni=ni, psr=psr, bcols=bcols:
                         e.tensor_copy(
                        kT_blk[:, ni - 8, 0:bcols], PSQ1[psr][:, 0:bcols]),
                        reads=[PSQ1K[psr]], writes=["kT"])

            for ii, j in enumerate(tiles if _ph >= 3 else []):
                for nh in range(2):
                    psr = (ii * 2 + nh) % 4
                    for d in range(8):
                        k.op("tensor", lambda e, d=d, ii=ii, nh=nh, psr=psr,
                             ring=ring: e.matmul(
                            PSQ1[psr],
                            xhT[:, ring, d, ii * TIL:(ii + 1) * TIL],
                            wv_res[:, d, nh * 512:(nh + 1) * 512],
                            start=(d == 0), stop=False),
                            reads=[("xhT", ring), "wv"], writes=[PSQ1K[psr]])
                    k.op("tensor", lambda e, nh=nh, psr=psr, j=j: e.matmul(
                        PSQ1[psr], frow_b[:, j * TIL:(j + 1) * TIL],
                        biasq_b[:, 2 * D + nh * 512:2 * D + (nh + 1) * 512],
                        start=False, stop=True),
                        reads=["SMBq", "SMBf"], writes=[PSQ1K[psr]])
                    if (ii * 2 + nh) % 2 == 0:
                        k.op("vector", lambda e, ii=ii, nh=nh, psr=psr:
                             e.tensor_copy(
                            v_blk[:, ii, nh * 512:(nh + 1) * 512], PSQ1[psr]),
                            reads=[PSQ1K[psr]], writes=["v"])
                    else:
                        k.op("scalar", lambda e, ii=ii, nh=nh, psr=psr:
                             e.activation(
                            v_blk[:, ii, nh * 512:(nh + 1) * 512], PSQ1[psr],
                            AF.Identity, bias=0.0, scale=1.0),
                            reads=[PSQ1K[psr]], writes=["v"])

            for ii, j in enumerate(tiles if _ph >= 4 else []):
                transpose_128(
                    lambda c, ii=ii: kT_blk[:, c, ii * TIL:(ii + 1) * TIL], 8,
                    lambda h, ii=ii: ktok_blk[:, ii, 4 * h * TIL:(4 * h + 4) * TIL]
                    .rearrange("p (a b) -> p a b", b=TIL),
                    ["kT"], ["ktok"])

            for ii, j in enumerate(tiles if _ph >= 5 else []):
                for g in range(4):
                    psr = g % 2
                    for hh in range(4):
                        h = g * 4 + hh
                        p, b = h // 2, (h % 2) * 64
                        k.op("tensor", lambda e, p=p, b=b, hh=hh, psr=psr,
                             ii=ii: e.matmul(
                            PS_S[:, psr * 512 + hh * TIL:psr * 512 + (hh + 1) * TIL],
                            kT_blk[b:b + 64, p, ii * TIL:(ii + 1) * TIL],
                            qT_blk[b:b + 64, p, ii * TIL:(ii + 1) * TIL],
                            start=True, stop=True),
                            reads=["kT", "qT"], writes=[("PSS", psr)])
                    k.op("vector", lambda e, psr=psr: e.tensor_tensor(
                        sT_m[:], PS_S[:, psr * 512:psr * 512 + 512]
                        .rearrange("p (a b) -> p a b", b=TIL),
                        causalT[:].unsqueeze(1).broadcast_to([TIL, 4, TIL]),
                        op=ALU.mult),
                        reads=[("PSS", psr), "causalT"], writes=["sT"])
                    for hh in range(4):
                        h = g * 4 + hh
                        p, b = h // 2, (h % 2) * 64
                        bank, col = h // 8, (h % 8) * 64
                        k.op("tensor", lambda e, h=h, hh=hh, bank=bank, col=col,
                             ii=ii: e.matmul(
                            PS_O[:, bank, col:col + 64], sT_m[:, hh, :],
                            v_blk[:, ii, h * 64:(h + 1) * 64],
                            start=True, stop=False),
                            reads=["sT", "v", "stsb"], writes=[("PSO", bank)])
                        k.op("tensor", lambda e, h=h, p=p, b=b, bank=bank,
                             col=col, ii=ii: e.matmul(
                            PS_O[:, bank, col:col + 64],
                            qT_blk[b:b + 64, p, ii * TIL:(ii + 1) * TIL],
                            state_sb[b:b + 64, p * 64:(p + 1) * 64],
                            start=False, stop=True),
                            reads=["qT", "stsb"], writes=[("PSO", bank)])

                for h in range(16):
                    p, b = h // 2, (h % 2) * 64
                    k.op("tensor", lambda e, h=h, p=p, b=b, ii=ii: e.matmul(
                        PS_ST[b:b + 64, p * 64:(p + 1) * 64],
                        ktok_blk[:, ii, h * 64:(h + 1) * 64],
                        v_blk[:, ii, h * 64:(h + 1) * 64],
                        start=True, stop=True, tile_position=(0, b)),
                        reads=["ktok", "v"], writes=["PSST"])
                k.op("vector", lambda e: e.tensor_tensor(
                    state_acc[:], state_acc[:], PS_ST[:], op=ALU.add),
                    reads=["PSST", "stacc"], writes=["stacc"])
                k.op("vector", lambda e: e.tensor_copy(state_sb[:], state_acc[:]),
                    reads=["stacc"], writes=["stsb"])

                for bank in range(2):
                    k.op("scalar", lambda e, bank=bank: e.activation(
                        o_tok[:, bank * 512:(bank + 1) * 512], PS_O[:, bank, :],
                        AF.Identity, bias=0.0, scale=1.0),
                        reads=[("PSO", bank)], writes=["otok"])
                transpose_128(
                    lambda c: o_tok[:, c * TIL:(c + 1) * TIL], 8,
                    lambda h: oT_t[:, 4 * h:4 * h + 4, :], ["otok"], ["oT"])

                for nh in range(2 if _ph >= 6 else 0):
                    psr = nh
                    for c in range(8):
                        k.op("tensor", lambda e, c=c, nh=nh, psr=psr: e.matmul(
                            PS_MM[:, psr, :], oT_t[:, c, :],
                            woT_res[:, c, nh * 512:(nh + 1) * 512],
                            start=(c == 0), stop=False),
                            reads=["oT", "woT"], writes=[("PSMM", psr)])
                    k.op("tensor", lambda e, nh=nh, psr=psr, j=j: e.matmul(
                        PS_MM[:, psr, :], frow_b[:, j * TIL:(j + 1) * TIL],
                        biaso_b[:, nh * 512:(nh + 1) * 512],
                        start=False, stop=True),
                        reads=["SMBf", "SMBo"], writes=[("PSMM", psr)])

                if _ph < 7:
                    continue
                k.op("scalar", lambda e: e.activation(
                    junk[:, 0:512], PS_MM[:, 0, :], AF.Square, bias=0.0,
                    scale=1.0, accum_out=stat[:, 8:9]),
                    reads=[("PSMM", 0)], writes=["junk", "stat"])
                k.op("scalar", lambda e: e.activation(
                    junk[:, 512:1024], PS_MM[:, 1, :], AF.Square, bias=0.0,
                    scale=1.0, accum_out=stat[:, 9:10]),
                    reads=[("PSMM", 1)], writes=["junk", "stat"])
                if _ph < 8:
                    continue
                k.op("vector", lambda e: e.reduce_sum(
                    stat2[:, 8:9].unsqueeze(2), PS_MM[:, 0, :].unsqueeze(1),
                    axis=AX.X), reads=[("PSMM", 0)], writes=["stat2"])
                k.op("vector", lambda e: e.reduce_sum(
                    stat2[:, 9:10].unsqueeze(2), PS_MM[:, 1, :].unsqueeze(1),
                    axis=AX.X), reads=[("PSMM", 1)], writes=["stat2"])
                if _ph < 82:
                    continue
                k.op("vector", lambda e: e.tensor_tensor(
                    statm[:, 8:9], stat2[:, 8:9], stat2[:, 9:10], op=ALU.add),
                    reads=["stat2"], writes=["statm"])
                k.op("vector", lambda e: e.tensor_scalar(
                    statm[:, 8:9], statm[:, 8:9], 1.0 / D, 0.0,
                    op0=ALU.mult, op1=ALU.add), reads=["statm"], writes=["statm"])
                k.op("vector", lambda e: e.tensor_tensor(
                    stat[:, 8:9], stat[:, 8:9], stat[:, 9:10], op=ALU.add),
                    reads=["stat"], writes=["stat"])
                k.op("vector", lambda e: e.tensor_scalar(
                    stat[:, 8:9], stat[:, 8:9], 1.0 / D, 0.0,
                    op0=ALU.mult, op1=ALU.add), reads=["stat"], writes=["stat"])
                k.op("vector", lambda e: e.tensor_tensor(
                    stat2[:, 10:11], statm[:, 8:9], statm[:, 8:9], op=ALU.mult),
                    reads=["statm"], writes=["stat2"])
                k.op("vector", lambda e: e.tensor_tensor(
                    stat[:, 8:9], stat[:, 8:9], stat2[:, 10:11],
                    op=ALU.subtract), reads=["stat", "stat2"], writes=["stat"])
                k.op("vector", lambda e: e.tensor_scalar(
                    stat[:, 8:9], stat[:, 8:9], EPS, 0.0, op0=ALU.add,
                    op1=ALU.add), reads=["stat"], writes=["stat"])
                if _ph < 84:
                    continue
                k.op("scalar", lambda e: e.activation(
                    stat2[:, 8:9], stat[:, 8:9], AF.Sqrt, bias=0.0, scale=1.0),
                    reads=["stat"], writes=["stat2"])
                k.op("vector", lambda e: e.reciprocal(
                    stati[:, 8:9], stat2[:, 8:9]), reads=["stat2"],
                    writes=["stati"])
                if _ph < 90:
                    continue
                k.op("vector", lambda e: e.tensor_scalar(
                    ynorm[:, 0:512], PS_MM[:, 0, :], statm[:, 8:9],
                    stati[:, 8:9], op0=ALU.subtract, op1=ALU.mult),
                    reads=[("PSMM", 0), "statm", "stati"], writes=["ynorm"])
                k.op("vector", lambda e: e.tensor_scalar(
                    ynorm[:, 512:1024], PS_MM[:, 1, :], statm[:, 8:9],
                    stati[:, 8:9], op0=ALU.subtract, op1=ALU.mult),
                    reads=[("PSMM", 1), "statm", "stati"], writes=["ynorm"])
                k.op("vector", lambda e, j=j: e.tensor_tensor(
                    XW[:, j % XWN, :], XW[:, j % XWN, :], ynorm[:], op=ALU.add),
                    reads=["ynorm", ("XW", j % XWN)], writes=[("XW", j % XWN)])
                if _ph < 100:
                    continue
                t_x1_st[j] = k.dma("gpsimd", "xst", x1_d[j * TIL:(j + 1) * TIL, :],
                                   XW[:, j % XWN, :], reads=[("XW", j % XWN)],
                                   writes=[("x1", j)])

        def barrier():
            toks = [k.last(s) for s in list(k.cnt)]
            for e in ENGINES:
                for t in toks:
                    k.wait(e, t)

        def wffn_rhs(d, part):
            qq = d * 3 + part
            if qq < 16:
                return wffn_v[:, qq * 512:(qq + 1) * 512]
            return wffn2[:, (qq - 16) * 512:(qq - 16 + 1) * 512]

        def wffn_keys():
            return ["wv", "woT"]

        def wo_rhs(c, nh):
            off = 4096 + c * 1024 + nh * 512
            return wffn2[:, off:off + 512]

        Pm = PTA[:, 9472:10112]

        def prep_tile(ind, in_done, j, do_ln):
            if do_ln:
                k.dma("sync", "xw%d" % (j % 2), XW[:, j % XWN, :],
                      ind[j * TIL:(j + 1) * TIL, :],
                      waits=[in_done[j]] if in_done else (),
                      writes=[("XW", j % XWN)])
                ln_stats([(XW[:, j % XWN, :], [("XW", j % XWN)])])
                k.op("vector", lambda e, j=j: e.tensor_scalar(
                    A2W[:, j % 4, :], XW[:, j % XWN, :], statm[:, 0:1],
                    stati[:, 0:1], op0=ALU.subtract, op1=ALU.mult),
                    reads=[("XW", j % XWN), "statm", "stati"],
                    writes=[("A2W", j % 4)])
            else:
                k.dma("sync", "xw%d" % (j % 2), A2W[:, j % 4, :],
                      ind[j * TIL:(j + 1) * TIL, :],
                      waits=[in_done[j]] if in_done else (),
                      writes=[("A2W", j % 4)])
            k.op("scalar", lambda e, j=j: e.activation(
                o_tok[:], A2W[:, j % 4, :], AF.Identity, bias=0.0, scale=1.0),
                reads=[("A2W", j % 4)], writes=["otok"])
            for c in range(8):
                k.op("tensor", lambda e, c=c: e.transpose(
                    PS_T[:, c * TIL:(c + 1) * TIL],
                    o_tok[:, c * TIL:(c + 1) * TIL], identb[:]),
                    reads=["otok", "identb"], writes=["PST"])
            hr = hT_ring[:, j % 2, :].rearrange("p (a b) -> p a b", b=129)
            hprev = hT_ring[:, (j + 1) % 2, :].rearrange("p (a b) -> p a b", b=129)
            for h in range(2):
                k.op("vector", lambda e, h=h, hr=hr: e.tensor_copy(
                    hr[:, 4 * h:4 * h + 4, 1:129],
                    PS_T[:, 4 * h * TIL:(4 * h + 4) * TIL]
                    .rearrange("p (a b) -> p a b", b=TIL)),
                    reads=["PST"], writes=[("hT", j % 2)])
            k.op("vector", lambda e, hr=hr, hprev=hprev: e.tensor_copy(
                hr[:, :, 0:1], hprev[:, :, 128:129]),
                reads=[("hT", (j + 1) % 2)], writes=[("hT", j % 2)])

        def attn_tile(l, j, ii, out_done):
            key_tiles = list(range(j - 4, j + 1))
            segs = []
            poff = 0
            run_start = key_tiles[0] % 8
            run_len = 1
            for t in key_tiles[1:]:
                if t % 8 == (run_start + run_len) % 8 and run_start + run_len < 8:
                    run_len += 1
                else:
                    segs.append((poff, run_start * TIL, run_len * TIL))
                    poff += run_len * TIL
                    run_start = t % 8
                    run_len = 1
            segs.append((poff, run_start * TIL, run_len * TIL))
            sub = []
            for (po, ro, w) in segs:
                while w > 0:
                    lim = 512 - (po % 512)
                    ww = min(w, lim, 512)
                    sub.append((po, ro, ww))
                    po, ro, w = po + ww, ro + ww, w - ww
            if 17 <= j <= 20:
                k.dma("gpsimd", "cstg", maskcur[:],
                      mask_sp_d[j - 17], writes=["msp"])
            mk = maskcur[:] if 17 <= j <= 20 else mask_base[:]
            mkk = "msp" if 17 <= j <= 20 else "mb"
            PSMMf = PS_MM[:].rearrange("p a b -> p (a b)")
            slots = [PS_S, PSMMf]
            skeys = [[("PSS", 0), ("PSS", 1)], [("PSMM", 0), ("PSMM", 1)]]

            def s_mms(h):
                sl_ = slots[h % 2]
                for (po, ro, w) in sub:
                    p, bb = h // 2, (h % 2 == 1) * 0 + (h % 2) * 0
                    p = h // 2
                    bb = (h % 2) * 64
                    k.op("tensor", lambda e, p=p, bb=bb, po=po, ro=ro, w=w,
                         ii=ii, sl_=sl_: e.matmul(
                        sl_[:, po:po + w],
                        qT2[bb:bb + 64, ii, p * TIL:(p + 1) * TIL],
                        kT_roll[bb:bb + 64, p, ro:ro + w],
                        start=True, stop=True),
                        reads=["qT2"] + [("ktr", s % 8) for s in key_tiles],
                        writes=skeys[h % 2])

            def v_chain(h):
                sl_ = slots[h % 2]
                k.op("vector", lambda e, mk=mk, sl_=sl_: e.tensor_tensor(
                    Sm[:], sl_[:, 0:640], mk, op=ALU.add),
                    reads=skeys[h % 2] + [mkk], writes=["Sm"])
                k.op("vector", lambda e: e.reduce_max(
                    stat2[:, 40:41], Sm[:], axis=AX.X, negate=True),
                    reads=["Sm"], writes=["stat2"])
                k.op("scalar", lambda e: e.activation(
                    Pm[:], Sm[:], AF.Exp, bias=stat2[:, 40:41], scale=1.0,
                    accum_out=stat[:, 40:41]),
                    reads=["Sm", "stat2"], writes=["Pm", "stat"])
                k.op("vector", lambda e, h=h: e.reciprocal(
                    stati[:, 40 + h:41 + h], stat[:, 40:41]),
                    reads=["stat"], writes=["stati"])

            def pv_chain(h):
                for c in range(5):
                    k.op("tensor", lambda e, c=c: e.transpose(
                        PS_T[:, c * TIL:(c + 1) * TIL],
                        Pm[:, c * TIL:(c + 1) * TIL], identb[:]),
                        reads=["Pm", "identb"], writes=["PST"])
                k.op("vector", lambda e: e.tensor_copy(
                    PT5[:], PS_T[:, 0:640].rearrange("p (a b) -> p a b", b=TIL)),
                    reads=["PST"], writes=["PT5"])
                for t5, jt in enumerate(key_tiles):
                    k.op("tensor", lambda e, t5=t5, jt=jt, h=h, j=j: e.matmul(
                        PS_O[:, j % 2, h * 64:(h + 1) * 64], PT5[:, t5, :],
                        v_roll[:, jt % 8, h * 64:(h + 1) * 64],
                        start=(t5 == 0), stop=(t5 == 4)),
                        reads=["PT5", ("vr", jt % 8)], writes=[("PSO", j % 2)])

            for h in range(9):
                if h < 8:
                    s_mms(h)
                if h >= 1:
                    v_chain(h - 1)
                    pv_chain(h - 1)
            for h in range(8):
                k.op("scalar", lambda e, h=h, j=j: e.activation(
                    ob2[:, h * 64:(h + 1) * 64],
                    PS_O[:, j % 2, h * 64:(h + 1) * 64],
                    AF.Identity, bias=0.0, scale=stati[:, 40 + h:41 + h]),
                    reads=[("PSO", j % 2), "stati"], writes=["ob2"])
            for c in range(4):
                k.op("tensor", lambda e, c=c: e.transpose(
                    PS_T[:, c * TIL:(c + 1) * TIL],
                    ob2[:, c * TIL:(c + 1) * TIL], identb[:]),
                    reads=["ob2", "identb"], writes=["PST"])
            k.op("vector", lambda e: e.tensor_copy(
                oT2[:], PS_T[:, 0:512].rearrange("p (a b) -> p a b", b=TIL)),
                reads=["PST"], writes=["oT2"])
            for nh in range(2):
                for c in range(4):
                    k.op("tensor", lambda e, c=c, nh=nh: e.matmul(
                        PS_MM[:, nh, :], oT2[:, c, :], wo_rhs(c, nh),
                        start=(c == 0), stop=(c == 3)),
                        reads=["oT2", "woT"], writes=[("PSMM", nh)])
                k.op("vector", lambda e, nh=nh, j=j: e.scalar_tensor_tensor(
                    ynorm[:, nh * 512:(nh + 1) * 512], PS_MM[:, nh, :],
                    fpart[:, j:j + 1], A2W[:, j % 4, nh * 512:(nh + 1) * 512],
                    op0=ALU.mult, op1=ALU.add),
                    reads=[("PSMM", nh), "fpart", ("A2W", j % 4)],
                    writes=["ynorm"])
            out_done[j] = k.dma("gpsimd", "xst", y_d[j * TIL:(j + 1) * TIL, :],
                                ynorm[:], reads=["ynorm"], writes=[("y", j)])

        def emit_l0ffn_attn(l, in_stream, in_done, kv_lo, q_lo, do_ln):
            ind = streams[in_stream]
            y_done = [None] * NT
            for qq in range(24):
                dst = wffn_rhs(qq // 3, qq % 3)
                d, part = qq // 3, qq % 3
                k.dma("gpsimd", "wres%d" % (qq % 2), dst,
                      w_ffn_qkv_d[l, d * TIL:(d + 1) * TIL,
                                  part * 512:(part + 1) * 512],
                      writes=wffn_keys())
            for c in range(4):
                for nh in range(2):
                    k.dma("gpsimd", "wres%d" % ((c * 2 + nh) % 2),
                          wo_rhs(c, nh),
                          w_ffn_wo_d[l, c * TIL:(c + 1) * TIL,
                                     nh * 512:(nh + 1) * 512], writes=["woT"])
            k.dma("sync", "cst", qs4[:], qs4_d[l], writes=["qs4"])
            k.dma("gpsimd", "cstg", mask_base[:], mask_base_d, writes=["mb"])


            prep_tile(ind, in_done, kv_lo - 1, do_ln)
            for tiles in chunks4(kv_lo, NT):
                nt_b = len(tiles)
                i0t = tiles[0]
                k.dma("sync", "tab", tb_raw[:, 0:nt_b, :],
                      tab4_d[i0t:i0t + nt_b].rearrange("a p b -> p a b"),
                      writes=["tbr"])
                for t4 in range(4):
                    rollix = t4 % 2
                    qk = 0 if t4 < 2 else 2
                    k.op("vector", lambda e, t4=t4, rollix=rollix, qk=qk,
                         nt_b=nt_b: e.tensor_tensor(
                        tb_f[:, 0:nt_b, t4 * 64:(t4 + 1) * 64],
                        tb_raw[:, 0:nt_b, t4 * 64:(t4 + 1) * 64],
                        qs4[:, qk + rollix, :].unsqueeze(1)
                        .broadcast_to([TIL, nt_b, 64]), op=ALU.mult),
                        reads=["tbr", "qs4"], writes=["tbf"])
                PSQ = [PS_MM[:, 0, :], PS_MM[:, 1, :],
                       PS_S[:, 0:512], PS_S[:, 512:1024]]
                PSQK = [("PSMM", 0), ("PSMM", 1), ("PSS", 0), ("PSS", 1)]
                for ii, j in enumerate(tiles):
                    prep_tile(ind, in_done, j, do_ln)
                    hr = hT_ring[:, j % 2, :].rearrange("p (a b) -> p a b", b=129)
                    for part in range(3):
                        psr = (ii * 3 + part) % 4
                        for d in range(8):
                            sh = 1 if d < 4 else 0
                            k.op("tensor", lambda e, d=d, part=part, psr=psr,
                                 sh=sh, hr=hr: e.matmul(
                                PSQ[psr], hr[:, d, sh:sh + TIL],
                                wffn_rhs(d, part), start=(d == 0),
                                stop=(d == 7)),
                                reads=[("hT", j % 2)] + wffn_keys(),
                                writes=[PSQK[psr]])
                        if part == 0:
                            k.op("vector", lambda e, ii=ii, psr=psr:
                                 e.tensor_copy(qraw[:, ii, :], PSQ[psr]),
                                 reads=[PSQK[psr]], writes=["qraw"])
                        elif part == 1:
                            k.op("scalar", lambda e, ii=ii, psr=psr:
                                 e.activation(kraw[:, ii, :], PSQ[psr],
                                              AF.Identity, bias=0.0,
                                              scale=1.0),
                                 reads=[PSQK[psr]], writes=["kraw"])
                        else:
                            k.op("vector", lambda e, j=j, psr=psr:
                                 e.tensor_copy(v_roll[:, j % 8, :], PSQ[psr]),
                                 reads=[PSQK[psr]], writes=[("vr", j % 8)])

                for side in range(2):
                    raw = qraw if side == 0 else kraw
                    rk = "qraw" if side == 0 else "kraw"
                    fb = qfb if side == 0 else kfb
                    fbk = "qfb" if side == 0 else "kfb"
                    toff = 0 if side == 0 else 128
                    sl2 = slice(0, nt_b * 8)
                    k.op("scalar", lambda e, raw=raw, nt_b=nt_b: e.activation(
                        junk2v[:, 0:nt_b, :], raw[:, 0:nt_b, :], AF.Square,
                        bias=0.0, scale=1.0), reads=[rk], writes=["junk2"])
                    k.op("vector", lambda e, nt_b=nt_b, sl2=sl2: e.reduce_sum(
                        stat[:, sl2].rearrange("p (a b) -> p a b", b=8)
                        .unsqueeze(3),
                        junk2v[:, 0:nt_b, :].rearrange("p a (b c) -> p a b c",
                                                       c=64), axis=AX.X),
                        reads=["junk2"], writes=["stat"])
                    k.op("scalar", lambda e, sl2=sl2: e.activation(
                        stat2[:, sl2], stat[:, sl2], AF.Sqrt, bias=0.0,
                        scale=1.0), reads=["stat"], writes=["stat2"])
                    k.op("vector", lambda e, sl2=sl2: e.tensor_scalar(
                        stat2[:, sl2], stat2[:, sl2], 1e-12, 0.0,
                        op0=ALU.max, op1=ALU.add),
                        reads=["stat2"], writes=["stat2"])
                    k.op("vector", lambda e, sl2=sl2: e.reciprocal(
                        stati[:, sl2], stat2[:, sl2]),
                        reads=["stat2"], writes=["stati"])
                    k.op("vector", lambda e, nt_b=nt_b: e.tensor_copy(
                        junk3v[:, 0:nt_b, :].rearrange("p a (b c) -> p a b c",
                                                       c=64),
                        stati[:, 0:nt_b * 8].rearrange("p (a b) -> p a b", b=8)
                        .unsqueeze(3).broadcast_to([TIL, nt_b, 8, 64])),
                        reads=["stati"], writes=["junk3"])
                    k.op("vector", lambda e, raw=raw, nt_b=nt_b: e.tensor_tensor(
                        raw[:, 0:nt_b, :], raw[:, 0:nt_b, :],
                        junk3v[:, 0:nt_b, :], op=ALU.mult),
                        reads=[rk, "junk3"], writes=[rk])
                    vw = lambda a: a.rearrange("p t (h d) -> p t h d", d=64)
                    k.op("vector", lambda e, raw=raw, nt_b=nt_b, toff=toff,
                         vw=vw: e.tensor_tensor(
                        vw(junk2v[:, 0:nt_b, :])[:, :, :, 0:32],
                        vw(raw[:, 0:nt_b, :])[:, :, :, 32:64],
                        tb_f[:, 0:nt_b, toff + 64:toff + 96].unsqueeze(2)
                        .broadcast_to([TIL, nt_b, 8, 32]), op=ALU.mult),
                        reads=[rk, "tbf"], writes=["junk2"])
                    k.op("vector", lambda e, raw=raw, nt_b=nt_b, toff=toff,
                         vw=vw: e.tensor_tensor(
                        vw(junk2v[:, 0:nt_b, :])[:, :, :, 32:64],
                        vw(raw[:, 0:nt_b, :])[:, :, :, 0:32],
                        tb_f[:, 0:nt_b, toff + 96:toff + 128].unsqueeze(2)
                        .broadcast_to([TIL, nt_b, 8, 32]), op=ALU.mult),
                        reads=[rk, "tbf"], writes=["junk2"])
                    k.op("vector", lambda e, raw=raw, nt_b=nt_b, toff=toff,
                         vw=vw: e.tensor_tensor(
                        vw(junk3v[:, 0:nt_b, :]), vw(raw[:, 0:nt_b, :]),
                        tb_f[:, 0:nt_b, toff + 0:toff + 64].unsqueeze(2)
                        .broadcast_to([TIL, nt_b, 8, 64]), op=ALU.mult),
                        reads=[rk, "tbf"], writes=["junk3"])
                    k.op("vector", lambda e, fb=fb, nt_b=nt_b: e.tensor_tensor(
                        fb[:, 0:nt_b, :], junk2v[:, 0:nt_b, :],
                        junk3v[:, 0:nt_b, :], op=ALU.add),
                        reads=["junk2", "junk3"], writes=[fbk])

                for ii, j in enumerate(tiles):
                    transpose_128(
                        lambda c, ii=ii: qfb[:, ii, c * TIL:(c + 1) * TIL], 4,
                        lambda h, ii=ii: qT2[:, ii, :]
                        .rearrange("p (a b) -> p a b", b=TIL),
                        ["qfb"], ["qT2"])
                    transpose_128(
                        lambda c, ii=ii: kfb[:, ii, c * TIL:(c + 1) * TIL], 4,
                        lambda h, j=j: kT_roll[:, :, (j % 8) * TIL:
                                               (j % 8 + 1) * TIL],
                        ["kfb"], [("ktr", j % 8)])

                for ii, j in enumerate(tiles):
                    if j >= q_lo:
                        attn_tile(l, j, ii, y_done)
            return y_done

        def emit_l0ffn_ffn(l, y_done, out_stream, q_lo):
            outd = streams[out_stream]
            out_done = [None] * NT
            k.dma("gpsimd", "cstg", gam_cur[:], gam_d[l], writes=["gam"])
            k.dma("gpsimd", "cstg", bet_cur[:], bet_d[l], writes=["bet"])
            k.dma("sync", "cst", b1a_col[:], b1a_d[l], writes=["b1a"])
            k.dma("sync", "cst", b1g_col[:], b1g_d[l], writes=["b1g"])
            k.dma("gpsimd", "cstg", biaso_b, b2_d[l], writes=["SMBo"])
            for tiles in chunks4(q_lo, NT):
                nt_b = len(tiles)
                for ii, j in enumerate(tiles):
                    k.dma("sync", "xw%d" % (j % 2), XW[:, j % XWN, :],
                          y_d[j * TIL:(j + 1) * TIL, :],
                          waits=[y_done[j]], writes=[("XW", j % XWN)])
                for ii, j in enumerate(tiles):
                    k.dma("sync", "hsh%d" % (ii % 2), A2W[:, ii, 0:512],
                          XW[:, j % XWN, 0:512], reads=[("XW", j % XWN)],
                          writes=[("A2W", ii)])
                    if j == q_lo:
                        k.dma("sync", "hsh%d" % (ii % 2), A2W[0:1, ii, 512:1024],
                              XW[0:1, j % XWN, 512:1024], reads=[("XW", j % XWN)],
                              writes=[("A2W", ii)])
                    else:
                        k.dma("sync", "hsh%d" % (ii % 2), A2W[0:1, ii, 512:1024],
                              XW[127:128, (j - 1) % XWN, 512:1024],
                              reads=[("XW", (j - 1) % XWN)], writes=[("A2W", ii)])
                    k.dma("sync", "hsh%d" % (ii % 2), A2W[1:128, ii, 512:1024],
                          XW[0:127, j % XWN, 512:1024], reads=[("XW", j % XWN)],
                          writes=[("A2W", ii)])
                ln_stats([(A2W[:, ii, :], [("A2W", ii)])
                          for ii in range(nt_b)])
                for ii, j in enumerate(tiles):
                    k.op("vector", lambda e, ii=ii: e.tensor_scalar(
                        junk[:], A2W[:, ii, :], statm[:, ii:ii + 1],
                        stati[:, ii:ii + 1], op0=ALU.subtract, op1=ALU.mult),
                        reads=[("A2W", ii), "statm", "stati"], writes=["junk"])
                    k.op("vector", lambda e: e.tensor_tensor(
                        junk[:], junk[:], gam_cur[:], op=ALU.mult),
                        reads=["junk", "gam"], writes=["junk"])
                    k.op("vector", lambda e: e.tensor_tensor(
                        o_tok[:], junk[:], bet_cur[:], op=ALU.add),
                        reads=["junk", "bet"], writes=["otok"])
                    for c in range(8):
                        k.op("tensor", lambda e, c=c: e.transpose(
                            PS_T[:, c * TIL:(c + 1) * TIL],
                            o_tok[:, c * TIL:(c + 1) * TIL], identb[:]),
                            reads=["otok", "identb"], writes=["PST"])
                    for h in range(2):
                        k.op("vector", lambda e, h=h, ii=ii: e.tensor_copy(
                            hTf[:, 4 * h:4 * h + 4, ii * TIL:(ii + 1) * TIL],
                            PS_T[:, 4 * h * TIL:(4 * h + 4) * TIL]
                            .rearrange("p (a b) -> p a b", b=TIL)),
                            reads=["PST"],
                            writes=["hTf", "kfb", "junk2"])
                PT_KEYS = ["pT", "junk3", "tbr", "tbf", "Sm", "Pm"]
                bcol = nt_b * TIL
                for cc in range(22):
                    wdt = 42 if cc == 21 else TIL
                    wr = cc % 2
                    k.dma("sync", "w1a%d" % wr, w1a_buf[:, wr, :, 0:wdt],
                          w1_st[l, :, cc * TIL:cc * TIL + wdt]
                          .rearrange("(a p) b -> p a b", p=TIL),
                          writes=[("wqk", wr)])
                    k.dma("sync", "w1g%d" % wr, w1g_buf[:, wr, :, 0:wdt],
                          w1_st[l, :, 2730 + cc * TIL:2730 + cc * TIL + wdt]
                          .rearrange("(a p) b -> p a b", p=TIL),
                          writes=[("w1g", wr)])
                    pa = PS_MM[:, 0, :] if cc % 2 == 0 else PS_MM[:, 1, :]
                    pak = ("PSMM", 0) if cc % 2 == 0 else ("PSMM", 1)
                    pg = PS_S[:, 0:512] if cc % 2 == 0 else PS_S[:, 512:1024]
                    pgk = ("PSS", 0) if cc % 2 == 0 else ("PSS", 1)
                    for d in range(8):
                        k.op("tensor", lambda e, d=d, cc=cc, wr=wr, wdt=wdt,
                             bcol=bcol, pa=pa: e.matmul(
                            pa[0:wdt, 0:bcol],
                            w1a_buf[:, wr, d, 0:wdt], hTf[:, d, 0:bcol],
                            start=(d == 0), stop=(d == 7)),
                            reads=[("wqk", wr), "hTf"], writes=[pak])
                        k.op("tensor", lambda e, d=d, cc=cc, wr=wr, wdt=wdt,
                             bcol=bcol, pg=pg: e.matmul(
                            pg[0:wdt, 0:bcol], w1g_buf[:, wr, d, 0:wdt],
                            hTf[:, d, 0:bcol], start=(d == 0), stop=(d == 7)),
                            reads=[("w1g", wr), "hTf"],
                            writes=[pgk])
                    k.op("scalar", lambda e, cc=cc, wdt=wdt, bcol=bcol, pg=pg:
                         e.activation(
                        junk[0:wdt, 0:bcol], pg[0:wdt, 0:bcol], AF.Gelu,
                        bias=b1g_col[0:wdt, cc:cc + 1], scale=1.0),
                        reads=[pgk, "b1g"], writes=["junk"])
                    k.op("vector", lambda e, cc=cc, wdt=wdt, bcol=bcol, pa=pa:
                         e.scalar_tensor_tensor(
                        pT_g[0:wdt, cc, 0:bcol], pa[0:wdt, 0:bcol],
                        b1a_col[0:wdt, cc:cc + 1], junk[0:wdt, 0:bcol],
                        op0=ALU.add, op1=ALU.mult),
                        reads=[pak, "b1a", "junk"], writes=PT_KEYS)
                PSB = [PS_MM[:, 0, :], PS_MM[:, 1, :],
                       PS_S[:, 0:512], PS_S[:, 512:1024]]
                PSK = [("PSMM", 0), ("PSMM", 1), ("PSS", 0), ("PSS", 1)]
                for half in range(2):
                    for cc in range(22):
                        wdt = 42 if cc == 21 else TIL
                        wr = cc % 2
                        k.dma("sync", "w2%d" % wr, w2_buf[0:wdt, wr, :],
                              w2_st[l, cc * TIL:cc * TIL + wdt,
                                    half * 512:(half + 1) * 512],
                              writes=[("w2", wr)])
                        for ti in range(nt_b):
                            k.op("tensor", lambda e, cc=cc, wr=wr, wdt=wdt,
                                 ti=ti: e.matmul(
                                PSB[ti], pT_g[0:wdt, cc, ti * TIL:(ti + 1) * TIL],
                                w2_buf[0:wdt, wr, :],
                                start=(cc == 0), stop=False),
                                reads=[("w2", wr), "pT"],
                                writes=[PSK[ti]])
                    for ti, j in enumerate(tiles):
                        k.op("tensor", lambda e, ti=ti, j=j, half=half: e.matmul(
                            PSB[ti], frow_b[:, j * TIL:(j + 1) * TIL],
                            biaso_b[:, half * 512:(half + 1) * 512],
                            start=False, stop=True),
                            reads=["SMBf", "SMBo"], writes=[PSK[ti]])
                        k.op("vector", lambda e, ti=ti, j=j, half=half:
                             e.scalar_tensor_tensor(
                            A2W[:, ti, half * 512:(half + 1) * 512], PSB[ti],
                            fpart[:, j:j + 1],
                            XW[:, j % XWN, half * 512:(half + 1) * 512],
                            op0=ALU.mult, op1=ALU.add),
                            reads=[PSK[ti], "fpart", ("XW", j % XWN)],
                            writes=[("A2W", ti)])
                for ti, j in enumerate(tiles):
                    out_done[j] = k.dma(
                        "sync", "xst", outd[j * TIL:(j + 1) * TIL, :],
                        A2W[:, ti, :], reads=[("A2W", ti)],
                        writes=[(out_stream, j)])
            return out_done

        def emit_l1_attn(in_done):
            ind = streams["x2"]
            outd = streams["x3"]
            out_done = [None] * NT
            kT1 = v_blk[:].rearrange("p a b -> p (a b)").rearrange(
                "p (a b) -> p a b", b=512)
            v1 = ktok_blk
            vprev = a_tile[:, 0, :]
            kTprev = a_tile[:, 1, :].rearrange("p (a b) -> p a b", b=TIL)
            for a in range(8):
                k.dma("gpsimd", "wres%d" % (a % 2), wv_res[:, a, :],
                      w_l1v_d[a * TIL:(a + 1) * TIL, :], writes=["wv"])
            k.dma("gpsimd", "cstg", mask_base[:, 0:256], m1_base_d,
                  writes=["mb"])
            k.dma("gpsimd", "cstg", maskcur[:, 0:256], m1_sp_d, writes=["msp"])
            blocks5 = chunks4(16, NT)
            for bi, tiles in enumerate(blocks5):
                nt_b = len(tiles)
                bcols = nt_b * TIL
                ring = bi % 2
                for ii, j in enumerate(tiles):
                    k.dma("sync", "xw%d" % (j % 2), XW[:, j % XWN, :],
                          ind[j * TIL:(j + 1) * TIL, :],
                          waits=[in_done[j]], writes=[("XW", j % XWN)])
                ln_stats([(XW[:, j % XWN, :], [("XW", j % XWN)])
                          for j in tiles])
                for ii, j in enumerate(tiles):
                    k.op("scalar", lambda e, j=j, ii=ii: e.activation(
                        o_tok[:], XW[:, j % XWN, :], AF.Identity,
                        bias=statm[:, ii:ii + 1], scale=stati[:, ii:ii + 1]),
                        reads=[("XW", j % XWN), "statm", "stati"],
                        writes=["otok"]) if False else None
                    k.op("vector", lambda e, j=j, ii=ii: e.tensor_scalar(
                        junk[:], XW[:, j % XWN, :], statm[:, ii:ii + 1],
                        stati[:, ii:ii + 1], op0=ALU.subtract, op1=ALU.mult),
                        reads=[("XW", j % XWN), "statm", "stati"],
                        writes=["junk"])
                    k.op("scalar", lambda e: e.activation(
                        o_tok[:], junk[:], AF.Identity, bias=0.0, scale=1.0),
                        reads=["junk"], writes=["otok"])
                    transpose_128(
                        lambda c: o_tok[:, c * TIL:(c + 1) * TIL], 8,
                        lambda h, ii=ii, ring=ring: xhT[:, ring, 4 * h:4 * h + 4,
                                                        ii * TIL:(ii + 1) * TIL],
                        ["otok"], [("xhT", ring)])
                for ni in range(16):
                    wring = ni % 2
                    wd = w_l1q_d if ni < 8 else w_l1k_d
                    nc_ = ni % 8
                    k.dma("gpsimd", "wqk%d" % wring, wqk_buf[:, wring, :, :],
                          wd[:, nc_ * TIL:(nc_ + 1) * TIL]
                          .rearrange("(a p) b -> p a b", p=TIL),
                          writes=[("wqk", wring)])
                    psr = ni % 2
                    for d in range(8):
                        k.op("tensor", lambda e, d=d, wring=wring, psr=psr,
                             bcols=bcols, ring=ring: e.matmul(
                            PS_MM[:, psr, 0:bcols], wqk_buf[:, wring, d, :],
                            xhT[:, ring, d, 0:bcols], start=(d == 0),
                            stop=(d == 7)),
                            reads=[("wqk", wring), ("xhT", ring)],
                            writes=[("PSMM", psr)])
                    if ni < 8:
                        k.op("scalar", lambda e, ni=ni, psr=psr, bcols=bcols:
                             e.activation(
                            qT_blk[:, ni, 0:bcols], PS_MM[:, psr, 0:bcols],
                            AF.Identity, bias=0.0, scale=cst[:, 0:1]),
                            reads=[("PSMM", psr), "cst"], writes=["qT"])
                    else:
                        for ii, j in enumerate(tiles):
                            k.op("vector", lambda e, ni=ni, psr=psr, ii=ii,
                                 j=j: e.tensor_copy(
                                kT1[:, ni - 8, (j % 4) * TIL:(j % 4 + 1) * TIL],
                                PS_MM[:, psr, ii * TIL:(ii + 1) * TIL]),
                                reads=[("PSMM", psr)], writes=[("k1", j % 4)])
                for ii, j in enumerate(tiles):
                    for nh in range(2):
                        psr = (ii * 2 + nh) % 2
                        for d in range(8):
                            k.op("tensor", lambda e, d=d, ii=ii, nh=nh, psr=psr,
                                 ring=ring: e.matmul(
                                PS_MM[:, psr, :],
                                xhT[:, ring, d, ii * TIL:(ii + 1) * TIL],
                                wv_res[:, d, nh * 512:(nh + 1) * 512],
                                start=(d == 0), stop=(d == 7)),
                                reads=[("xhT", ring), "wv"],
                                writes=[("PSMM", psr)])
                        k.op("vector", lambda e, j=j, nh=nh, psr=psr:
                             e.tensor_copy(
                            v1[:, j % 4, nh * 512:(nh + 1) * 512],
                            PS_MM[:, psr, :]),
                            reads=[("PSMM", psr)], writes=[("v1", j % 4)])
                for ii, j in enumerate(tiles):
                    if j < 17:
                        continue
                    mk = maskcur[:, 0:256] if j == 17 else mask_base[:, 0:256]
                    mkk = "msp" if j == 17 else "mb"
                    prev_in_block = (j - 1) in tiles
                    for h in range(16):
                        p, b = h // 2, (h % 2) * 64
                        for si in range(2):
                            sl = (j - 1) % 4 if si == 0 else j % 4
                            if si == 0 and not prev_in_block:
                                krhs = kTprev[b:b + 64, p, :]
                                krd = "kprev"
                            else:
                                krhs = kT1[b:b + 64, p, sl * TIL:(sl + 1) * TIL]
                                krd = ("k1", sl)
                            k.op("tensor", lambda e, p=p, b=b, si=si,
                                 krhs=krhs, ii=ii: e.matmul(
                                PS_S[:, si * TIL:(si + 1) * TIL],
                                qT_blk[b:b + 64, p, ii * TIL:(ii + 1) * TIL],
                                krhs, start=True, stop=True),
                                reads=["qT", krd],
                                writes=[("PSS", 0)])
                        k.op("vector", lambda e, mk=mk: e.tensor_tensor(
                            Sm[:, 0:256], PS_S[:, 0:256], mk, op=ALU.add),
                            reads=[("PSS", 0), mkk], writes=["Sm"])
                        k.op("vector", lambda e: e.reduce_max(
                            stat2[:, 40:41], Sm[:, 0:256], axis=AX.X,
                            negate=True), reads=["Sm"], writes=["stat2"])
                        k.op("scalar", lambda e: e.activation(
                            Pm[:, 0:256], Sm[:, 0:256], AF.Exp,
                            bias=stat2[:, 40:41], scale=1.0,
                            accum_out=stat[:, 40:41]),
                            reads=["Sm", "stat2"], writes=["Pm", "stat"])
                        k.op("vector", lambda e, h=h: e.reciprocal(
                            stati[:, 40 + h:41 + h], stat[:, 40:41]),
                            reads=["stat"], writes=["stati"])
                        for c in range(2):
                            k.op("tensor", lambda e, c=c: e.transpose(
                                PS_T[:, c * TIL:(c + 1) * TIL],
                                Pm[:, c * TIL:(c + 1) * TIL], identb[:]),
                                reads=["Pm", "identb"], writes=["PST"])
                        k.op("vector", lambda e: e.tensor_copy(
                            PT5[:, 0:2, :], PS_T[:, 0:256]
                            .rearrange("p (a b) -> p a b", b=TIL)),
                            reads=["PST"], writes=["PT5"])
                        bank = h // 8
                        col = (h % 8) * 64
                        for si in range(2):
                            sl = (j - 1) % 4 if si == 0 else j % 4
                            if si == 0 and not prev_in_block:
                                vrhs = vprev[:, h * 64:(h + 1) * 64]
                                vrd = "vprev"
                            else:
                                vrhs = v1[:, sl, h * 64:(h + 1) * 64]
                                vrd = ("v1", sl)
                            k.op("tensor", lambda e, si=si, vrhs=vrhs, h=h,
                                 bank=bank, col=col: e.matmul(
                                PS_O[:, bank, col:col + 64], PT5[:, si, :],
                                vrhs, start=(si == 0), stop=(si == 1)),
                                reads=["PT5", vrd],
                                writes=[("PSO", bank)])
                    for h in range(16):
                        bank, col = h // 8, (h % 8) * 64
                        k.op("scalar", lambda e, h=h, bank=bank, col=col:
                             e.activation(
                            ynorm[:, h * 64:(h + 1) * 64],
                            PS_O[:, bank, col:col + 64], AF.Identity,
                            bias=0.0, scale=stati[:, 40 + h:41 + h]),
                            reads=[("PSO", bank), "stati"], writes=["ynorm"])
                    if _os.environ.get("KX3RAW", "0") == "1":
                        out_done[j] = k.dma("gpsimd", "xst",
                                            outd[j * TIL:(j + 1) * TIL, :],
                                            ynorm[:], reads=["ynorm"],
                                            writes=[("x3", j)])
                        continue
                    ln_stats([(ynorm[:], ["ynorm"])])
                    k.op("vector", lambda e: e.tensor_scalar(
                        junk[:], ynorm[:], statm[:, 0:1], stati[:, 0:1],
                        op0=ALU.subtract, op1=ALU.mult),
                        reads=["ynorm", "statm", "stati"], writes=["junk"])
                    k.op("vector", lambda e, j=j: e.tensor_tensor(
                        XW[:, j % XWN, :], XW[:, j % XWN, :], junk[:],
                        op=ALU.add),
                        reads=["junk", ("XW", j % XWN)],
                        writes=[("XW", j % XWN)])
                    out_done[j] = k.dma("gpsimd", "xst",
                                        outd[j * TIL:(j + 1) * TIL, :],
                                        XW[:, j % XWN, :],
                                        reads=[("XW", j % XWN)],
                                        writes=[("x3", j)])
                jl = tiles[-1]
                k.op("vector", lambda e, jl=jl: e.tensor_copy(
                    vprev[:], v1[:, jl % 4, :]),
                    reads=[("v1", jl % 4)], writes=["vprev"])
                k.op("vector", lambda e, jl=jl: e.tensor_copy(
                    kTprev[:], kT1[:, :, (jl % 4) * TIL:(jl % 4 + 1) * TIL]),
                    reads=[("k1", jl % 4)], writes=["kprev"])
            return out_done

        def emit_l1_ffn(in_done):
            ind = streams["x3"]
            outd = streams["x4"]
            out_done = [None] * NT
            for a in range(8):
                k.dma("gpsimd", "wres%d" % (a % 2), woT_res[:, a, :],
                      w_l1fT_d[a * TIL:(a + 1) * TIL, :], writes=["woT"])
                k.dma("gpsimd", "wres%d" % (a % 2), U_buf[:, a, :].unsqueeze(1).squeeze(1) if False else U_buf[:, a, :],
                      UcT_d[a * TIL:(a + 1) * TIL, :], writes=["Ub"])
                k.dma("gpsimd", "wres%d" % (a % 2), gw_buf[:, a, :],
                      gwT_d[a * TIL:(a + 1) * TIL, :], writes=["gw"])
            k.dma("gpsimd", "cstg", V_buf[:], Vc_d, writes=["Vb"])
            for tiles in chunks4(17, NT):
                for ii, j in enumerate(tiles):
                    k.dma("sync", "xw%d" % (j % 2), XW[:, j % XWN, :],
                          ind[j * TIL:(j + 1) * TIL, :],
                          waits=[in_done[j]], writes=[("XW", j % XWN)])
                ln_stats([(XW[:, j % XWN, :], [("XW", j % XWN)])
                          for j in tiles])
                for ii, j in enumerate(tiles):
                    k.op("vector", lambda e, j=j, ii=ii: e.tensor_scalar(
                        junk[:], XW[:, j % XWN, :], statm[:, ii:ii + 1],
                        stati[:, ii:ii + 1], op0=ALU.subtract, op1=ALU.mult),
                        reads=[("XW", j % XWN), "statm", "stati"],
                        writes=["junk"])
                    k.op("scalar", lambda e: e.activation(
                        o_tok[:], junk[:], AF.Identity, bias=0.0, scale=1.0),
                        reads=["junk"], writes=["otok"])
                    transpose_128(
                        lambda c: o_tok[:, c * TIL:(c + 1) * TIL], 8,
                        lambda h: oT_t[:, 4 * h:4 * h + 4, :],
                        ["otok"], ["oT"])
                    for d in range(8):
                        k.op("tensor", lambda e, d=d: e.matmul(
                            PS_ST[:, 0:8], oT_t[:, d, :], gw_buf[:, d, :],
                            start=(d == 0), stop=(d == 7)),
                            reads=["oT", "gw"], writes=["PSST"])
                    for d in range(8):
                        k.op("tensor", lambda e, d=d: e.matmul(
                            PS_S[:, 0:TIL], oT_t[:, d, :], U_buf[:, d, :],
                            start=(d == 0), stop=(d == 7)),
                            reads=["oT", "Ub"], writes=[("PSS", 0)])
                    k.op("vector", lambda e: e.reduce_max(
                        stat2[:, 40:41], PS_ST[:, 0:8], axis=AX.X,
                        negate=True), reads=["PSST"], writes=["stat2"])
                    k.op("scalar", lambda e: e.activation(
                        junk[:, 0:8], PS_ST[:, 0:8], AF.Exp,
                        bias=stat2[:, 40:41], scale=1.0,
                        accum_out=stat[:, 40:41]),
                        reads=["PSST", "stat2"], writes=["junk", "stat"])
                    k.op("vector", lambda e: e.reciprocal(
                        stati[:, 40:41], stat[:, 40:41]),
                        reads=["stat"], writes=["stati"])
                    k.op("vector", lambda e: e.tensor_scalar(
                        stati[:, 41:42], stati[:, 40:41], 1.0 / 16.0, 0.0,
                        op0=ALU.mult, op1=ALU.add),
                        reads=["stati"], writes=["stati"])
                    k.op("vector", lambda e: e.scalar_tensor_tensor(
                        sT_m[:, 0, :].rearrange("p (a b) -> p a b", b=16),
                        PS_S[:, 0:TIL].rearrange("p (a b) -> p a b", b=16),
                        stati[:, 41:42],
                        junk[:, 0:8].unsqueeze(2).broadcast_to([TIL, 8, 16]),
                        op0=ALU.mult, op1=ALU.mult),
                        reads=[("PSS", 0), "stati", "junk"], writes=["sT"])
                    k.op("tensor", lambda e: e.transpose(
                        PS_T[:, 0:TIL], sT_m[:, 0, :], identb[:]),
                        reads=["sT", "identb"], writes=["PST"])
                    k.op("vector", lambda e: e.tensor_copy(
                        PT5[:, 0, :], PS_T[:, 0:TIL]),
                        reads=["PST"], writes=["PT5"])
                    for nh in range(2):
                        psr = nh
                        for d in range(8):
                            k.op("tensor", lambda e, d=d, nh=nh, psr=psr:
                                 e.matmul(
                                PS_MM[:, psr, :], oT_t[:, d, :],
                                woT_res[:, d, nh * 512:(nh + 1) * 512],
                                start=(d == 0), stop=False),
                                reads=["oT", "woT"], writes=[("PSMM", psr)])
                        k.op("tensor", lambda e, nh=nh, psr=psr: e.matmul(
                            PS_MM[:, psr, :], PT5[:, 0, :],
                            V_buf[:, nh * 512:(nh + 1) * 512],
                            start=False, stop=True),
                            reads=["PT5", "Vb"], writes=[("PSMM", psr)])
                    k.op("scalar", lambda e: e.activation(
                        junk[:, 0:512], PS_MM[:, 0, :], AF.Square, bias=0.0,
                        scale=1.0, accum_out=stat[:, 8:9]),
                        reads=[("PSMM", 0)], writes=["junk", "stat"])
                    k.op("scalar", lambda e: e.activation(
                        junk[:, 512:1024], PS_MM[:, 1, :], AF.Square,
                        bias=0.0, scale=1.0, accum_out=stat[:, 9:10]),
                        reads=[("PSMM", 1)], writes=["junk", "stat"])
                    k.op("vector", lambda e: e.reduce_sum(
                        stat2[:, 8:9].unsqueeze(2), PS_MM[:, 0, :]
                        .unsqueeze(1), axis=AX.X),
                        reads=[("PSMM", 0)], writes=["stat2"])
                    k.op("vector", lambda e: e.reduce_sum(
                        stat2[:, 9:10].unsqueeze(2), PS_MM[:, 1, :]
                        .unsqueeze(1), axis=AX.X),
                        reads=[("PSMM", 1)], writes=["stat2"])
                    k.op("vector", lambda e: e.tensor_tensor(
                        statm[:, 8:9], stat2[:, 8:9], stat2[:, 9:10],
                        op=ALU.add), reads=["stat2"], writes=["statm"])
                    k.op("vector", lambda e: e.tensor_scalar(
                        statm[:, 8:9], statm[:, 8:9], 1.0 / D, 0.0,
                        op0=ALU.mult, op1=ALU.add),
                        reads=["statm"], writes=["statm"])
                    k.op("vector", lambda e: e.tensor_tensor(
                        stat[:, 8:9], stat[:, 8:9], stat[:, 9:10], op=ALU.add),
                        reads=["stat"], writes=["stat"])
                    k.op("vector", lambda e: e.tensor_scalar(
                        stat[:, 8:9], stat[:, 8:9], 1.0 / D, 0.0,
                        op0=ALU.mult, op1=ALU.add),
                        reads=["stat"], writes=["stat"])
                    k.op("vector", lambda e: e.tensor_tensor(
                        stat2[:, 10:11], statm[:, 8:9], statm[:, 8:9],
                        op=ALU.mult), reads=["statm"], writes=["stat2"])
                    k.op("vector", lambda e: e.tensor_tensor(
                        stat[:, 8:9], stat[:, 8:9], stat2[:, 10:11],
                        op=ALU.subtract), reads=["stat", "stat2"],
                        writes=["stat"])
                    k.op("vector", lambda e: e.tensor_scalar(
                        stat[:, 8:9], stat[:, 8:9], EPS, 0.0, op0=ALU.add,
                        op1=ALU.add), reads=["stat"], writes=["stat"])
                    k.op("scalar", lambda e: e.activation(
                        stat2[:, 8:9], stat[:, 8:9], AF.Sqrt, bias=0.0,
                        scale=1.0), reads=["stat"], writes=["stat2"])
                    k.op("vector", lambda e: e.reciprocal(
                        stati[:, 8:9], stat2[:, 8:9]),
                        reads=["stat2"], writes=["stati"])
                    k.op("vector", lambda e: e.tensor_scalar(
                        ynorm[:, 0:512], PS_MM[:, 0, :], statm[:, 8:9],
                        stati[:, 8:9], op0=ALU.subtract, op1=ALU.mult),
                        reads=[("PSMM", 0), "statm", "stati"],
                        writes=["ynorm"])
                    k.op("vector", lambda e: e.tensor_scalar(
                        ynorm[:, 512:1024], PS_MM[:, 1, :], statm[:, 8:9],
                        stati[:, 8:9], op0=ALU.subtract, op1=ALU.mult),
                        reads=[("PSMM", 1), "statm", "stati"],
                        writes=["ynorm"])
                    k.op("vector", lambda e, j=j: e.tensor_tensor(
                        XW[:, j % XWN, :], XW[:, j % XWN, :], ynorm[:],
                        op=ALU.add),
                        reads=["ynorm", ("XW", j % XWN)],
                        writes=[("XW", j % XWN)])
                    out_done[j] = k.dma("gpsimd", "xst",
                                        outd[j * TIL:(j + 1) * TIL, :],
                                        XW[:, j % XWN, :],
                                        reads=[("XW", j % XWN)],
                                        writes=[("x4", j)])
            return out_done

        stream_done = {"x1": t_x1_st}
        final_stream = "x1"
        if n_stages >= 2:
            barrier()
            yd = emit_l0ffn_attn(0, "x1", t_x1_st, kv_lo=1, q_lo=8, do_ln=True)
            stream_done["y"] = yd
            final_stream = "y"
        if n_stages >= 3:
            barrier()
            xl2d = emit_l0ffn_ffn(0, yd, "xl2", q_lo=8)
            stream_done["xl2"] = xl2d
            final_stream = "xl2"
        if n_stages >= 4:
            barrier()
            yd2 = emit_l0ffn_attn(1, "xl2", xl2d, kv_lo=9, q_lo=16, do_ln=False)
            barrier()
            zd = emit_l0ffn_ffn(1, yd2, "z", q_lo=16)
            stream_done["z"] = zd
            barrier()
            x2d = [None] * NT
            for tiles in chunks4(16, NT):
                nt_b = len(tiles)
                for ii, j in enumerate(tiles):
                    k.dma("sync", "xw%d" % (j % 2), XW[:, j % XWN, :],
                          x1_d[j * TIL:(j + 1) * TIL, :],
                          waits=[t_x1_st[j]], writes=[("XW", j % XWN)])
                    k.dma("sync", "hsh%d" % (j % 2), A2W[:, ii, :],
                          z_d[j * TIL:(j + 1) * TIL, :],
                          waits=[zd[j]], writes=[("A2W", ii)])
                ln_stats([(A2W[:, ii, :], [("A2W", ii)])
                          for ii in range(nt_b)])
                for ii, j in enumerate(tiles):
                    k.op("vector", lambda e, ii=ii: e.tensor_scalar(
                        ynorm[:], A2W[:, ii, :], statm[:, ii:ii + 1],
                        stati[:, ii:ii + 1], op0=ALU.subtract, op1=ALU.mult),
                        reads=[("A2W", ii), "statm", "stati"], writes=["ynorm"])
                    k.op("vector", lambda e, j=j: e.tensor_tensor(
                        XW[:, j % XWN, :], XW[:, j % XWN, :], ynorm[:], op=ALU.add),
                        reads=["ynorm", ("XW", j % XWN)], writes=[("XW", j % XWN)])
                    x2d[j] = k.dma("gpsimd", "xst",
                                   x2_d[j * TIL:(j + 1) * TIL, :],
                                   XW[:, j % XWN, :], reads=[("XW", j % XWN)],
                                   writes=[("x2", j)])
            stream_done["x2"] = x2d
            final_stream = "x2"
        if n_stages >= 5:
            barrier()
            x3d = emit_l1_attn(x2d)
            stream_done["x3"] = x3d
            final_stream = "x3"
        if n_stages >= 6:
            barrier()
            x4d = emit_l1_ffn(x3d)
            stream_done["x4"] = x4d
            final_stream = "x4"

        if spy_out is not None:
            k.op("vector", lambda e: e.tensor_copy(
                junk[:], xhT[:, 0, 0:2, 0:512].rearrange("p a b -> p (a b)")),
                reads=[("xhT", 0)], writes=["junk"])
            k.dma("sync", "spy", spy_out[0:TIL, :], junk[:], reads=["junk"])
            k.op("vector", lambda e: e.tensor_copy(
                junk[:], qT_blk[:, 0:2, 0:512].rearrange("p a b -> p (a b)")),
                reads=["qT"], writes=["junk"])
            k.dma("sync", "spy", spy_out[TIL:2 * TIL, :], junk[:], reads=["junk"])
            k.op("vector", lambda e: e.tensor_copy(
                junk[:], kT_blk[:, 0:2, 0:512].rearrange("p a b -> p (a b)")),
                reads=["kT"], writes=["junk"])
            k.dma("sync", "spy", spy_out[2 * TIL:3 * TIL, :], junk[:], reads=["junk"])
            k.op("vector", lambda e: e.tensor_copy(junk[:], v_blk[:, 3, :]),
                reads=["v"], writes=["junk"])
            k.dma("sync", "spy", spy_out[3 * TIL:4 * TIL, :], junk[:], reads=["junk"])
            k.op("vector", lambda e: e.tensor_copy(junk[:], ktok_blk[:, 3, :]),
                reads=["ktok"], writes=["junk"])
            k.dma("sync", "spy", spy_out[4 * TIL:5 * TIL, :], junk[:], reads=["junk"])
            k.op("vector", lambda e: e.tensor_copy(
                junk[:, 0:512], state_sb[:]), reads=["stsb"], writes=["junk"])
            k.op("vector", lambda e: e.tensor_copy(
                junk2s[:, 0:512], sT_m[:].rearrange("p a b -> p (a b)")),
                reads=["sT"], writes=["junk2s"])
            k.dma("sync", "spy", spy_out[6 * TIL:7 * TIL, :], junk2s[:],
                  reads=["junk2s"])
            k.op("vector", lambda e: e.tensor_copy(junk2s[:], o_tok[:]),
                reads=["otok"], writes=["junk2s"])
            k.dma("sync", "spy", spy_out[7 * TIL:8 * TIL, :], junk2s[:],
                  reads=["junk2s"])
            k.op("vector", lambda e: e.tensor_copy(junk2s[:], ynorm[:]),
                reads=["ynorm"], writes=["junk2s"])
            k.dma("sync", "spy", spy_out[8 * TIL:9 * TIL, :], junk2s[:],
                  reads=["junk2s"])
            k.op("vector", lambda e: e.tensor_copy(
                junk[0:1, 512:1024], SMB[0:1, NTOK:NTOK + 512]),
                reads=["SMBq"], writes=["junk"])
            k.dma("sync", "spy", spy_out[5 * TIL:6 * TIL, :], junk[:], reads=["junk"])

        sdram = streams[final_stream]
        built = {j for tt in blocks for j in tt}
        if dbg is not None:
            nm, tl = dbg
            tl = [j for j in tl if j in built]
            for i, j in enumerate(tl):
                k.dma("sync", "out%d" % (i % 2), dbg_out[i * TIL:(i + 1) * TIL, :],
                      streams[nm][j * TIL:(j + 1) * TIL, :], reads=[(nm, j)])
        for i, j in enumerate(range(NT - OWN // TIL, NT)):
            if j not in built:
                continue
            k.dma("sync", "out%d" % (i % 2), xout[i * TIL:(i + 1) * TIL, :],
                  sdram[j * TIL:(j + 1) * TIL, :], reads=[(final_stream, j)])

        for s in k.sems_needed():
            k.semh[s] = E(nc.semaphore(s))
        final_out = [k.last("c_out0"), k.last("c_out1")]
        with nc.Block() as block:
            @block.sync
            def _(sync):
                for fn in k.q["sync"]:
                    fn(sync)
                for t in final_out:
                    if t[1] > 0:
                        sync.wait_ge(k.semh[t[0]], t[1])

            @block.gpsimd
            def _(gpsimd):
                for fn in k.q["gpsimd"]:
                    fn(gpsimd)

            @block.tensor
            def _(tensor):
                for fn in k.q["tensor"]:
                    fn(tensor)

            @block.vector
            def _(vector):
                for fn in k.q["vector"]:
                    fn(vector)

            @block.scalar
            def _(scalar):
                for fn in k.q["scalar"]:
                    fn(scalar)

    return nc


def _rot_tables(t):
    t = t.astype(np.float64)[:, None]
    half = np.arange(0, 64, 2, dtype=np.float64)
    inv = 1.0 / 10000.0 ** (half / 64.0)
    fr = t * inv[None, :]
    fr = np.concatenate([fr, fr], axis=1)
    basev = (half + 0.4 * 64.0) / (1.4 * 64.0)
    power = (t - 2048.0) / 256.0
    sc = basev[None, :] ** power
    sc = np.concatenate([sc, sc], axis=1)
    cos, sin = np.cos(fr), np.sin(fr)
    sgn = np.where(np.arange(64) < 32, -1.0, 1.0)[None, :]
    Aq = 8.0 * cos * sc
    Bq = 8.0 * sgn * sin * sc
    Ak = cos / sc
    Bk = sgn * sin / sc
    return np.concatenate([Aq, Bq, Ak, Bk], axis=1).astype(np.float32)


def _host_prep(L0_attn_x):
    B = L0_attn_x.shape[0]
    base = np.zeros((TIL, 640), np.float32)
    r = np.arange(TIL)[:, None]
    c = np.arange(640)[None, :]
    base[~((c >= r) & (c <= r + 512))] = NEG
    cores = []
    for b in range(B):
        for h in range(2):
            s = h * OWN
            lo = s - HALO
            pad = max(0, -lo)
            xsl = np.zeros((NTOK, D), np.float32)
            xsl[pad:] = np.asarray(L0_attn_x[b, max(0, lo):s + OWN], np.float32)
            frow = np.zeros((1, NTOK), np.float32)
            frow[0, pad:] = 1.0
            fpart = np.zeros((TIL, NT), np.float32)
            for j in range(NT):
                fpart[:, j] = (j * TIL + np.arange(TIL)) >= pad
            msp = np.zeros((4, TIL, 640), np.float32)
            for q in range(4):
                lj = (17 + q) * TIL
                cut = pad - lj + 512
                m = base.copy()
                if cut > 0:
                    m[:, 0:cut] += NEG
                msp[q] = m
            tglob = np.arange(NTOK, dtype=np.float64) + (s - HALO)
            tab = _rot_tables(tglob).reshape(NT, TIL, 256)
            m1b = np.zeros((TIL, 256), np.float32)
            r1 = np.arange(TIL)[:, None]
            c1 = np.arange(256)[None, :]
            allowed = ((c1 >= 64) & (c1 <= 128 + r1) & (r1 < 64)) | \
                      ((c1 >= 128) & (c1 <= 128 + r1) & (r1 >= 64))
            m1b[~allowed] = NEG
            m1s = m1b.copy()
            cut1 = pad - 2048
            if cut1 > 0:
                m1s[:, 0:cut1] += NEG
            cores.append(dict(xin=np.ascontiguousarray(xsl), frow=frow,
                              fpart=fpart, mask_base=base, mask_sp=msp,
                              tab4=np.ascontiguousarray(tab),
                              m1_base=m1b, m1_sp=m1s))
    return cores


def _host_consts():
    ident = np.eye(TIL, dtype=np.float32)
    causalT = np.triu(np.ones((TIL, TIL), np.float32))
    return dict(ident=ident, causalT=causalT)


def _host_weights(i):
    qs = np.asarray(i["L0_ffn_q_scale"], np.float32)
    ks = np.asarray(i["L0_ffn_k_scale"], np.float32)
    qs4 = np.zeros((2, TIL, 4, 64), np.float32)
    for l in range(2):
        qs4[l, :, 0] = np.tile(qs[l], (TIL, 1))
        qs4[l, :, 1] = np.tile(np.roll(qs[l], -32), (TIL, 1))
        qs4[l, :, 2] = np.tile(ks[l], (TIL, 1))
        qs4[l, :, 3] = np.tile(np.roll(ks[l], -32), (TIL, 1))
    b1 = np.asarray(i["L0_ffn_b1"], np.float32)
    b1a = np.zeros((2, TIL, 22), np.float32)
    b1g = np.zeros((2, TIL, 22), np.float32)
    for cc in range(22):
        w = min(TIL, 2730 - cc * TIL)
        b1a[:, 0:w, cc] = b1[:, cc * TIL:cc * TIL + w]
        b1g[:, 0:w, cc] = b1[:, 2730 + cc * TIL:2730 + cc * TIL + w]
    return dict(
        w_l0a_qkvT=np.ascontiguousarray(np.asarray(i["L0_attn_Wqkv_w"], np.float32).T),
        b_l0a_qkv=np.asarray(i["L0_attn_Wqkv_b"], np.float32).reshape(1, -1),
        w_l0a_outT=np.ascontiguousarray(np.asarray(i["L0_attn_out_w"], np.float32).T),
        b_l0a_out=np.asarray(i["L0_attn_out_b"], np.float32).reshape(1, -1),
        w_ffn_qkv=np.ascontiguousarray(np.asarray(i["L0_ffn_wqkv"], np.float32)),
        w_ffn_wo=np.ascontiguousarray(np.asarray(i["L0_ffn_wo"], np.float32)),
        w_ffn_w1=np.ascontiguousarray(np.asarray(i["L0_ffn_w1"], np.float32)),
        w_ffn_w2=np.ascontiguousarray(np.asarray(i["L0_ffn_w2"], np.float32)),
        qs4=qs4,
        gam=np.ascontiguousarray(np.tile(
            np.asarray(i["L0_ffn_ln_g"], np.float32)[:, None, :], (1, TIL, 1))),
        bet=np.ascontiguousarray(np.tile(
            np.asarray(i["L0_ffn_ln_b"], np.float32)[:, None, :], (1, TIL, 1))),
        b1a=b1a, b1g=b1g,
        b2=np.asarray(i["L0_ffn_b2"], np.float32).reshape(2, 1, D),
        w_l1_q=np.ascontiguousarray(np.asarray(i["L1_attn_Wq"], np.float32)),
        w_l1_k=np.ascontiguousarray(np.asarray(i["L1_attn_Wk"], np.float32)),
        w_l1_v=np.ascontiguousarray(np.asarray(i["L1_attn_Wv"], np.float32)),
        w_l1f_T=np.ascontiguousarray(np.asarray(i["L1_ffn_weight"], np.float32).T),
        gate_wT=np.ascontiguousarray(np.asarray(i["L1_ffn_gate_w"], np.float32).T),
        U_catT=np.ascontiguousarray(np.asarray(i["L1_ffn_lora_U"], np.float32)
                                    .transpose(2, 0, 1).reshape(D, TIL)),
        V_cat=np.ascontiguousarray(np.asarray(i["L1_ffn_lora_V"], np.float32)
                                   .transpose(0, 2, 1).reshape(TIL, D)),
    )


def kernel(**inputs):
    x = np.asarray(inputs["L0_attn_x"], np.float32)
    B = x.shape[0]
    cores = _host_prep(x)
    consts = _host_consts()
    wmap = _host_weights(inputs)
    nc = build_nc(n_stages=N_STAGES, dbg=None)
    in_maps = [dict(c, **consts, **wmap) for c in cores]
    res = run_bass_kernel_spmd(nc, in_maps, core_ids=list(range(8)))
    out = np.zeros((B, 2 * OWN, D), np.float32)
    for ci in range(8):
        b, h = ci // 2, ci % 2
        out[b, h * OWN:(h + 1) * OWN] = res.results[ci]["xout"]
    return out


# version 07
# speedup vs baseline: 1.4564x; 1.3265x over previous
import contextlib
import os
import numpy as np
import ml_dtypes
import concourse.bass as bass
import concourse.mybir as mybir
from concourse.bass_utils import run_bass_kernel_spmd

F32 = mybir.dt.float32
BF16 = mybir.dt.bfloat16
AF = mybir.ActivationFunctionType
ALU = mybir.AluOpType
AX = mybir.AxisListType

TIL = 128
HALO = 2176
OWN = 2048
NT = (HALO + OWN) // TIL
NTOK = HALO + OWN
D = 1024
NEG = -1.0e30
EPS = 1e-5

N_STAGES = 6

ENGINES = ("sync", "gpsimd", "tensor", "vector", "scalar")


class K:

    def __init__(self):
        self.fence = {}
        self.ops = []
        self.last_w = {}
        self.readers = {}
        self.chan_last = {}
        self.eng_last = {}
        self.semh = {}
        self.esem = {"tensor": "tns", "vector": "vec", "scalar": "scl",
                     "gpsimd": "gps", "sync": "snc"}

    PSUM_KEYS = ("PSMM", "PSS", "PSO", "PSST", "PST")

    def _is_psum(self, key):
        kk = key[0] if isinstance(key, tuple) else key
        return kk in self.PSUM_KEYS

    def _deps(self, reads, writes, waits):
        deps = set()
        for d in waits:
            if d is not None:
                deps.add(d)
        for r in reads:
            d = self.last_w.get(r)
            if d is not None:
                deps.add(d)
            if self._is_psum(r):
                deps.update(self.readers.get(r, ()))
        for w in writes:
            d = self.last_w.get(w)
            if d is not None:
                deps.add(d)
            deps.update(self.readers.get(w, ()))
        return deps

    def _record(self, t, reads, writes):
        for r in reads:
            if self._is_psum(r):
                self.last_w[r] = t
                self.readers[r] = []
            else:
                self.readers.setdefault(r, []).append(t)
        for w in writes:
            self.last_w[w] = t
            self.readers[w] = []

    def op(self, eng, fn, reads=(), writes=(), waits=()):
        deps = self._deps(reads, writes, waits)
        if eng in self.fence:
            deps.add(self.fence[eng])
        t = len(self.ops)
        self.ops.append(dict(eng=eng, fn=fn, deps=deps, ch=None))
        self.eng_last[eng] = t
        self._record(t, reads, writes)
        return t

    def dma(self, eng, ch, out, in_, reads=(), writes=(), waits=()):
        deps = self._deps(reads, writes, waits)
        if eng in self.fence:
            deps.add(self.fence[eng])
        ch = eng + "_" + ch
        if ch in self.chan_last:
            deps.add(self.chan_last[ch])
        t = len(self.ops)
        def fn(e, o=out, i=in_):
            return e.dma_start(out=o, in_=i)
        self.ops.append(dict(eng=eng, fn=fn, deps=deps, ch=ch))
        self.eng_last[eng] = t
        self.chan_last[ch] = t
        self._record(t, reads, writes)
        return t

    def barrier(self):
        lasts = list(self.eng_last.values()) + list(self.chan_last.values())
        for e in ENGINES:
            self.fence[e] = self.op(e, lambda en: en.nop(), waits=lasts)

    def finalize(self):
        n = len(self.ops)
        level = [0] * n
        for i, o in enumerate(self.ops):
            lv = 0
            for d in o["deps"]:
                if level[d] >= lv:
                    lv = level[d] + 1
            level[i] = lv
        order = {e: [] for e in ENGINES}
        for i, o in enumerate(self.ops):
            order[o["eng"]].append(i)
        for e in ENGINES:
            order[e].sort(key=lambda i: (level[i], i))
        semname = [None] * n
        semval = [0] * n
        chan_count = {}
        for e in ENGINES:
            pos = 0
            for i in order[e]:
                o = self.ops[i]
                if o["ch"] is not None:
                    c = "c_" + o["ch"]
                    chan_count[c] = chan_count.get(c, 0) + 16
                    semname[i], semval[i] = c, chan_count[c]
                else:
                    pos += 1
                    semname[i] = self.esem[e]
                    semval[i] = pos
        sems = sorted(set(x for x in semname if x))
        queues = {e: [] for e in ENGINES}
        for e in ENGINES:
            water = {}
            for i in order[e]:
                o = self.ops[i]
                for d in sorted(o["deps"], key=lambda d: semval[d]):
                    s, v = semname[d], semval[d]
                    if water.get(s, 0) >= v:
                        continue
                    water[s] = v
                    queues[e].append(
                        lambda en, s=s, v=v: en.wait_ge(self.semh[s], v))
                inc = 16 if o["ch"] is not None else 1
                queues[e].append(
                    lambda en, o=o, s=semname[i], inc=inc:
                    o["fn"](en).then_inc(self.semh[s], inc))
        return queues, sems, semname, semval


def chunks4(lo, hi, w=4):
    out = []
    j = lo
    while j < hi:
        out.append(list(range(j, min(hi, j + w))))
        j += w
    return out


def build_nc(n_stages=N_STAGES, dbg=None):
    nc = bass.Bass("TRN2", target_bir_lowering=False, debug=False)
    k = K()

    def din(name, shape, dt=F32):
        return nc.dram_tensor(name, shape, dt, kind="ExternalInput").ap()

    xin = din("xin", [NTOK, D])
    frow_d = din("frow", [1, NTOK])
    ident_d = din("ident", [TIL, TIL])
    causalT_d = din("causalT", [TIL, TIL])
    fpart_d = din("fpart", [TIL, NT])
    mask_base_d = din("mask_base", [TIL, 640])
    mask_sp_d = din("mask_sp", [4, TIL, 640])
    tab4_d = din("tab4", [NT, TIL, 256])
    qs4_d = din("qs4", [2, TIL, 4, 64])
    gam_d = din("gam", [2, TIL, D])
    bet_d = din("bet", [2, TIL, D])
    b1a_d = din("b1a", [2, TIL, 22])
    b1g_d = din("b1g", [2, TIL, 22])
    b2_d = din("b2", [2, 1, D])
    w_ffn_qkv_d = din("w_ffn_qkv", [2, D, 1536])
    w_ffn_wo_d = din("w_ffn_wo", [2, 512, D])
    w_ffn_w1_d = din("w_ffn_w1", [2, D, 5460])
    w_ffn_w2_d = din("w_ffn_w2", [2, 2730, D])
    w_l1q_d = din("w_l1_q", [D, D])
    w_l1k_d = din("w_l1_k", [D, D])
    w_l1v_d = din("w_l1_v", [D, D])
    w_l1fT_d = din("w_l1f_T", [D, D])
    gwT_d = din("gate_wT", [D, 8])
    UcT_d = din("U_catT", [D, TIL])
    Vc_d = din("V_cat", [TIL, D])
    m1_base_d = din("m1_base", [TIL, 256])
    m1_sp_d = din("m1_sp", [TIL, 256])
    w_qkvT_d = din("w_l0a_qkvT", [D, 3 * D])
    b_qkv_d = din("b_l0a_qkv", [1, 3 * D])
    w_outT_d = din("w_l0a_outT", [D, D])
    b_out_d = din("b_l0a_out", [1, D])

    xout = nc.dram_tensor("xout", [OWN, D], F32, kind="ExternalOutput").ap()
    x1_d = nc.dram_tensor("x1_d", [NTOK, D], F32, kind="Internal").ap()
    w1_st = nc.dram_tensor("w1_st", [2, D, 5460], BF16, kind="Internal").ap()
    w2_st = nc.dram_tensor("w2_st", [2, 2730, D], BF16, kind="Internal").ap()
    y_d = nc.dram_tensor("y_d", [NTOK, D], F32, kind="Internal").ap()
    xl2_d = nc.dram_tensor("xl2_d", [NTOK, D], F32, kind="Internal").ap()
    z_d = nc.dram_tensor("z_d", [NTOK, D], F32, kind="Internal").ap()
    x2_d = nc.dram_tensor("x2_d", [NTOK, D], F32, kind="Internal").ap()
    streams = {"x1": x1_d, "y": y_d, "xl2": xl2_d, "z": z_d, "x2": x2_d,
               "x3": z_d, "x4": x1_d}
    spy_out = None
    if int(os.environ.get("KSPY", "0")):
        spy_out = nc.dram_tensor("spy", [9 * TIL, 1024], F32,
                                 kind="ExternalOutput").ap()
    dbg_out = None
    if dbg is not None:
        dbg_out = nc.dram_tensor("dbgo", [len(dbg[1]) * TIL, D], F32,
                                 kind="ExternalOutput").ap()

    with contextlib.ExitStack() as ctx:
        E = ctx.enter_context

        def sb(name, shape, dt):
            return E(nc.sbuf_tensor(name, shape, dt))

        def ps(name, shape, dt):
            return E(nc.psum_tensor(name, shape, dt))

        XWN = 5
        XW = sb("XW", [TIL, XWN, D], F32)
        SMB = sb("SMB", [TIL, NTOK + 4 * D], BF16)
        frow_b = SMB[0:1, 0:NTOK]
        biasq_b = SMB[0:1, NTOK:NTOK + 3 * D]
        biaso_b = SMB[0:1, NTOK + 3 * D:NTOK + 4 * D]
        identb = sb("identb_s", [TIL, TIL], BF16)
        causalT = sb("causalT_s", [TIL, TIL], F32)

        xhT = sb("xhT", [TIL, 2, 8, 512], BF16)
        wqk_buf = sb("wqk_buf", [TIL, 2, 8, TIL], BF16)
        wv_res = sb("wv_res", [TIL, 8, D], BF16)
        woT_res = sb("woT_res", [TIL, 8, D], BF16)
        qT_blk = sb("qT_blk", [TIL, 8, 512], BF16)
        kT_blk = sb("kT_blk", [TIL, 8, 512], BF16)
        v_blk = sb("v_blk", [TIL, 4, D], BF16)
        ktok_blk = sb("ktok_blk", [TIL, 4, D], BF16)
        sT_m = sb("sT_m", [TIL, 4, TIL], BF16)
        state_acc = sb("state_acc", [TIL, 512], F32)
        state_sb = sb("state_sb", [TIL, 512], BF16)
        o_tok = sb("o_tok", [TIL, D], BF16)
        oT_t = sb("oT_t", [TIL, 8, TIL], BF16)
        stat = sb("stat", [TIL, 64], F32)
        stat2 = sb("stat2", [TIL, 64], F32)
        statm = sb("statm", [TIL, 64], F32)
        stati = sb("stati", [TIL, 64], F32)
        a_tile = sb("a_tile", [TIL, 4, D], BF16)
        junk = sb("junk", [TIL, D], F32)
        cst = sb("cst", [TIL, 4], F32)
        junk2s = sb("junk2s", [TIL, D], F32) if spy_out is not None else None
        ynorm = sb("ynorm", [TIL, D], F32)

        A2W = sb("A2W", [TIL, 4, D], F32)
        PTA = sb("PTA", [TIL, 11264], BF16)
        EXT2 = sb("EXT2", [TIL, 4096], BF16)
        w1a_buf = wqk_buf
        w1g_buf = sb("w1g_buf", [TIL, 2, 8, TIL], BF16)
        w2_buf = sb("w2_buf", [TIL, 2, 512], BF16)
        mask_base = sb("mask_base_s", [TIL, 640], BF16)
        maskcur = sb("maskcur", [TIL, 640], BF16)
        qs4 = sb("qs4_s", [TIL, 4, 64], F32)
        gam_cur = sb("gam_cur", [TIL, D], BF16)
        bet_cur = sb("bet_cur", [TIL, D], BF16)
        b1a_col = sb("b1a_col", [TIL, 22], F32)
        b1g_col = sb("b1g_col", [TIL, 22], F32)
        fpart = sb("fpart_s", [TIL, NT], F32)
        U_buf = wqk_buf[:, 0, :, :]
        V_buf = v_blk[:, 0, :]
        gw_buf = wqk_buf[:, 1, :, 0:8]

        v_roll = v_blk[:].rearrange("p a b -> p (a b)").rearrange(
            "p (a b) -> p a b", b=512)
        kT_roll = ktok_blk[:].rearrange("p a b -> p (a b)").rearrange(
            "p (a b) -> p a b", b=1024)
        qT2 = a_tile[:].rearrange("p a b -> p (a b)")[:, 0:2048].rearrange(
            "p (a b) -> p a b", b=512)
        qfb = a_tile[:].rearrange("p a b -> p (a b)")[:, 2048:4096].rearrange(
            "p (a b) -> p a b", b=512)
        kfb = xhT[:].rearrange("p a b c -> p (a b c)")[:, 0:2048].rearrange(
            "p (a b) -> p a b", b=512)
        _xhT_flat = xhT[:].rearrange("p a b c -> p (a b c)")
        junk2v = _xhT_flat[:, 2048:6144].bitcast(F32).rearrange(
            "p (a b) -> p a b", b=512)
        junk3v = PTA[:, 0:4096].bitcast(F32).rearrange(
            "p (a b) -> p a b", b=512)
        tb_raw = PTA[:, 4096:6144].bitcast(F32).rearrange(
            "p (a b) -> p a b", b=256)
        tb_f = PTA[:, 6144:8192].bitcast(F32).rearrange(
            "p (a b) -> p a b", b=256)
        Sm = PTA[:, 8192:9472].bitcast(F32)
        qraw = qT_blk[:].rearrange("p a b -> p (a b)").bitcast(F32).rearrange(
            "p (a b) -> p a b", b=512)
        kraw = kT_blk[:].rearrange("p a b -> p (a b)").bitcast(F32).rearrange(
            "p (a b) -> p a b", b=512)
        hT_ring = EXT2[:, 0:2064].rearrange("p (a b) -> p a b", b=1032)
        PT5 = EXT2[:, 2064:2704].rearrange("p (a b) -> p a b", b=TIL)
        ob2 = EXT2[:, 2704:3216]
        oT2 = EXT2[:, 3216:3728].rearrange("p (a b) -> p a b", b=TIL)
        pT_g = PTA[:, 0:11264].rearrange("p (a b) -> p a b", b=512)
        hTf = _xhT_flat[:, 0:4096].rearrange("p (a b) -> p a b", b=512)
        wffn_v = wv_res[:].rearrange("p a b -> p (a b)")
        wffn2 = woT_res[:].rearrange("p a b -> p (a b)")
        PS_MM = ps("PS_MM", [TIL, 2, 512], F32)
        PS_S = ps("PS_S", [TIL, 1024], F32)
        PS_O = ps("PS_O", [TIL, 2, 512], F32)
        PS_ST = ps("PS_ST", [TIL, 512], F32)
        PS_T = ps("PS_T", [TIL, 1024], BF16)

        t_c125 = k.op("vector", lambda e: e.memset(cst[:, 0:1], 0.125),
                      writes=["cst"])
        k.op("vector", lambda e: e.memset(hT_ring[:, 0, :], 0.0),
             writes=[("hT", 0)])
        k.op("vector", lambda e: e.memset(hT_ring[:, 1, :], 0.0),
             writes=[("hT", 1)])
        k.dma("sync", "cst", fpart[:], fpart_d, writes=["fpart"])
        if spy_out is not None:
            k.op("vector", lambda e: e.memset(junk2s[:], 0.0), writes=["junk2s"])
        k.dma("sync", "cst", causalT[:], causalT_d, writes=["causalT"])
        k.dma("gpsimd", "cstg", identb[:], ident_d, writes=["identb"])
        k.dma("gpsimd", "cstg", frow_b, frow_d, writes=["SMBf"])
        k.dma("gpsimd", "cstg", biasq_b, b_qkv_d, writes=["SMBq"])
        k.dma("gpsimd", "cstg", biaso_b, b_out_d, writes=["SMBo"])
        for a in range(8):
            k.dma("gpsimd", "wres%d" % (a % 2), wv_res[:, a, :],
                  w_qkvT_d[a * TIL:(a + 1) * TIL, 2 * D:3 * D], writes=["wv"])
            k.dma("gpsimd", "wres%d" % (a % 2), woT_res[:, a, :],
                  w_outT_d[a * TIL:(a + 1) * TIL, :], writes=["woT"])

        for l in range(2 if n_stages >= 3 else 0):
            for a in range(8):
                for half in range(6):
                    wslc = slice(half * 910, (half + 1) * 910)
                    rr = (a * 6 + half) % 2
                    buf = w1g_buf[:, rr, :, :] \
                        .rearrange("p a b -> p (a b)")[:, 0:910]
                    k.dma("gpsimd", "stg%d" % rr, buf,
                          w_ffn_w1_d[l, a * TIL:(a + 1) * TIL, wslc],
                          writes=[("w1g", rr)])
                    k.dma("sync", "stgo%d" % rr,
                          w1_st[l, a * TIL:(a + 1) * TIL, wslc], buf,
                          reads=[("w1g", rr)])
            for a in range(22):
                wdt = 42 if a == 21 else TIL
                buf = w1g_buf[:, a % 2, :, :] \
                    .rearrange("p a b -> p (a b)")[0:wdt, 0:1024]
                k.dma("gpsimd", "stg%d" % (a % 2), buf,
                      w_ffn_w2_d[l, a * TIL:a * TIL + wdt, :],
                      writes=[("w1g", a % 2)])
                k.dma("sync", "stgo%d" % (a % 2),
                      w2_st[l, a * TIL:a * TIL + wdt, :], buf,
                      reads=[("w1g", a % 2)])

        def ln_stats(srcs, n_feat=D):
            n = len(srcs)
            for ii, (ap, rk) in enumerate(srcs):
                k.op("scalar", lambda e, ap=ap, ii=ii: e.activation(
                    junk[:], ap, AF.Square, bias=0.0, scale=1.0,
                    accum_out=stat[:, ii:ii + 1]),
                    reads=list(rk), writes=["junk", "stat"])
                k.op("vector", lambda e, ap=ap, ii=ii: e.reduce_sum(
                    stat2[:, ii:ii + 1].unsqueeze(2), ap.unsqueeze(1), axis=AX.X),
                    reads=list(rk), writes=["stat2"])
            sl = slice(0, n)
            k.op("vector", lambda e, sl=sl: e.tensor_scalar(
                statm[:, sl], stat2[:, sl], 1.0 / n_feat, 0.0,
                op0=ALU.mult, op1=ALU.add), reads=["stat2"], writes=["statm"])
            k.op("vector", lambda e, sl=sl: e.tensor_scalar(
                stat[:, sl], stat[:, sl], 1.0 / n_feat, 0.0,
                op0=ALU.mult, op1=ALU.add), reads=["stat"], writes=["stat"])
            k.op("vector", lambda e, sl=sl: e.tensor_tensor(
                stat2[:, sl], statm[:, sl], statm[:, sl], op=ALU.mult),
                reads=["statm"], writes=["stat2"])
            k.op("vector", lambda e, sl=sl: e.tensor_tensor(
                stat[:, sl], stat[:, sl], stat2[:, sl], op=ALU.subtract),
                reads=["stat", "stat2"], writes=["stat"])
            k.op("vector", lambda e, sl=sl: e.tensor_scalar(
                stat[:, sl], stat[:, sl], EPS, 0.0, op0=ALU.add, op1=ALU.add),
                reads=["stat"], writes=["stat"])
            k.op("scalar", lambda e, sl=sl: e.activation(
                stat2[:, sl], stat[:, sl], AF.Sqrt, bias=0.0, scale=1.0),
                reads=["stat"], writes=["stat2"])
            return k.op("vector", lambda e, sl=sl: e.reciprocal(
                stati[:, sl], stat2[:, sl]), reads=["stat2"], writes=["stati"])

        def transpose_128(src_ap_fn, nchunk, dst_fn, src_reads, dst_writes):
            for c in range(nchunk):
                k.op("tensor", lambda e, c=c: e.transpose(
                    PS_T[:, c * TIL:(c + 1) * TIL], src_ap_fn(c), identb[:]),
                    reads=list(src_reads) + ["identb"], writes=["PST"])
            t = None
            nh = (nchunk + 3) // 4
            for h in range(nh):
                c0, c1 = 4 * h, min(4 * h + 4, nchunk)
                t = k.op("vector", lambda e, h=h, c0=c0, c1=c1: e.tensor_copy(
                    dst_fn(h), PS_T[:, c0 * TIL:c1 * TIL]
                    .rearrange("p (a b) -> p a b", b=TIL)),
                    reads=["PST"], writes=list(dst_writes))
            return t

        _os = os
        _nb = int(_os.environ.get("KBLOCKS", "99"))
        _ph = int(_os.environ.get("KPHASE", "999"))
        blocks = chunks4(0, NT)[:_nb]
        t_x1_st = [None] * NT
        k.op("vector", lambda e: e.memset(state_acc[:], 0.0), writes=["stacc"])
        k.op("vector", lambda e: e.memset(state_sb[:], 0.0), writes=["stsb"])

        for bi, tiles in enumerate(blocks):
            nt_b = len(tiles)
            bcols = nt_b * TIL
            ring = bi % 2
            lo = tiles[0] * TIL

            for j in tiles:
                k.dma("sync", "xw%d" % (j % 2), XW[:, j % XWN, :],
                      xin[j * TIL:(j + 1) * TIL, :], writes=[("XW", j % XWN)])

            ln_stats([(XW[:, j % XWN, :], [("XW", j % XWN)]) for j in tiles])
            for ii, j in enumerate(tiles):
                k.op("vector", lambda e, j=j, ii=ii: e.tensor_scalar(
                    a_tile[:, ii, :], XW[:, j % XWN, :], statm[:, ii:ii + 1],
                    stati[:, ii:ii + 1], op0=ALU.subtract, op1=ALU.mult),
                    reads=[("XW", j % XWN), "statm", "stati"], writes=[("a", ii)])
                transpose_128(
                    lambda c, ii=ii: a_tile[:, ii, c * TIL:(c + 1) * TIL], 8,
                    lambda h, ii=ii, ring=ring: xhT[:, ring, 4 * h:4 * h + 4,
                                                    ii * TIL:(ii + 1) * TIL],
                    [("a", ii)], [("xhT", ring)])

            PSQ1 = [PS_MM[:, 0, :], PS_MM[:, 1, :],
                    PS_S[:, 0:512], PS_S[:, 512:1024]]
            PSQ1K = [("PSMM", 0), ("PSMM", 1), ("PSS", 0), ("PSS", 1)]
            for ni in range(16 if _ph >= 2 else 0):
                wring = ni % 2
                k.dma("gpsimd", "wqk%d" % wring, wqk_buf[:, wring, :, :],
                      w_qkvT_d[:, ni * TIL:(ni + 1) * TIL]
                      .rearrange("(a p) b -> p a b", p=TIL),
                      writes=[("wqk", wring)])
                psr = ni % 4
                for d in range(8):
                    k.op("tensor", lambda e, d=d, wring=wring, psr=psr,
                         bcols=bcols, ring=ring: e.matmul(
                        PSQ1[psr][:, 0:bcols], wqk_buf[:, wring, d, :],
                        xhT[:, ring, d, 0:bcols], start=(d == 0), stop=False),
                        reads=[("wqk", wring), ("xhT", ring)],
                        writes=[PSQ1K[psr]])
                k.op("tensor", lambda e, ni=ni, psr=psr, bcols=bcols, lo=lo:
                     e.matmul(
                    PSQ1[psr][:, 0:bcols], biasq_b[:, ni * TIL:(ni + 1) * TIL],
                    frow_b[:, lo:lo + bcols], start=False, stop=True),
                    reads=["SMBq", "SMBf"], writes=[PSQ1K[psr]])
                if ni < 8:
                    k.op("scalar", lambda e, ni=ni, psr=psr, bcols=bcols:
                         e.activation(
                        qT_blk[:, ni, 0:bcols], PSQ1[psr][:, 0:bcols],
                        AF.Identity, bias=0.0, scale=cst[:, 0:1]),
                        reads=[PSQ1K[psr], "cst"], writes=["qT"])
                else:
                    k.op("vector", lambda e, ni=ni, psr=psr, bcols=bcols:
                         e.tensor_copy(
                        kT_blk[:, ni - 8, 0:bcols], PSQ1[psr][:, 0:bcols]),
                        reads=[PSQ1K[psr]], writes=["kT"])

            for ii, j in enumerate(tiles if _ph >= 3 else []):
                for nh in range(2):
                    psr = (ii * 2 + nh) % 4
                    for d in range(8):
                        k.op("tensor", lambda e, d=d, ii=ii, nh=nh, psr=psr,
                             ring=ring: e.matmul(
                            PSQ1[psr],
                            xhT[:, ring, d, ii * TIL:(ii + 1) * TIL],
                            wv_res[:, d, nh * 512:(nh + 1) * 512],
                            start=(d == 0), stop=False),
                            reads=[("xhT", ring), "wv"], writes=[PSQ1K[psr]])
                    k.op("tensor", lambda e, nh=nh, psr=psr, j=j: e.matmul(
                        PSQ1[psr], frow_b[:, j * TIL:(j + 1) * TIL],
                        biasq_b[:, 2 * D + nh * 512:2 * D + (nh + 1) * 512],
                        start=False, stop=True),
                        reads=["SMBq", "SMBf"], writes=[PSQ1K[psr]])
                    if (ii * 2 + nh) % 2 == 0:
                        k.op("vector", lambda e, ii=ii, nh=nh, psr=psr:
                             e.tensor_copy(
                            v_blk[:, ii, nh * 512:(nh + 1) * 512], PSQ1[psr]),
                            reads=[PSQ1K[psr]], writes=["v"])
                    else:
                        k.op("scalar", lambda e, ii=ii, nh=nh, psr=psr:
                             e.activation(
                            v_blk[:, ii, nh * 512:(nh + 1) * 512], PSQ1[psr],
                            AF.Identity, bias=0.0, scale=1.0),
                            reads=[PSQ1K[psr]], writes=["v"])

            for ii, j in enumerate(tiles if _ph >= 4 else []):
                transpose_128(
                    lambda c, ii=ii: kT_blk[:, c, ii * TIL:(ii + 1) * TIL], 8,
                    lambda h, ii=ii: ktok_blk[:, ii, 4 * h * TIL:(4 * h + 4) * TIL]
                    .rearrange("p (a b) -> p a b", b=TIL),
                    ["kT"], ["ktok"])

            for ii, j in enumerate(tiles if _ph >= 5 else []):
                for g in range(4):
                    psr = g % 2
                    for hh in range(4):
                        h = g * 4 + hh
                        p, b = h // 2, (h % 2) * 64
                        k.op("tensor", lambda e, p=p, b=b, hh=hh, psr=psr,
                             ii=ii: e.matmul(
                            PS_S[:, psr * 512 + hh * TIL:psr * 512 + (hh + 1) * TIL],
                            kT_blk[b:b + 64, p, ii * TIL:(ii + 1) * TIL],
                            qT_blk[b:b + 64, p, ii * TIL:(ii + 1) * TIL],
                            start=True, stop=True),
                            reads=["kT", "qT"], writes=[("PSS", psr)])
                    k.op("vector", lambda e, psr=psr: e.tensor_tensor(
                        sT_m[:], PS_S[:, psr * 512:psr * 512 + 512]
                        .rearrange("p (a b) -> p a b", b=TIL),
                        causalT[:].unsqueeze(1).broadcast_to([TIL, 4, TIL]),
                        op=ALU.mult),
                        reads=[("PSS", psr), "causalT"], writes=["sT"])
                    for hh in range(4):
                        h = g * 4 + hh
                        p, b = h // 2, (h % 2) * 64
                        bank, col = h // 8, (h % 8) * 64
                        k.op("tensor", lambda e, h=h, hh=hh, bank=bank, col=col,
                             ii=ii: e.matmul(
                            PS_O[:, bank, col:col + 64], sT_m[:, hh, :],
                            v_blk[:, ii, h * 64:(h + 1) * 64],
                            start=True, stop=False),
                            reads=["sT", "v", "stsb"], writes=[("PSO", bank)])
                        k.op("tensor", lambda e, h=h, p=p, b=b, bank=bank,
                             col=col, ii=ii: e.matmul(
                            PS_O[:, bank, col:col + 64],
                            qT_blk[b:b + 64, p, ii * TIL:(ii + 1) * TIL],
                            state_sb[b:b + 64, p * 64:(p + 1) * 64],
                            start=False, stop=True),
                            reads=["qT", "stsb"], writes=[("PSO", bank)])

                for h in range(16):
                    p, b = h // 2, (h % 2) * 64
                    k.op("tensor", lambda e, h=h, p=p, b=b, ii=ii: e.matmul(
                        PS_ST[b:b + 64, p * 64:(p + 1) * 64],
                        ktok_blk[:, ii, h * 64:(h + 1) * 64],
                        v_blk[:, ii, h * 64:(h + 1) * 64],
                        start=True, stop=True, tile_position=(0, b)),
                        reads=["ktok", "v"], writes=["PSST"])
                k.op("vector", lambda e: e.tensor_tensor(
                    state_acc[:], state_acc[:], PS_ST[:], op=ALU.add),
                    reads=["PSST", "stacc"], writes=["stacc"])
                k.op("vector", lambda e: e.tensor_copy(state_sb[:], state_acc[:]),
                    reads=["stacc"], writes=["stsb"])

                for bank in range(2):
                    k.op("scalar", lambda e, bank=bank: e.activation(
                        o_tok[:, bank * 512:(bank + 1) * 512], PS_O[:, bank, :],
                        AF.Identity, bias=0.0, scale=1.0),
                        reads=[("PSO", bank)], writes=["otok"])
                transpose_128(
                    lambda c: o_tok[:, c * TIL:(c + 1) * TIL], 8,
                    lambda h: oT_t[:, 4 * h:4 * h + 4, :], ["otok"], ["oT"])

                for nh in range(2 if _ph >= 6 else 0):
                    psr = nh
                    for c in range(8):
                        k.op("tensor", lambda e, c=c, nh=nh, psr=psr: e.matmul(
                            PS_MM[:, psr, :], oT_t[:, c, :],
                            woT_res[:, c, nh * 512:(nh + 1) * 512],
                            start=(c == 0), stop=False),
                            reads=["oT", "woT"], writes=[("PSMM", psr)])
                    k.op("tensor", lambda e, nh=nh, psr=psr, j=j: e.matmul(
                        PS_MM[:, psr, :], frow_b[:, j * TIL:(j + 1) * TIL],
                        biaso_b[:, nh * 512:(nh + 1) * 512],
                        start=False, stop=True),
                        reads=["SMBf", "SMBo"], writes=[("PSMM", psr)])

                if _ph < 7:
                    continue
                k.op("scalar", lambda e: e.activation(
                    junk[:, 0:512], PS_MM[:, 0, :], AF.Square, bias=0.0,
                    scale=1.0, accum_out=stat[:, 8:9]),
                    reads=[("PSMM", 0)], writes=["junk", "stat"])
                k.op("scalar", lambda e: e.activation(
                    junk[:, 512:1024], PS_MM[:, 1, :], AF.Square, bias=0.0,
                    scale=1.0, accum_out=stat[:, 9:10]),
                    reads=[("PSMM", 1)], writes=["junk", "stat"])
                if _ph < 8:
                    continue
                k.op("vector", lambda e: e.reduce_sum(
                    stat2[:, 8:9].unsqueeze(2), PS_MM[:, 0, :].unsqueeze(1),
                    axis=AX.X), reads=[("PSMM", 0)], writes=["stat2"])
                k.op("vector", lambda e: e.reduce_sum(
                    stat2[:, 9:10].unsqueeze(2), PS_MM[:, 1, :].unsqueeze(1),
                    axis=AX.X), reads=[("PSMM", 1)], writes=["stat2"])
                if _ph < 82:
                    continue
                k.op("vector", lambda e: e.tensor_tensor(
                    statm[:, 8:9], stat2[:, 8:9], stat2[:, 9:10], op=ALU.add),
                    reads=["stat2"], writes=["statm"])
                k.op("vector", lambda e: e.tensor_scalar(
                    statm[:, 8:9], statm[:, 8:9], 1.0 / D, 0.0,
                    op0=ALU.mult, op1=ALU.add), reads=["statm"], writes=["statm"])
                k.op("vector", lambda e: e.tensor_tensor(
                    stat[:, 8:9], stat[:, 8:9], stat[:, 9:10], op=ALU.add),
                    reads=["stat"], writes=["stat"])
                k.op("vector", lambda e: e.tensor_scalar(
                    stat[:, 8:9], stat[:, 8:9], 1.0 / D, 0.0,
                    op0=ALU.mult, op1=ALU.add), reads=["stat"], writes=["stat"])
                k.op("vector", lambda e: e.tensor_tensor(
                    stat2[:, 10:11], statm[:, 8:9], statm[:, 8:9], op=ALU.mult),
                    reads=["statm"], writes=["stat2"])
                k.op("vector", lambda e: e.tensor_tensor(
                    stat[:, 8:9], stat[:, 8:9], stat2[:, 10:11],
                    op=ALU.subtract), reads=["stat", "stat2"], writes=["stat"])
                k.op("vector", lambda e: e.tensor_scalar(
                    stat[:, 8:9], stat[:, 8:9], EPS, 0.0, op0=ALU.add,
                    op1=ALU.add), reads=["stat"], writes=["stat"])
                if _ph < 84:
                    continue
                k.op("scalar", lambda e: e.activation(
                    stat2[:, 8:9], stat[:, 8:9], AF.Sqrt, bias=0.0, scale=1.0),
                    reads=["stat"], writes=["stat2"])
                k.op("vector", lambda e: e.reciprocal(
                    stati[:, 8:9], stat2[:, 8:9]), reads=["stat2"],
                    writes=["stati"])
                if _ph < 90:
                    continue
                k.op("vector", lambda e: e.tensor_scalar(
                    ynorm[:, 0:512], PS_MM[:, 0, :], statm[:, 8:9],
                    stati[:, 8:9], op0=ALU.subtract, op1=ALU.mult),
                    reads=[("PSMM", 0), "statm", "stati"], writes=["ynorm"])
                k.op("vector", lambda e: e.tensor_scalar(
                    ynorm[:, 512:1024], PS_MM[:, 1, :], statm[:, 8:9],
                    stati[:, 8:9], op0=ALU.subtract, op1=ALU.mult),
                    reads=[("PSMM", 1), "statm", "stati"], writes=["ynorm"])
                k.op("vector", lambda e, j=j: e.tensor_tensor(
                    XW[:, j % XWN, :], XW[:, j % XWN, :], ynorm[:], op=ALU.add),
                    reads=["ynorm", ("XW", j % XWN)], writes=[("XW", j % XWN)])
                if _ph < 100:
                    continue
                t_x1_st[j] = k.dma("gpsimd", "xst", x1_d[j * TIL:(j + 1) * TIL, :],
                                   XW[:, j % XWN, :], reads=[("XW", j % XWN)],
                                   writes=[("x1", j)])

        barrier = k.barrier

        def wffn_rhs(d, part):
            qq = d * 3 + part
            if qq < 16:
                return wffn_v[:, qq * 512:(qq + 1) * 512]
            return wffn2[:, (qq - 16) * 512:(qq - 16 + 1) * 512]

        def wffn_keys():
            return ["wv", "woT"]

        def wo_rhs(c, nh):
            off = 4096 + c * 1024 + nh * 512
            return wffn2[:, off:off + 512]

        Pm = PTA[:, 9472:10112]

        def prep_tile(ind, in_done, j, do_ln):
            if do_ln:
                k.dma("sync", "xw%d" % (j % 2), XW[:, j % XWN, :],
                      ind[j * TIL:(j + 1) * TIL, :],
                      waits=[in_done[j]] if in_done else (),
                      writes=[("XW", j % XWN)])
                ln_stats([(XW[:, j % XWN, :], [("XW", j % XWN)])])
                k.op("vector", lambda e, j=j: e.tensor_scalar(
                    A2W[:, j % 4, :], XW[:, j % XWN, :], statm[:, 0:1],
                    stati[:, 0:1], op0=ALU.subtract, op1=ALU.mult),
                    reads=[("XW", j % XWN), "statm", "stati"],
                    writes=[("A2W", j % 4)])
            else:
                k.dma("sync", "xw%d" % (j % 2), A2W[:, j % 4, :],
                      ind[j * TIL:(j + 1) * TIL, :],
                      waits=[in_done[j]] if in_done else (),
                      writes=[("A2W", j % 4)])
            k.op("scalar", lambda e, j=j: e.activation(
                o_tok[:], A2W[:, j % 4, :], AF.Identity, bias=0.0, scale=1.0),
                reads=[("A2W", j % 4)], writes=["otok"])
            for c in range(8):
                k.op("tensor", lambda e, c=c: e.transpose(
                    PS_T[:, c * TIL:(c + 1) * TIL],
                    o_tok[:, c * TIL:(c + 1) * TIL], identb[:]),
                    reads=["otok", "identb"], writes=["PST"])
            hr = hT_ring[:, j % 2, :].rearrange("p (a b) -> p a b", b=129)
            hprev = hT_ring[:, (j + 1) % 2, :].rearrange("p (a b) -> p a b", b=129)
            for h in range(2):
                k.op("vector", lambda e, h=h, hr=hr: e.tensor_copy(
                    hr[:, 4 * h:4 * h + 4, 1:129],
                    PS_T[:, 4 * h * TIL:(4 * h + 4) * TIL]
                    .rearrange("p (a b) -> p a b", b=TIL)),
                    reads=["PST"], writes=[("hT", j % 2)])
            k.op("vector", lambda e, hr=hr, hprev=hprev: e.tensor_copy(
                hr[:, :, 0:1], hprev[:, :, 128:129]),
                reads=[("hT", (j + 1) % 2)], writes=[("hT", j % 2)])

        def attn_tile(l, j, ii, out_done):
            key_tiles = list(range(j - 4, j + 1))
            segs = []
            poff = 0
            run_start = key_tiles[0] % 8
            run_len = 1
            for t in key_tiles[1:]:
                if t % 8 == (run_start + run_len) % 8 and run_start + run_len < 8:
                    run_len += 1
                else:
                    segs.append((poff, run_start * TIL, run_len * TIL))
                    poff += run_len * TIL
                    run_start = t % 8
                    run_len = 1
            segs.append((poff, run_start * TIL, run_len * TIL))
            sub = []
            for (po, ro, w) in segs:
                while w > 0:
                    lim = 512 - (po % 512)
                    ww = min(w, lim, 512)
                    sub.append((po, ro, ww))
                    po, ro, w = po + ww, ro + ww, w - ww
            if 17 <= j <= 20:
                k.dma("gpsimd", "cstg", maskcur[:],
                      mask_sp_d[j - 17], writes=["msp"])
            mk = maskcur[:] if 17 <= j <= 20 else mask_base[:]
            mkk = "msp" if 17 <= j <= 20 else "mb"
            PSMMf = PS_MM[:].rearrange("p a b -> p (a b)")
            slots = [PS_S, PSMMf]
            skeys = [[("PSS", 0), ("PSS", 1)], [("PSMM", 0), ("PSMM", 1)]]

            def s_mms(h):
                sl_ = slots[h % 2]
                for (po, ro, w) in sub:
                    p, bb = h // 2, (h % 2 == 1) * 0 + (h % 2) * 0
                    p = h // 2
                    bb = (h % 2) * 64
                    k.op("tensor", lambda e, p=p, bb=bb, po=po, ro=ro, w=w,
                         ii=ii, sl_=sl_: e.matmul(
                        sl_[:, po:po + w],
                        qT2[bb:bb + 64, ii, p * TIL:(p + 1) * TIL],
                        kT_roll[bb:bb + 64, p, ro:ro + w],
                        start=True, stop=True),
                        reads=["qT2"] + [("ktr", s % 8) for s in key_tiles],
                        writes=skeys[h % 2])

            def v_chain(h):
                sl_ = slots[h % 2]
                k.op("vector", lambda e, mk=mk, sl_=sl_: e.tensor_tensor(
                    Sm[:], sl_[:, 0:640], mk, op=ALU.add),
                    reads=skeys[h % 2] + [mkk], writes=["Sm"])
                k.op("vector", lambda e: e.reduce_max(
                    stat2[:, 40:41], Sm[:], axis=AX.X, negate=True),
                    reads=["Sm"], writes=["stat2"])
                k.op("scalar", lambda e: e.activation(
                    Pm[:], Sm[:], AF.Exp, bias=stat2[:, 40:41], scale=1.0,
                    accum_out=stat[:, 40:41]),
                    reads=["Sm", "stat2"], writes=["Pm", "stat"])
                k.op("vector", lambda e, h=h: e.reciprocal(
                    stati[:, 40 + h:41 + h], stat[:, 40:41]),
                    reads=["stat"], writes=["stati"])

            def pv_chain(h):
                for c in range(5):
                    k.op("tensor", lambda e, c=c: e.transpose(
                        PS_T[:, c * TIL:(c + 1) * TIL],
                        Pm[:, c * TIL:(c + 1) * TIL], identb[:]),
                        reads=["Pm", "identb"], writes=["PST"])
                k.op("vector", lambda e: e.tensor_copy(
                    PT5[:], PS_T[:, 0:640].rearrange("p (a b) -> p a b", b=TIL)),
                    reads=["PST"], writes=["PT5"])
                for t5, jt in enumerate(key_tiles):
                    k.op("tensor", lambda e, t5=t5, jt=jt, h=h, j=j: e.matmul(
                        PS_O[:, j % 2, h * 64:(h + 1) * 64], PT5[:, t5, :],
                        v_roll[:, jt % 8, h * 64:(h + 1) * 64],
                        start=(t5 == 0), stop=(t5 == 4)),
                        reads=["PT5", ("vr", jt % 8)], writes=[("PSO", j % 2)])

            for h in range(9):
                if h < 8:
                    s_mms(h)
                if h >= 1:
                    v_chain(h - 1)
                    pv_chain(h - 1)
            for h in range(8):
                k.op("scalar", lambda e, h=h, j=j: e.activation(
                    ob2[:, h * 64:(h + 1) * 64],
                    PS_O[:, j % 2, h * 64:(h + 1) * 64],
                    AF.Identity, bias=0.0, scale=stati[:, 40 + h:41 + h]),
                    reads=[("PSO", j % 2), "stati"], writes=["ob2"])
            for c in range(4):
                k.op("tensor", lambda e, c=c: e.transpose(
                    PS_T[:, c * TIL:(c + 1) * TIL],
                    ob2[:, c * TIL:(c + 1) * TIL], identb[:]),
                    reads=["ob2", "identb"], writes=["PST"])
            k.op("vector", lambda e: e.tensor_copy(
                oT2[:], PS_T[:, 0:512].rearrange("p (a b) -> p a b", b=TIL)),
                reads=["PST"], writes=["oT2"])
            for nh in range(2):
                for c in range(4):
                    k.op("tensor", lambda e, c=c, nh=nh: e.matmul(
                        PS_MM[:, nh, :], oT2[:, c, :], wo_rhs(c, nh),
                        start=(c == 0), stop=(c == 3)),
                        reads=["oT2", "woT"], writes=[("PSMM", nh)])
                k.op("vector", lambda e, nh=nh, j=j: e.scalar_tensor_tensor(
                    ynorm[:, nh * 512:(nh + 1) * 512], PS_MM[:, nh, :],
                    fpart[:, j:j + 1], A2W[:, j % 4, nh * 512:(nh + 1) * 512],
                    op0=ALU.mult, op1=ALU.add),
                    reads=[("PSMM", nh), "fpart", ("A2W", j % 4)],
                    writes=["ynorm"])
            out_done[j] = k.dma("gpsimd", "xst", y_d[j * TIL:(j + 1) * TIL, :],
                                ynorm[:], reads=["ynorm"], writes=[("y", j)])

        def emit_l0ffn_attn(l, in_stream, in_done, kv_lo, q_lo, do_ln):
            ind = streams[in_stream]
            y_done = [None] * NT
            for qq in range(24):
                dst = wffn_rhs(qq // 3, qq % 3)
                d, part = qq // 3, qq % 3
                k.dma("gpsimd", "wres%d" % (qq % 2), dst,
                      w_ffn_qkv_d[l, d * TIL:(d + 1) * TIL,
                                  part * 512:(part + 1) * 512],
                      writes=wffn_keys())
            for c in range(4):
                for nh in range(2):
                    k.dma("gpsimd", "wres%d" % ((c * 2 + nh) % 2),
                          wo_rhs(c, nh),
                          w_ffn_wo_d[l, c * TIL:(c + 1) * TIL,
                                     nh * 512:(nh + 1) * 512], writes=["woT"])
            k.dma("sync", "cst", qs4[:], qs4_d[l], writes=["qs4"])
            k.dma("gpsimd", "cstg", mask_base[:], mask_base_d, writes=["mb"])


            prep_tile(ind, in_done, kv_lo - 1, do_ln)
            for tiles in chunks4(kv_lo, NT):
                nt_b = len(tiles)
                i0t = tiles[0]
                k.dma("sync", "tab", tb_raw[:, 0:nt_b, :],
                      tab4_d[i0t:i0t + nt_b].rearrange("a p b -> p a b"),
                      writes=["tbr"])
                for t4 in range(4):
                    rollix = t4 % 2
                    qk = 0 if t4 < 2 else 2
                    k.op("vector", lambda e, t4=t4, rollix=rollix, qk=qk,
                         nt_b=nt_b: e.tensor_tensor(
                        tb_f[:, 0:nt_b, t4 * 64:(t4 + 1) * 64],
                        tb_raw[:, 0:nt_b, t4 * 64:(t4 + 1) * 64],
                        qs4[:, qk + rollix, :].unsqueeze(1)
                        .broadcast_to([TIL, nt_b, 64]), op=ALU.mult),
                        reads=["tbr", "qs4"], writes=["tbf"])
                PSQ = [PS_MM[:, 0, :], PS_MM[:, 1, :],
                       PS_S[:, 0:512], PS_S[:, 512:1024]]
                PSQK = [("PSMM", 0), ("PSMM", 1), ("PSS", 0), ("PSS", 1)]
                for ii, j in enumerate(tiles):
                    prep_tile(ind, in_done, j, do_ln)
                    hr = hT_ring[:, j % 2, :].rearrange("p (a b) -> p a b", b=129)
                    for part in range(3):
                        psr = (ii * 3 + part) % 4
                        for d in range(8):
                            sh = 1 if d < 4 else 0
                            k.op("tensor", lambda e, d=d, part=part, psr=psr,
                                 sh=sh, hr=hr: e.matmul(
                                PSQ[psr], hr[:, d, sh:sh + TIL],
                                wffn_rhs(d, part), start=(d == 0),
                                stop=(d == 7)),
                                reads=[("hT", j % 2)] + wffn_keys(),
                                writes=[PSQK[psr]])
                        if part == 0:
                            k.op("vector", lambda e, ii=ii, psr=psr:
                                 e.tensor_copy(qraw[:, ii, :], PSQ[psr]),
                                 reads=[PSQK[psr]], writes=["qraw"])
                        elif part == 1:
                            k.op("scalar", lambda e, ii=ii, psr=psr:
                                 e.activation(kraw[:, ii, :], PSQ[psr],
                                              AF.Identity, bias=0.0,
                                              scale=1.0),
                                 reads=[PSQK[psr]], writes=["kraw"])
                        else:
                            k.op("vector", lambda e, j=j, psr=psr:
                                 e.tensor_copy(v_roll[:, j % 8, :], PSQ[psr]),
                                 reads=[PSQK[psr]], writes=[("vr", j % 8)])

                for side in range(2):
                    raw = qraw if side == 0 else kraw
                    rk = "qraw" if side == 0 else "kraw"
                    fb = qfb if side == 0 else kfb
                    fbk = "qfb" if side == 0 else "kfb"
                    toff = 0 if side == 0 else 128
                    sl2 = slice(0, nt_b * 8)
                    k.op("scalar", lambda e, raw=raw, nt_b=nt_b: e.activation(
                        junk2v[:, 0:nt_b, :], raw[:, 0:nt_b, :], AF.Square,
                        bias=0.0, scale=1.0), reads=[rk], writes=["junk2"])
                    k.op("vector", lambda e, nt_b=nt_b, sl2=sl2: e.reduce_sum(
                        stat[:, sl2].rearrange("p (a b) -> p a b", b=8)
                        .unsqueeze(3),
                        junk2v[:, 0:nt_b, :].rearrange("p a (b c) -> p a b c",
                                                       c=64), axis=AX.X),
                        reads=["junk2"], writes=["stat"])
                    k.op("scalar", lambda e, sl2=sl2: e.activation(
                        stat2[:, sl2], stat[:, sl2], AF.Sqrt, bias=0.0,
                        scale=1.0), reads=["stat"], writes=["stat2"])
                    k.op("vector", lambda e, sl2=sl2: e.tensor_scalar(
                        stat2[:, sl2], stat2[:, sl2], 1e-12, 0.0,
                        op0=ALU.max, op1=ALU.add),
                        reads=["stat2"], writes=["stat2"])
                    k.op("vector", lambda e, sl2=sl2: e.reciprocal(
                        stati[:, sl2], stat2[:, sl2]),
                        reads=["stat2"], writes=["stati"])
                    k.op("vector", lambda e, nt_b=nt_b: e.tensor_copy(
                        junk3v[:, 0:nt_b, :].rearrange("p a (b c) -> p a b c",
                                                       c=64),
                        stati[:, 0:nt_b * 8].rearrange("p (a b) -> p a b", b=8)
                        .unsqueeze(3).broadcast_to([TIL, nt_b, 8, 64])),
                        reads=["stati"], writes=["junk3"])
                    k.op("vector", lambda e, raw=raw, nt_b=nt_b: e.tensor_tensor(
                        raw[:, 0:nt_b, :], raw[:, 0:nt_b, :],
                        junk3v[:, 0:nt_b, :], op=ALU.mult),
                        reads=[rk, "junk3"], writes=[rk])
                    vw = lambda a: a.rearrange("p t (h d) -> p t h d", d=64)
                    k.op("vector", lambda e, raw=raw, nt_b=nt_b, toff=toff,
                         vw=vw: e.tensor_tensor(
                        vw(junk2v[:, 0:nt_b, :])[:, :, :, 0:32],
                        vw(raw[:, 0:nt_b, :])[:, :, :, 32:64],
                        tb_f[:, 0:nt_b, toff + 64:toff + 96].unsqueeze(2)
                        .broadcast_to([TIL, nt_b, 8, 32]), op=ALU.mult),
                        reads=[rk, "tbf"], writes=["junk2"])
                    k.op("vector", lambda e, raw=raw, nt_b=nt_b, toff=toff,
                         vw=vw: e.tensor_tensor(
                        vw(junk2v[:, 0:nt_b, :])[:, :, :, 32:64],
                        vw(raw[:, 0:nt_b, :])[:, :, :, 0:32],
                        tb_f[:, 0:nt_b, toff + 96:toff + 128].unsqueeze(2)
                        .broadcast_to([TIL, nt_b, 8, 32]), op=ALU.mult),
                        reads=[rk, "tbf"], writes=["junk2"])
                    k.op("vector", lambda e, raw=raw, nt_b=nt_b, toff=toff,
                         vw=vw: e.tensor_tensor(
                        vw(junk3v[:, 0:nt_b, :]), vw(raw[:, 0:nt_b, :]),
                        tb_f[:, 0:nt_b, toff + 0:toff + 64].unsqueeze(2)
                        .broadcast_to([TIL, nt_b, 8, 64]), op=ALU.mult),
                        reads=[rk, "tbf"], writes=["junk3"])
                    k.op("vector", lambda e, fb=fb, nt_b=nt_b: e.tensor_tensor(
                        fb[:, 0:nt_b, :], junk2v[:, 0:nt_b, :],
                        junk3v[:, 0:nt_b, :], op=ALU.add),
                        reads=["junk2", "junk3"], writes=[fbk])

                for ii, j in enumerate(tiles):
                    transpose_128(
                        lambda c, ii=ii: qfb[:, ii, c * TIL:(c + 1) * TIL], 4,
                        lambda h, ii=ii: qT2[:, ii, :]
                        .rearrange("p (a b) -> p a b", b=TIL),
                        ["qfb"], ["qT2"])
                    transpose_128(
                        lambda c, ii=ii: kfb[:, ii, c * TIL:(c + 1) * TIL], 4,
                        lambda h, j=j: kT_roll[:, :, (j % 8) * TIL:
                                               (j % 8 + 1) * TIL],
                        ["kfb"], [("ktr", j % 8)])

                for ii, j in enumerate(tiles):
                    if j >= q_lo:
                        attn_tile(l, j, ii, y_done)
            return y_done

        def emit_l0ffn_ffn(l, y_done, out_stream, q_lo):
            outd = streams[out_stream]
            out_done = [None] * NT
            k.dma("gpsimd", "cstg", gam_cur[:], gam_d[l], writes=["gam"])
            k.dma("gpsimd", "cstg", bet_cur[:], bet_d[l], writes=["bet"])
            k.dma("sync", "cst", b1a_col[:], b1a_d[l], writes=["b1a"])
            k.dma("sync", "cst", b1g_col[:], b1g_d[l], writes=["b1g"])
            k.dma("gpsimd", "cstg", biaso_b, b2_d[l], writes=["SMBo"])
            for tiles in chunks4(q_lo, NT):
                nt_b = len(tiles)
                for ii, j in enumerate(tiles):
                    k.dma("sync", "xw%d" % (j % 2), XW[:, j % XWN, :],
                          y_d[j * TIL:(j + 1) * TIL, :],
                          waits=[y_done[j]], writes=[("XW", j % XWN)])
                for ii, j in enumerate(tiles):
                    k.dma("sync", "hsh%d" % (ii % 2), A2W[:, ii, 0:512],
                          XW[:, j % XWN, 0:512], reads=[("XW", j % XWN)],
                          writes=[("A2W", ii)])
                    if j == q_lo:
                        k.dma("sync", "hsh%d" % (ii % 2), A2W[0:1, ii, 512:1024],
                              XW[0:1, j % XWN, 512:1024], reads=[("XW", j % XWN)],
                              writes=[("A2W", ii)])
                    else:
                        k.dma("sync", "hsh%d" % (ii % 2), A2W[0:1, ii, 512:1024],
                              XW[127:128, (j - 1) % XWN, 512:1024],
                              reads=[("XW", (j - 1) % XWN)], writes=[("A2W", ii)])
                    k.dma("sync", "hsh%d" % (ii % 2), A2W[1:128, ii, 512:1024],
                          XW[0:127, j % XWN, 512:1024], reads=[("XW", j % XWN)],
                          writes=[("A2W", ii)])
                ln_stats([(A2W[:, ii, :], [("A2W", ii)])
                          for ii in range(nt_b)])
                for ii, j in enumerate(tiles):
                    k.op("vector", lambda e, ii=ii: e.tensor_scalar(
                        junk[:], A2W[:, ii, :], statm[:, ii:ii + 1],
                        stati[:, ii:ii + 1], op0=ALU.subtract, op1=ALU.mult),
                        reads=[("A2W", ii), "statm", "stati"], writes=["junk"])
                    k.op("vector", lambda e: e.tensor_tensor(
                        junk[:], junk[:], gam_cur[:], op=ALU.mult),
                        reads=["junk", "gam"], writes=["junk"])
                    k.op("vector", lambda e: e.tensor_tensor(
                        o_tok[:], junk[:], bet_cur[:], op=ALU.add),
                        reads=["junk", "bet"], writes=["otok"])
                    for c in range(8):
                        k.op("tensor", lambda e, c=c: e.transpose(
                            PS_T[:, c * TIL:(c + 1) * TIL],
                            o_tok[:, c * TIL:(c + 1) * TIL], identb[:]),
                            reads=["otok", "identb"], writes=["PST"])
                    for h in range(2):
                        k.op("vector", lambda e, h=h, ii=ii: e.tensor_copy(
                            hTf[:, 4 * h:4 * h + 4, ii * TIL:(ii + 1) * TIL],
                            PS_T[:, 4 * h * TIL:(4 * h + 4) * TIL]
                            .rearrange("p (a b) -> p a b", b=TIL)),
                            reads=["PST"],
                            writes=["hTf", "kfb", "junk2"])
                PT_KEYS = ["pT", "junk3", "tbr", "tbf", "Sm", "Pm"]
                bcol = nt_b * TIL
                for cc in range(22):
                    wdt = 42 if cc == 21 else TIL
                    wr = cc % 2
                    k.dma("sync", "w1a%d" % wr, w1a_buf[:, wr, :, 0:wdt],
                          w1_st[l, :, cc * TIL:cc * TIL + wdt]
                          .rearrange("(a p) b -> p a b", p=TIL),
                          writes=[("wqk", wr)])
                    k.dma("sync", "w1g%d" % wr, w1g_buf[:, wr, :, 0:wdt],
                          w1_st[l, :, 2730 + cc * TIL:2730 + cc * TIL + wdt]
                          .rearrange("(a p) b -> p a b", p=TIL),
                          writes=[("w1g", wr)])
                    pa = PS_MM[:, 0, :] if cc % 2 == 0 else PS_MM[:, 1, :]
                    pak = ("PSMM", 0) if cc % 2 == 0 else ("PSMM", 1)
                    pg = PS_S[:, 0:512] if cc % 2 == 0 else PS_S[:, 512:1024]
                    pgk = ("PSS", 0) if cc % 2 == 0 else ("PSS", 1)
                    for d in range(8):
                        k.op("tensor", lambda e, d=d, cc=cc, wr=wr, wdt=wdt,
                             bcol=bcol, pa=pa: e.matmul(
                            pa[0:wdt, 0:bcol],
                            w1a_buf[:, wr, d, 0:wdt], hTf[:, d, 0:bcol],
                            start=(d == 0), stop=(d == 7)),
                            reads=[("wqk", wr), "hTf"], writes=[pak])
                        k.op("tensor", lambda e, d=d, cc=cc, wr=wr, wdt=wdt,
                             bcol=bcol, pg=pg: e.matmul(
                            pg[0:wdt, 0:bcol], w1g_buf[:, wr, d, 0:wdt],
                            hTf[:, d, 0:bcol], start=(d == 0), stop=(d == 7)),
                            reads=[("w1g", wr), "hTf"],
                            writes=[pgk])
                    k.op("scalar", lambda e, cc=cc, wdt=wdt, bcol=bcol, pg=pg:
                         e.activation(
                        junk[0:wdt, 0:bcol], pg[0:wdt, 0:bcol], AF.Gelu,
                        bias=b1g_col[0:wdt, cc:cc + 1], scale=1.0),
                        reads=[pgk, "b1g"], writes=["junk"])
                    k.op("vector", lambda e, cc=cc, wdt=wdt, bcol=bcol, pa=pa:
                         e.scalar_tensor_tensor(
                        pT_g[0:wdt, cc, 0:bcol], pa[0:wdt, 0:bcol],
                        b1a_col[0:wdt, cc:cc + 1], junk[0:wdt, 0:bcol],
                        op0=ALU.add, op1=ALU.mult),
                        reads=[pak, "b1a", "junk"], writes=PT_KEYS)
                PSB = [PS_MM[:, 0, :], PS_MM[:, 1, :],
                       PS_S[:, 0:512], PS_S[:, 512:1024]]
                PSK = [("PSMM", 0), ("PSMM", 1), ("PSS", 0), ("PSS", 1)]
                for half in range(2):
                    for cc in range(22):
                        wdt = 42 if cc == 21 else TIL
                        wr = cc % 2
                        k.dma("sync", "w2%d" % wr, w2_buf[0:wdt, wr, :],
                              w2_st[l, cc * TIL:cc * TIL + wdt,
                                    half * 512:(half + 1) * 512],
                              writes=[("w2", wr)])
                        for ti in range(nt_b):
                            k.op("tensor", lambda e, cc=cc, wr=wr, wdt=wdt,
                                 ti=ti: e.matmul(
                                PSB[ti], pT_g[0:wdt, cc, ti * TIL:(ti + 1) * TIL],
                                w2_buf[0:wdt, wr, :],
                                start=(cc == 0), stop=False),
                                reads=[("w2", wr), "pT"],
                                writes=[PSK[ti]])
                    for ti, j in enumerate(tiles):
                        k.op("tensor", lambda e, ti=ti, j=j, half=half: e.matmul(
                            PSB[ti], frow_b[:, j * TIL:(j + 1) * TIL],
                            biaso_b[:, half * 512:(half + 1) * 512],
                            start=False, stop=True),
                            reads=["SMBf", "SMBo"], writes=[PSK[ti]])
                        k.op("vector", lambda e, ti=ti, j=j, half=half:
                             e.scalar_tensor_tensor(
                            A2W[:, ti, half * 512:(half + 1) * 512], PSB[ti],
                            fpart[:, j:j + 1],
                            XW[:, j % XWN, half * 512:(half + 1) * 512],
                            op0=ALU.mult, op1=ALU.add),
                            reads=[PSK[ti], "fpart", ("XW", j % XWN)],
                            writes=[("A2W", ti)])
                for ti, j in enumerate(tiles):
                    out_done[j] = k.dma(
                        "sync", "xst", outd[j * TIL:(j + 1) * TIL, :],
                        A2W[:, ti, :], reads=[("A2W", ti)],
                        writes=[(out_stream, j)])
            return out_done

        def emit_l1_attn(in_done):
            ind = streams["x2"]
            outd = streams["x3"]
            out_done = [None] * NT
            kT1 = v_blk[:].rearrange("p a b -> p (a b)").rearrange(
                "p (a b) -> p a b", b=512)
            v1 = ktok_blk
            vprev = a_tile[:, 0, :]
            kTprev = a_tile[:, 1, :].rearrange("p (a b) -> p a b", b=TIL)
            for a in range(8):
                k.dma("gpsimd", "wres%d" % (a % 2), wv_res[:, a, :],
                      w_l1v_d[a * TIL:(a + 1) * TIL, :], writes=["wv"])
            k.dma("gpsimd", "cstg", mask_base[:, 0:256], m1_base_d,
                  writes=["mb"])
            k.dma("gpsimd", "cstg", maskcur[:, 0:256], m1_sp_d, writes=["msp"])
            blocks5 = chunks4(16, NT)
            for bi, tiles in enumerate(blocks5):
                nt_b = len(tiles)
                bcols = nt_b * TIL
                ring = bi % 2
                for ii, j in enumerate(tiles):
                    k.dma("sync", "xw%d" % (j % 2), XW[:, j % XWN, :],
                          ind[j * TIL:(j + 1) * TIL, :],
                          waits=[in_done[j]], writes=[("XW", j % XWN)])
                ln_stats([(XW[:, j % XWN, :], [("XW", j % XWN)])
                          for j in tiles])
                for ii, j in enumerate(tiles):
                    k.op("scalar", lambda e, j=j, ii=ii: e.activation(
                        o_tok[:], XW[:, j % XWN, :], AF.Identity,
                        bias=statm[:, ii:ii + 1], scale=stati[:, ii:ii + 1]),
                        reads=[("XW", j % XWN), "statm", "stati"],
                        writes=["otok"]) if False else None
                    k.op("vector", lambda e, j=j, ii=ii: e.tensor_scalar(
                        junk[:], XW[:, j % XWN, :], statm[:, ii:ii + 1],
                        stati[:, ii:ii + 1], op0=ALU.subtract, op1=ALU.mult),
                        reads=[("XW", j % XWN), "statm", "stati"],
                        writes=["junk"])
                    k.op("scalar", lambda e: e.activation(
                        o_tok[:], junk[:], AF.Identity, bias=0.0, scale=1.0),
                        reads=["junk"], writes=["otok"])
                    transpose_128(
                        lambda c: o_tok[:, c * TIL:(c + 1) * TIL], 8,
                        lambda h, ii=ii, ring=ring: xhT[:, ring, 4 * h:4 * h + 4,
                                                        ii * TIL:(ii + 1) * TIL],
                        ["otok"], [("xhT", ring)])
                for ni in range(16):
                    wring = ni % 2
                    wd = w_l1q_d if ni < 8 else w_l1k_d
                    nc_ = ni % 8
                    k.dma("gpsimd", "wqk%d" % wring, wqk_buf[:, wring, :, :],
                          wd[:, nc_ * TIL:(nc_ + 1) * TIL]
                          .rearrange("(a p) b -> p a b", p=TIL),
                          writes=[("wqk", wring)])
                    psr = ni % 2
                    for d in range(8):
                        k.op("tensor", lambda e, d=d, wring=wring, psr=psr,
                             bcols=bcols, ring=ring: e.matmul(
                            PS_MM[:, psr, 0:bcols], wqk_buf[:, wring, d, :],
                            xhT[:, ring, d, 0:bcols], start=(d == 0),
                            stop=(d == 7)),
                            reads=[("wqk", wring), ("xhT", ring)],
                            writes=[("PSMM", psr)])
                    if ni < 8:
                        k.op("scalar", lambda e, ni=ni, psr=psr, bcols=bcols:
                             e.activation(
                            qT_blk[:, ni, 0:bcols], PS_MM[:, psr, 0:bcols],
                            AF.Identity, bias=0.0, scale=cst[:, 0:1]),
                            reads=[("PSMM", psr), "cst"], writes=["qT"])
                    else:
                        for ii, j in enumerate(tiles):
                            k.op("vector", lambda e, ni=ni, psr=psr, ii=ii,
                                 j=j: e.tensor_copy(
                                kT1[:, ni - 8, (j % 4) * TIL:(j % 4 + 1) * TIL],
                                PS_MM[:, psr, ii * TIL:(ii + 1) * TIL]),
                                reads=[("PSMM", psr)], writes=[("k1", j % 4)])
                for ii, j in enumerate(tiles):
                    for nh in range(2):
                        psr = (ii * 2 + nh) % 2
                        for d in range(8):
                            k.op("tensor", lambda e, d=d, ii=ii, nh=nh, psr=psr,
                                 ring=ring: e.matmul(
                                PS_MM[:, psr, :],
                                xhT[:, ring, d, ii * TIL:(ii + 1) * TIL],
                                wv_res[:, d, nh * 512:(nh + 1) * 512],
                                start=(d == 0), stop=(d == 7)),
                                reads=[("xhT", ring), "wv"],
                                writes=[("PSMM", psr)])
                        k.op("vector", lambda e, j=j, nh=nh, psr=psr:
                             e.tensor_copy(
                            v1[:, j % 4, nh * 512:(nh + 1) * 512],
                            PS_MM[:, psr, :]),
                            reads=[("PSMM", psr)], writes=[("v1", j % 4)])
                for ii, j in enumerate(tiles):
                    if j < 17:
                        continue
                    mk = maskcur[:, 0:256] if j == 17 else mask_base[:, 0:256]
                    mkk = "msp" if j == 17 else "mb"
                    prev_in_block = (j - 1) in tiles
                    for h in range(16):
                        p, b = h // 2, (h % 2) * 64
                        for si in range(2):
                            sl = (j - 1) % 4 if si == 0 else j % 4
                            if si == 0 and not prev_in_block:
                                krhs = kTprev[b:b + 64, p, :]
                                krd = "kprev"
                            else:
                                krhs = kT1[b:b + 64, p, sl * TIL:(sl + 1) * TIL]
                                krd = ("k1", sl)
                            k.op("tensor", lambda e, p=p, b=b, si=si,
                                 krhs=krhs, ii=ii: e.matmul(
                                PS_S[:, si * TIL:(si + 1) * TIL],
                                qT_blk[b:b + 64, p, ii * TIL:(ii + 1) * TIL],
                                krhs, start=True, stop=True),
                                reads=["qT", krd],
                                writes=[("PSS", 0)])
                        k.op("vector", lambda e, mk=mk: e.tensor_tensor(
                            Sm[:, 0:256], PS_S[:, 0:256], mk, op=ALU.add),
                            reads=[("PSS", 0), mkk], writes=["Sm"])
                        k.op("vector", lambda e: e.reduce_max(
                            stat2[:, 40:41], Sm[:, 0:256], axis=AX.X,
                            negate=True), reads=["Sm"], writes=["stat2"])
                        k.op("scalar", lambda e: e.activation(
                            Pm[:, 0:256], Sm[:, 0:256], AF.Exp,
                            bias=stat2[:, 40:41], scale=1.0,
                            accum_out=stat[:, 40:41]),
                            reads=["Sm", "stat2"], writes=["Pm", "stat"])
                        k.op("vector", lambda e, h=h: e.reciprocal(
                            stati[:, 40 + h:41 + h], stat[:, 40:41]),
                            reads=["stat"], writes=["stati"])
                        for c in range(2):
                            k.op("tensor", lambda e, c=c: e.transpose(
                                PS_T[:, c * TIL:(c + 1) * TIL],
                                Pm[:, c * TIL:(c + 1) * TIL], identb[:]),
                                reads=["Pm", "identb"], writes=["PST"])
                        k.op("vector", lambda e: e.tensor_copy(
                            PT5[:, 0:2, :], PS_T[:, 0:256]
                            .rearrange("p (a b) -> p a b", b=TIL)),
                            reads=["PST"], writes=["PT5"])
                        bank = h // 8
                        col = (h % 8) * 64
                        for si in range(2):
                            sl = (j - 1) % 4 if si == 0 else j % 4
                            if si == 0 and not prev_in_block:
                                vrhs = vprev[:, h * 64:(h + 1) * 64]
                                vrd = "vprev"
                            else:
                                vrhs = v1[:, sl, h * 64:(h + 1) * 64]
                                vrd = ("v1", sl)
                            k.op("tensor", lambda e, si=si, vrhs=vrhs, h=h,
                                 bank=bank, col=col: e.matmul(
                                PS_O[:, bank, col:col + 64], PT5[:, si, :],
                                vrhs, start=(si == 0), stop=(si == 1)),
                                reads=["PT5", vrd],
                                writes=[("PSO", bank)])
                    for h in range(16):
                        bank, col = h // 8, (h % 8) * 64
                        k.op("scalar", lambda e, h=h, bank=bank, col=col:
                             e.activation(
                            ynorm[:, h * 64:(h + 1) * 64],
                            PS_O[:, bank, col:col + 64], AF.Identity,
                            bias=0.0, scale=stati[:, 40 + h:41 + h]),
                            reads=[("PSO", bank), "stati"], writes=["ynorm"])
                    if _os.environ.get("KX3RAW", "0") == "1":
                        out_done[j] = k.dma("gpsimd", "xst",
                                            outd[j * TIL:(j + 1) * TIL, :],
                                            ynorm[:], reads=["ynorm"],
                                            writes=[("x3", j)])
                        continue
                    ln_stats([(ynorm[:], ["ynorm"])])
                    k.op("vector", lambda e: e.tensor_scalar(
                        junk[:], ynorm[:], statm[:, 0:1], stati[:, 0:1],
                        op0=ALU.subtract, op1=ALU.mult),
                        reads=["ynorm", "statm", "stati"], writes=["junk"])
                    k.op("vector", lambda e, j=j: e.tensor_tensor(
                        XW[:, j % XWN, :], XW[:, j % XWN, :], junk[:],
                        op=ALU.add),
                        reads=["junk", ("XW", j % XWN)],
                        writes=[("XW", j % XWN)])
                    out_done[j] = k.dma("gpsimd", "xst",
                                        outd[j * TIL:(j + 1) * TIL, :],
                                        XW[:, j % XWN, :],
                                        reads=[("XW", j % XWN)],
                                        writes=[("x3", j)])
                jl = tiles[-1]
                k.op("vector", lambda e, jl=jl: e.tensor_copy(
                    vprev[:], v1[:, jl % 4, :]),
                    reads=[("v1", jl % 4)], writes=["vprev"])
                k.op("vector", lambda e, jl=jl: e.tensor_copy(
                    kTprev[:], kT1[:, :, (jl % 4) * TIL:(jl % 4 + 1) * TIL]),
                    reads=[("k1", jl % 4)], writes=["kprev"])
            return out_done

        def emit_l1_ffn(in_done):
            ind = streams["x3"]
            outd = streams["x4"]
            out_done = [None] * NT
            for a in range(8):
                k.dma("gpsimd", "wres%d" % (a % 2), woT_res[:, a, :],
                      w_l1fT_d[a * TIL:(a + 1) * TIL, :], writes=["woT"])
                k.dma("gpsimd", "wres%d" % (a % 2), U_buf[:, a, :].unsqueeze(1).squeeze(1) if False else U_buf[:, a, :],
                      UcT_d[a * TIL:(a + 1) * TIL, :], writes=["Ub"])
                k.dma("gpsimd", "wres%d" % (a % 2), gw_buf[:, a, :],
                      gwT_d[a * TIL:(a + 1) * TIL, :], writes=["gw"])
            k.dma("gpsimd", "cstg", V_buf[:], Vc_d, writes=["Vb"])
            for tiles in chunks4(17, NT):
                for ii, j in enumerate(tiles):
                    k.dma("sync", "xw%d" % (j % 2), XW[:, j % XWN, :],
                          ind[j * TIL:(j + 1) * TIL, :],
                          waits=[in_done[j]], writes=[("XW", j % XWN)])
                ln_stats([(XW[:, j % XWN, :], [("XW", j % XWN)])
                          for j in tiles])
                for ii, j in enumerate(tiles):
                    k.op("vector", lambda e, j=j, ii=ii: e.tensor_scalar(
                        junk[:], XW[:, j % XWN, :], statm[:, ii:ii + 1],
                        stati[:, ii:ii + 1], op0=ALU.subtract, op1=ALU.mult),
                        reads=[("XW", j % XWN), "statm", "stati"],
                        writes=["junk"])
                    k.op("scalar", lambda e: e.activation(
                        o_tok[:], junk[:], AF.Identity, bias=0.0, scale=1.0),
                        reads=["junk"], writes=["otok"])
                    transpose_128(
                        lambda c: o_tok[:, c * TIL:(c + 1) * TIL], 8,
                        lambda h: oT_t[:, 4 * h:4 * h + 4, :],
                        ["otok"], ["oT"])
                    for d in range(8):
                        k.op("tensor", lambda e, d=d: e.matmul(
                            PS_ST[:, 0:8], oT_t[:, d, :], gw_buf[:, d, :],
                            start=(d == 0), stop=(d == 7)),
                            reads=["oT", "gw"], writes=["PSST"])
                    for d in range(8):
                        k.op("tensor", lambda e, d=d: e.matmul(
                            PS_S[:, 0:TIL], oT_t[:, d, :], U_buf[:, d, :],
                            start=(d == 0), stop=(d == 7)),
                            reads=["oT", "Ub"], writes=[("PSS", 0)])
                    k.op("vector", lambda e: e.reduce_max(
                        stat2[:, 40:41], PS_ST[:, 0:8], axis=AX.X,
                        negate=True), reads=["PSST"], writes=["stat2"])
                    k.op("scalar", lambda e: e.activation(
                        junk[:, 0:8], PS_ST[:, 0:8], AF.Exp,
                        bias=stat2[:, 40:41], scale=1.0,
                        accum_out=stat[:, 40:41]),
                        reads=["PSST", "stat2"], writes=["junk", "stat"])
                    k.op("vector", lambda e: e.reciprocal(
                        stati[:, 40:41], stat[:, 40:41]),
                        reads=["stat"], writes=["stati"])
                    k.op("vector", lambda e: e.tensor_scalar(
                        stati[:, 41:42], stati[:, 40:41], 1.0 / 16.0, 0.0,
                        op0=ALU.mult, op1=ALU.add),
                        reads=["stati"], writes=["stati"])
                    k.op("vector", lambda e: e.scalar_tensor_tensor(
                        sT_m[:, 0, :].rearrange("p (a b) -> p a b", b=16),
                        PS_S[:, 0:TIL].rearrange("p (a b) -> p a b", b=16),
                        stati[:, 41:42],
                        junk[:, 0:8].unsqueeze(2).broadcast_to([TIL, 8, 16]),
                        op0=ALU.mult, op1=ALU.mult),
                        reads=[("PSS", 0), "stati", "junk"], writes=["sT"])
                    k.op("tensor", lambda e: e.transpose(
                        PS_T[:, 0:TIL], sT_m[:, 0, :], identb[:]),
                        reads=["sT", "identb"], writes=["PST"])
                    k.op("vector", lambda e: e.tensor_copy(
                        PT5[:, 0, :], PS_T[:, 0:TIL]),
                        reads=["PST"], writes=["PT5"])
                    for nh in range(2):
                        psr = nh
                        for d in range(8):
                            k.op("tensor", lambda e, d=d, nh=nh, psr=psr:
                                 e.matmul(
                                PS_MM[:, psr, :], oT_t[:, d, :],
                                woT_res[:, d, nh * 512:(nh + 1) * 512],
                                start=(d == 0), stop=False),
                                reads=["oT", "woT"], writes=[("PSMM", psr)])
                        k.op("tensor", lambda e, nh=nh, psr=psr: e.matmul(
                            PS_MM[:, psr, :], PT5[:, 0, :],
                            V_buf[:, nh * 512:(nh + 1) * 512],
                            start=False, stop=True),
                            reads=["PT5", "Vb"], writes=[("PSMM", psr)])
                    k.op("scalar", lambda e: e.activation(
                        junk[:, 0:512], PS_MM[:, 0, :], AF.Square, bias=0.0,
                        scale=1.0, accum_out=stat[:, 8:9]),
                        reads=[("PSMM", 0)], writes=["junk", "stat"])
                    k.op("scalar", lambda e: e.activation(
                        junk[:, 512:1024], PS_MM[:, 1, :], AF.Square,
                        bias=0.0, scale=1.0, accum_out=stat[:, 9:10]),
                        reads=[("PSMM", 1)], writes=["junk", "stat"])
                    k.op("vector", lambda e: e.reduce_sum(
                        stat2[:, 8:9].unsqueeze(2), PS_MM[:, 0, :]
                        .unsqueeze(1), axis=AX.X),
                        reads=[("PSMM", 0)], writes=["stat2"])
                    k.op("vector", lambda e: e.reduce_sum(
                        stat2[:, 9:10].unsqueeze(2), PS_MM[:, 1, :]
                        .unsqueeze(1), axis=AX.X),
                        reads=[("PSMM", 1)], writes=["stat2"])
                    k.op("vector", lambda e: e.tensor_tensor(
                        statm[:, 8:9], stat2[:, 8:9], stat2[:, 9:10],
                        op=ALU.add), reads=["stat2"], writes=["statm"])
                    k.op("vector", lambda e: e.tensor_scalar(
                        statm[:, 8:9], statm[:, 8:9], 1.0 / D, 0.0,
                        op0=ALU.mult, op1=ALU.add),
                        reads=["statm"], writes=["statm"])
                    k.op("vector", lambda e: e.tensor_tensor(
                        stat[:, 8:9], stat[:, 8:9], stat[:, 9:10], op=ALU.add),
                        reads=["stat"], writes=["stat"])
                    k.op("vector", lambda e: e.tensor_scalar(
                        stat[:, 8:9], stat[:, 8:9], 1.0 / D, 0.0,
                        op0=ALU.mult, op1=ALU.add),
                        reads=["stat"], writes=["stat"])
                    k.op("vector", lambda e: e.tensor_tensor(
                        stat2[:, 10:11], statm[:, 8:9], statm[:, 8:9],
                        op=ALU.mult), reads=["statm"], writes=["stat2"])
                    k.op("vector", lambda e: e.tensor_tensor(
                        stat[:, 8:9], stat[:, 8:9], stat2[:, 10:11],
                        op=ALU.subtract), reads=["stat", "stat2"],
                        writes=["stat"])
                    k.op("vector", lambda e: e.tensor_scalar(
                        stat[:, 8:9], stat[:, 8:9], EPS, 0.0, op0=ALU.add,
                        op1=ALU.add), reads=["stat"], writes=["stat"])
                    k.op("scalar", lambda e: e.activation(
                        stat2[:, 8:9], stat[:, 8:9], AF.Sqrt, bias=0.0,
                        scale=1.0), reads=["stat"], writes=["stat2"])
                    k.op("vector", lambda e: e.reciprocal(
                        stati[:, 8:9], stat2[:, 8:9]),
                        reads=["stat2"], writes=["stati"])
                    k.op("vector", lambda e: e.tensor_scalar(
                        ynorm[:, 0:512], PS_MM[:, 0, :], statm[:, 8:9],
                        stati[:, 8:9], op0=ALU.subtract, op1=ALU.mult),
                        reads=[("PSMM", 0), "statm", "stati"],
                        writes=["ynorm"])
                    k.op("vector", lambda e: e.tensor_scalar(
                        ynorm[:, 512:1024], PS_MM[:, 1, :], statm[:, 8:9],
                        stati[:, 8:9], op0=ALU.subtract, op1=ALU.mult),
                        reads=[("PSMM", 1), "statm", "stati"],
                        writes=["ynorm"])
                    k.op("vector", lambda e, j=j: e.tensor_tensor(
                        XW[:, j % XWN, :], XW[:, j % XWN, :], ynorm[:],
                        op=ALU.add),
                        reads=["ynorm", ("XW", j % XWN)],
                        writes=[("XW", j % XWN)])
                    out_done[j] = k.dma("gpsimd", "xst",
                                        outd[j * TIL:(j + 1) * TIL, :],
                                        XW[:, j % XWN, :],
                                        reads=[("XW", j % XWN)],
                                        writes=[("x4", j)])
            return out_done

        stream_done = {"x1": t_x1_st}
        final_stream = "x1"
        if n_stages >= 2:
            barrier()
            yd = emit_l0ffn_attn(0, "x1", t_x1_st, kv_lo=1, q_lo=8, do_ln=True)
            stream_done["y"] = yd
            final_stream = "y"
        if n_stages >= 3:
            barrier()
            xl2d = emit_l0ffn_ffn(0, yd, "xl2", q_lo=8)
            stream_done["xl2"] = xl2d
            final_stream = "xl2"
        if n_stages >= 4:
            barrier()
            yd2 = emit_l0ffn_attn(1, "xl2", xl2d, kv_lo=9, q_lo=16, do_ln=False)
            barrier()
            zd = emit_l0ffn_ffn(1, yd2, "z", q_lo=16)
            stream_done["z"] = zd
            barrier()
            x2d = [None] * NT
            for tiles in chunks4(16, NT):
                nt_b = len(tiles)
                for ii, j in enumerate(tiles):
                    k.dma("sync", "xw%d" % (j % 2), XW[:, j % XWN, :],
                          x1_d[j * TIL:(j + 1) * TIL, :],
                          waits=[t_x1_st[j]], writes=[("XW", j % XWN)])
                    k.dma("sync", "hsh%d" % (j % 2), A2W[:, ii, :],
                          z_d[j * TIL:(j + 1) * TIL, :],
                          waits=[zd[j]], writes=[("A2W", ii)])
                ln_stats([(A2W[:, ii, :], [("A2W", ii)])
                          for ii in range(nt_b)])
                for ii, j in enumerate(tiles):
                    k.op("vector", lambda e, ii=ii: e.tensor_scalar(
                        ynorm[:], A2W[:, ii, :], statm[:, ii:ii + 1],
                        stati[:, ii:ii + 1], op0=ALU.subtract, op1=ALU.mult),
                        reads=[("A2W", ii), "statm", "stati"], writes=["ynorm"])
                    k.op("vector", lambda e, j=j: e.tensor_tensor(
                        XW[:, j % XWN, :], XW[:, j % XWN, :], ynorm[:], op=ALU.add),
                        reads=["ynorm", ("XW", j % XWN)], writes=[("XW", j % XWN)])
                    x2d[j] = k.dma("gpsimd", "xst",
                                   x2_d[j * TIL:(j + 1) * TIL, :],
                                   XW[:, j % XWN, :], reads=[("XW", j % XWN)],
                                   writes=[("x2", j)])
            stream_done["x2"] = x2d
            final_stream = "x2"
        if n_stages >= 5:
            barrier()
            x3d = emit_l1_attn(x2d)
            stream_done["x3"] = x3d
            final_stream = "x3"
        if n_stages >= 6:
            barrier()
            x4d = emit_l1_ffn(x3d)
            stream_done["x4"] = x4d
            final_stream = "x4"

        if spy_out is not None:
            k.op("vector", lambda e: e.tensor_copy(
                junk[:], xhT[:, 0, 0:2, 0:512].rearrange("p a b -> p (a b)")),
                reads=[("xhT", 0)], writes=["junk"])
            k.dma("sync", "spy", spy_out[0:TIL, :], junk[:], reads=["junk"])
            k.op("vector", lambda e: e.tensor_copy(
                junk[:], qT_blk[:, 0:2, 0:512].rearrange("p a b -> p (a b)")),
                reads=["qT"], writes=["junk"])
            k.dma("sync", "spy", spy_out[TIL:2 * TIL, :], junk[:], reads=["junk"])
            k.op("vector", lambda e: e.tensor_copy(
                junk[:], kT_blk[:, 0:2, 0:512].rearrange("p a b -> p (a b)")),
                reads=["kT"], writes=["junk"])
            k.dma("sync", "spy", spy_out[2 * TIL:3 * TIL, :], junk[:], reads=["junk"])
            k.op("vector", lambda e: e.tensor_copy(junk[:], v_blk[:, 3, :]),
                reads=["v"], writes=["junk"])
            k.dma("sync", "spy", spy_out[3 * TIL:4 * TIL, :], junk[:], reads=["junk"])
            k.op("vector", lambda e: e.tensor_copy(junk[:], ktok_blk[:, 3, :]),
                reads=["ktok"], writes=["junk"])
            k.dma("sync", "spy", spy_out[4 * TIL:5 * TIL, :], junk[:], reads=["junk"])
            k.op("vector", lambda e: e.tensor_copy(
                junk[:, 0:512], state_sb[:]), reads=["stsb"], writes=["junk"])
            k.op("vector", lambda e: e.tensor_copy(
                junk2s[:, 0:512], sT_m[:].rearrange("p a b -> p (a b)")),
                reads=["sT"], writes=["junk2s"])
            k.dma("sync", "spy", spy_out[6 * TIL:7 * TIL, :], junk2s[:],
                  reads=["junk2s"])
            k.op("vector", lambda e: e.tensor_copy(junk2s[:], o_tok[:]),
                reads=["otok"], writes=["junk2s"])
            k.dma("sync", "spy", spy_out[7 * TIL:8 * TIL, :], junk2s[:],
                  reads=["junk2s"])
            k.op("vector", lambda e: e.tensor_copy(junk2s[:], ynorm[:]),
                reads=["ynorm"], writes=["junk2s"])
            k.dma("sync", "spy", spy_out[8 * TIL:9 * TIL, :], junk2s[:],
                  reads=["junk2s"])
            k.op("vector", lambda e: e.tensor_copy(
                junk[0:1, 512:1024], SMB[0:1, NTOK:NTOK + 512]),
                reads=["SMBq"], writes=["junk"])
            k.dma("sync", "spy", spy_out[5 * TIL:6 * TIL, :], junk[:], reads=["junk"])

        sdram = streams[final_stream]
        built = {j for tt in blocks for j in tt}
        t_fin = []
        if dbg is not None:
            nm, tl = dbg
            tl = [j for j in tl if j in built]
            for i, j in enumerate(tl):
                t_fin.append(k.dma("sync", "out%d" % (i % 2),
                                   dbg_out[i * TIL:(i + 1) * TIL, :],
                                   streams[nm][j * TIL:(j + 1) * TIL, :],
                                   reads=[(nm, j)]))
        for i, j in enumerate(range(NT - OWN // TIL, NT)):
            if j not in built:
                continue
            t_fin.append(k.dma("sync", "out%d" % (i % 2),
                               xout[i * TIL:(i + 1) * TIL, :],
                               sdram[j * TIL:(j + 1) * TIL, :],
                               reads=[(final_stream, j)]))

        queues, sems, semname, semval = k.finalize()
        for s in sems:
            k.semh[s] = E(nc.semaphore(s))
        fin = [(semname[t], semval[t]) for t in t_fin if t is not None]
        with nc.Block() as block:
            @block.sync
            def _(sync):
                for fn in queues["sync"]:
                    fn(sync)
                for s, v in fin:
                    sync.wait_ge(k.semh[s], v)

            @block.gpsimd
            def _(gpsimd):
                for fn in queues["gpsimd"]:
                    fn(gpsimd)

            @block.tensor
            def _(tensor):
                for fn in queues["tensor"]:
                    fn(tensor)

            @block.vector
            def _(vector):
                for fn in queues["vector"]:
                    fn(vector)

            @block.scalar
            def _(scalar):
                for fn in queues["scalar"]:
                    fn(scalar)

    return nc


def _rot_tables(t):
    t = t.astype(np.float64)[:, None]
    half = np.arange(0, 64, 2, dtype=np.float64)
    inv = 1.0 / 10000.0 ** (half / 64.0)
    fr = t * inv[None, :]
    fr = np.concatenate([fr, fr], axis=1)
    basev = (half + 0.4 * 64.0) / (1.4 * 64.0)
    power = (t - 2048.0) / 256.0
    sc = basev[None, :] ** power
    sc = np.concatenate([sc, sc], axis=1)
    cos, sin = np.cos(fr), np.sin(fr)
    sgn = np.where(np.arange(64) < 32, -1.0, 1.0)[None, :]
    Aq = 8.0 * cos * sc
    Bq = 8.0 * sgn * sin * sc
    Ak = cos / sc
    Bk = sgn * sin / sc
    return np.concatenate([Aq, Bq, Ak, Bk], axis=1).astype(np.float32)


def _host_prep(L0_attn_x):
    B = L0_attn_x.shape[0]
    base = np.zeros((TIL, 640), np.float32)
    r = np.arange(TIL)[:, None]
    c = np.arange(640)[None, :]
    base[~((c >= r) & (c <= r + 512))] = NEG
    cores = []
    for b in range(B):
        for h in range(2):
            s = h * OWN
            lo = s - HALO
            pad = max(0, -lo)
            xsl = np.zeros((NTOK, D), np.float32)
            xsl[pad:] = np.asarray(L0_attn_x[b, max(0, lo):s + OWN], np.float32)
            frow = np.zeros((1, NTOK), np.float32)
            frow[0, pad:] = 1.0
            fpart = np.zeros((TIL, NT), np.float32)
            for j in range(NT):
                fpart[:, j] = (j * TIL + np.arange(TIL)) >= pad
            msp = np.zeros((4, TIL, 640), np.float32)
            for q in range(4):
                lj = (17 + q) * TIL
                cut = pad - lj + 512
                m = base.copy()
                if cut > 0:
                    m[:, 0:cut] += NEG
                msp[q] = m
            tglob = np.arange(NTOK, dtype=np.float64) + (s - HALO)
            tab = _rot_tables(tglob).reshape(NT, TIL, 256)
            m1b = np.zeros((TIL, 256), np.float32)
            r1 = np.arange(TIL)[:, None]
            c1 = np.arange(256)[None, :]
            allowed = ((c1 >= 64) & (c1 <= 128 + r1) & (r1 < 64)) | \
                      ((c1 >= 128) & (c1 <= 128 + r1) & (r1 >= 64))
            m1b[~allowed] = NEG
            m1s = m1b.copy()
            cut1 = pad - 2048
            if cut1 > 0:
                m1s[:, 0:cut1] += NEG
            cores.append(dict(xin=np.ascontiguousarray(xsl), frow=frow,
                              fpart=fpart, mask_base=base, mask_sp=msp,
                              tab4=np.ascontiguousarray(tab),
                              m1_base=m1b, m1_sp=m1s))
    return cores


def _host_consts():
    ident = np.eye(TIL, dtype=np.float32)
    causalT = np.triu(np.ones((TIL, TIL), np.float32))
    return dict(ident=ident, causalT=causalT)


def _host_weights(i):
    qs = np.asarray(i["L0_ffn_q_scale"], np.float32)
    ks = np.asarray(i["L0_ffn_k_scale"], np.float32)
    qs4 = np.zeros((2, TIL, 4, 64), np.float32)
    for l in range(2):
        qs4[l, :, 0] = np.tile(qs[l], (TIL, 1))
        qs4[l, :, 1] = np.tile(np.roll(qs[l], -32), (TIL, 1))
        qs4[l, :, 2] = np.tile(ks[l], (TIL, 1))
        qs4[l, :, 3] = np.tile(np.roll(ks[l], -32), (TIL, 1))
    b1 = np.asarray(i["L0_ffn_b1"], np.float32)
    b1a = np.zeros((2, TIL, 22), np.float32)
    b1g = np.zeros((2, TIL, 22), np.float32)
    for cc in range(22):
        w = min(TIL, 2730 - cc * TIL)
        b1a[:, 0:w, cc] = b1[:, cc * TIL:cc * TIL + w]
        b1g[:, 0:w, cc] = b1[:, 2730 + cc * TIL:2730 + cc * TIL + w]
    return dict(
        w_l0a_qkvT=np.ascontiguousarray(np.asarray(i["L0_attn_Wqkv_w"], np.float32).T),
        b_l0a_qkv=np.asarray(i["L0_attn_Wqkv_b"], np.float32).reshape(1, -1),
        w_l0a_outT=np.ascontiguousarray(np.asarray(i["L0_attn_out_w"], np.float32).T),
        b_l0a_out=np.asarray(i["L0_attn_out_b"], np.float32).reshape(1, -1),
        w_ffn_qkv=np.ascontiguousarray(np.asarray(i["L0_ffn_wqkv"], np.float32)),
        w_ffn_wo=np.ascontiguousarray(np.asarray(i["L0_ffn_wo"], np.float32)),
        w_ffn_w1=np.ascontiguousarray(np.asarray(i["L0_ffn_w1"], np.float32)),
        w_ffn_w2=np.ascontiguousarray(np.asarray(i["L0_ffn_w2"], np.float32)),
        qs4=qs4,
        gam=np.ascontiguousarray(np.tile(
            np.asarray(i["L0_ffn_ln_g"], np.float32)[:, None, :], (1, TIL, 1))),
        bet=np.ascontiguousarray(np.tile(
            np.asarray(i["L0_ffn_ln_b"], np.float32)[:, None, :], (1, TIL, 1))),
        b1a=b1a, b1g=b1g,
        b2=np.asarray(i["L0_ffn_b2"], np.float32).reshape(2, 1, D),
        w_l1_q=np.ascontiguousarray(np.asarray(i["L1_attn_Wq"], np.float32)),
        w_l1_k=np.ascontiguousarray(np.asarray(i["L1_attn_Wk"], np.float32)),
        w_l1_v=np.ascontiguousarray(np.asarray(i["L1_attn_Wv"], np.float32)),
        w_l1f_T=np.ascontiguousarray(np.asarray(i["L1_ffn_weight"], np.float32).T),
        gate_wT=np.ascontiguousarray(np.asarray(i["L1_ffn_gate_w"], np.float32).T),
        U_catT=np.ascontiguousarray(np.asarray(i["L1_ffn_lora_U"], np.float32)
                                    .transpose(2, 0, 1).reshape(D, TIL)),
        V_cat=np.ascontiguousarray(np.asarray(i["L1_ffn_lora_V"], np.float32)
                                   .transpose(0, 2, 1).reshape(TIL, D)),
    )


def kernel(**inputs):
    x = np.asarray(inputs["L0_attn_x"], np.float32)
    B = x.shape[0]
    cores = _host_prep(x)
    consts = _host_consts()
    wmap = _host_weights(inputs)
    nc = build_nc(n_stages=N_STAGES, dbg=None)
    in_maps = [dict(c, **consts, **wmap) for c in cores]
    res = run_bass_kernel_spmd(nc, in_maps, core_ids=list(range(8)))
    out = np.zeros((B, 2 * OWN, D), np.float32)
    for ci in range(8):
        b, h = ci // 2, ci % 2
        out[b, h * OWN:(h + 1) * OWN] = res.results[ci]["xout"]
    return out
